# version 13
# baseline (speedup 1.0000x reference)
import numpy as np
from contextlib import ExitStack

import concourse.bass as bass
import concourse.mybir as mybir
from concourse.bass_utils import run_bass_kernel_spmd
from concourse.bass_core_rust import IndirectOffsetOnAxis
import ml_dtypes

F32 = mybir.dt.float32
BF16 = mybir.dt.bfloat16
AF = mybir.ActivationFunctionType
ALU = mybir.AluOpType
AX = mybir.AxisListType

S = 4096
D = 1024
NB = 8
TB = 512
NCH = 8
NH = 16
FF = 2048
NE = 8
NEG = -1.0e30
LN_EPS = 1e-5

ENGINES = ("tensor", "vector", "scalar", "gpsimd", "sync")


class Op:
    __slots__ = ("eng", "fn", "reads", "writes", "dma", "tag", "deps",
                 "inc_idx", "waits", "sub")

    def __init__(self, eng, fn, reads, writes, dma, tag):
        self.eng = eng
        self.fn = fn
        self.reads = tuple(reads)
        self.writes = tuple(writes)
        self.dma = dma
        self.tag = tag


class Builder:
    def __init__(self):
        self.ops = []

    def op(self, eng, fn, reads=(), writes=(), dma=False, tag=""):
        self.ops.append(Op(eng, fn, reads, writes, dma, tag))

    KSUB = 6

    def resolve(self):
        dma_rank = {}
        for i, o in enumerate(self.ops):
            if o.dma:
                r = dma_rank.get(o.eng, 0)
                dma_rank[o.eng] = r + 1
                o.sub = r % self.KSUB
                o.inc_idx = r // self.KSUB + 1
            else:
                o.sub = 0

        def dom_of(o):
            if o.dma:
                return (o.eng, "d", o.sub)
            return (o.eng, "c", 0)

        last_w = {}
        readers = {}
        for i, o in enumerate(self.ops):
            deps = set()
            for r in o.reads:
                if r in last_w:
                    deps.add(last_w[r])
            for w in o.writes:
                if w in last_w:
                    deps.add(last_w[w])
                deps.update(readers.get(w, ()))
            for w in o.writes:
                last_w[w] = i
                readers[w] = []
            for r in o.reads:
                readers.setdefault(r, []).append(i)
            byd = {}
            for j in deps:
                p = self.ops[j]
                if (p.eng == o.eng and not p.dma and not o.dma
                        and o.eng == "tensor"):
                    continue
                dom = dom_of(p)
                if j > byd.get(dom, -1):
                    byd[dom] = j
            o.deps = set(byd.values())
        need_inc = set()
        for i, o in enumerate(self.ops):
            need_inc |= o.deps
        counters = {}
        for i, o in enumerate(self.ops):
            dom = dom_of(o)
            if o.dma:
                counters[dom] = o.inc_idx
            elif i in need_inc:
                counters[dom] = counters.get(dom, 0) + 1
                o.inc_idx = counters[dom]
            else:
                o.inc_idx = None
        emitted = {}
        for o in self.ops:
            waits = {}
            if o.dma and o.inc_idx > 1:
                waits[dom_of(o)] = (o.inc_idx - 1) * 16
            for j in o.deps:
                p = self.ops[j]
                dom = dom_of(p)
                v = p.inc_idx * (16 if p.dma else 1)
                if v > waits.get(dom, 0):
                    waits[dom] = v
            em = emitted.setdefault(o.eng, {})
            fin = []
            for dom, v in sorted(waits.items(), key=lambda kv: str(kv[0])):
                if v > em.get(dom, 0):
                    em[dom] = v
                    fin.append((dom, v))
            o.waits = fin
        self.domains = sorted(counters, key=str)
        self.counters = counters
        self._dom_of = dom_of

    def emit(self, nc):
        self.resolve()
        es = ExitStack()
        sems = {dom: es.enter_context(
                    nc.semaphore(f"sm_{dom[0][:2]}{dom[1]}{dom[2]}"))
                for dom in self.domains}
        by_eng = {e: [] for e in ENGINES}
        for o in self.ops:
            by_eng[o.eng].append(o)
        dom_of = self._dom_of

        def runner(lst):
            def body(eng):
                for o in lst:
                    for dom, v in o.waits:
                        eng.wait_ge(sems[dom], v)
                    ins = o.fn(eng)
                    if o.inc_idx is not None:
                        ins.then_inc(sems[dom_of(o)], 16 if o.dma else 1)
            return body

        block = es.enter_context(nc.Block())
        with es:
            block.tensor(runner(by_eng["tensor"]))
            block.vector(runner(by_eng["vector"]))
            block.scalar(runner(by_eng["scalar"]))
            block.gpsimd(runner(by_eng["gpsimd"]))
            block.sync(runner(by_eng["sync"]))


def _slopes():
    ratio = 2.0 ** (-8.0 / NH)
    return np.array([ratio ** (h + 1) for h in range(NH)], dtype=np.float64)


def build_program(upto="all"):
    nc = bass.Bass("TRN2", target_bir_lowering=False, debug=False)

    def inp(name, shape, dt=F32):
        return nc.dram_tensor(name, list(shape), dt, kind="ExternalInput").ap()

    x0 = inp("x0", (S, D))
    qw = inp("qw", (D, D))
    kw = inp("kw", (D, D))
    vw = inp("vw", (D, D))
    ow = inp("ow", (D, D))
    wiT = inp("wiT", (D, 3 * D))
    woT = inp("woT", (D, D))
    bi_cols = inp("bi_cols", (128, 24))
    bo_cols = inp("bo_cols", (128, 8))
    rows32_c = inp("rows32_c", (1, 3688))
    gbc_c = inp("gbc_c", (128, 32))
    bvh_c = inp("bvh_c", (64, 16))
    wroute = inp("wroute", (D, 12))
    w1 = inp("w1", (NE * 128, 16384))
    w2 = inp("w2", (NE * 128, 16384))
    b12r = inp("b12r", (NE * 128, 24))
    g2b2b = inp("g2b2b", (128, 2 * D))
    iotac = inp("iotac", (128, 1))
    ident_c = inp("ident_c", (128, 128), BF16)
    identf_c = inp("identf_c", (128, 128))
    ones16_c = inp("ones16_c", (128, 128), BF16)
    one16c = inp("one16c", (16, 128), BF16)
    exp16_c = inp("exp16_c", (16, 576), BF16)
    hcc = inp("hcc", (2, 128, 576))
    sl2 = inp("sl2", (NH, 2, S), BF16)
    qj_c = inp("qj_c", (2, 512), BF16)
    maskneg_c = inp("maskneg_c", (128, 896))

    out = nc.dram_tensor("out", [S, D], F32, kind="ExternalOutput").ap()
    x1sp = nc.dram_tensor("x1sp", [D, S], BF16, kind="ExternalOutput").ap()
    x2sp = nc.dram_tensor("x2sp", [D, S], BF16, kind="ExternalOutput").ap()
    w1b0 = nc.dram_tensor("w1b0", [NE * 128, 8192], BF16, kind="Internal").ap()
    w1b1 = nc.dram_tensor("w1b1", [NE * 128, 8192], BF16, kind="Internal").ap()
    w2b = nc.dram_tensor("w2b", [NE * 128, 16384], BF16, kind="Internal").ap()
    x1tok = nc.dram_tensor("x1tok", [S, D], BF16, kind="Internal").ap()
    x2tok = nc.dram_tensor("x2tok", [S, 1040], BF16, kind="Internal").ap()
    xg = nc.dram_tensor("xg", [15 * 512, 1040], BF16, kind="Internal").ap()
    yg = nc.dram_tensor("yg", [15 * 512, D], BF16, kind="Internal").ap()
    ib1 = nc.dram_tensor("ib1", [1, S], mybir.dt.int32, kind="Internal").ap()
    dbg_kind = "ExternalOutput" if upto in ("a0", "kv", "qloop") else "Internal"
    x0sp = nc.dram_tensor("x0sp", [D, S], BF16, kind=dbg_kind).ap()
    qsp = nc.dram_tensor("qsp", [NH, 64, S], BF16, kind=dbg_kind).ap()
    osp = nc.dram_tensor("osp", [D, S], BF16, kind=dbg_kind).ap()

    es = ExitStack()

    def sbuf(name, shape, dt=BF16):
        return es.enter_context(nc.sbuf_tensor(name, list(shape), dt))

    K_AR = sbuf("K_AR", (128, 8 * S))
    V_AR = sbuf("V_AR", (128, 32 * 520))
    XB = sbuf("XB", (128, 2 * 4096))
    W_AR = sbuf("W_AR", (128, 4 * 4096))
    QS = sbuf("QS", (128, 2 * 512))
    PTB = sbuf("PTB", (128, 4 * 512))
    STG = sbuf("STG", (128, 1040))
    SQ = sbuf("SQ", (128, 2 * 512))
    RB = sbuf("RB", (128, 512))
    OTS = sbuf("OTS", (128, 2 * 512))
    RLN = sbuf("RLN", (128, 8 * 512))
    CMB = sbuf("CMB", (128, 32 * 8))
    CMTX = sbuf("CMTX", (128, 2048))
    AGB = sbuf("AGB", (128, 1024))
    BRB = sbuf("BRB", (128, 12), F32)
    IDENT = sbuf("IDENT", (128, 128))
    ONES16 = sbuf("ONES16", (128, 128))
    ONE16 = sbuf("ONE16", (16, 128))
    EXP16 = sbuf("EXP16", (16, 576))
    ONESB = sbuf("ONESB", (128, 2))
    ALIBB = sbuf("ALIBB", (128, 576), F32)
    MASKN = sbuf("MASKN", (128, 896), F32)
    ROWS32 = sbuf("ROWS32", (128, 3688), F32)
    GBC = sbuf("GBC", (128, 32), F32)
    BVH = sbuf("BVH", (64, 16), F32)
    BCT = sbuf("BCT", (16, 128))
    BIC = sbuf("BIC", (128, 24), F32)
    BOC = sbuf("BOC", (128, 8), F32)
    WRT = sbuf("WRT", (128, 8 * 12))
    RTS = sbuf("RTS", (128, 128), F32)
    SC16 = sbuf("SC16", (16, 8), F32)
    OUTS = sbuf("OUTS", (128, 1024), F32)
    IDENT32 = sbuf("IDENT32", (128, 128), F32)
    DESTC = sbuf("DESTC", (128, 32), mybir.dt.int32)
    EIDF = sbuf("EIDF", (128, 16), F32)
    EIDCI = sbuf("EIDCI", (128, 16), mybir.dt.int32)
    IOCF = sbuf("IOCF", (128, 1), F32)
    B1T = sbuf("B1T", (128, 24), F32)
    BASEC = sbuf("BASEC", (128, 1), F32)
    ONES8F = sbuf("ONES8F", (8, 4), F32)
    XGL = sbuf("XGL", (128, 1040))
    SQT = sbuf("SQT", (128, 1040))
    MVC = sbuf("MVC", (128, 32), F32)
    WCTS = sbuf("WCTS", (128, 32), F32)
    PCS = sbuf("PCS", (128, 1024))
    V32 = V_AR[:, 0:16640].bitcast(F32)
    ONE8 = V32[0:8, 0:S]
    CUM8 = V32[0:8, S:2 * S]

    P = [es.enter_context(nc.psum_tensor(f"P{i}", [128, 512], F32))
         for i in range(6)]
    PT7 = es.enter_context(nc.psum_tensor("PT7", [128, 1024], BF16))
    PT8 = es.enter_context(nc.psum_tensor("PT8", [128, 1024], BF16))

    ones512_row = ROWS32[32:33, 0:512]
    ones128 = ROWS32[32:33, 0:128]
    ones64_row = ROWS32[32:33, 0:64]
    arow_r = ROWS32[32:33, 512:1024]
    brow_r = ROWS32[32:33, 1024:1536]
    rrow_r = ROWS32[32:33, 1536:2048]
    brt_row = ROWS32[32:33, 2048:2060]
    mean_r = ROWS32[32:33, 2064:2576]
    m2_r = ROWS32[32:33, 2576:3088]
    var_r = ROWS32[32:33, 3088:3600]
    iot16_r = ROWS32[32:33, 3600:3616]
    one1_r = ROWS32[32:33, 0:1]
    CMTR = K_AR[0:8, 8192:12288]

    def mr(a, b):
        return ROWS32[32:33, 3616 + a:3616 + b]
    IRV = OUTS[:, 512:1024].bitcast(mybir.dt.int32)

    bld = Builder()
    _sink = [bld.op]

    def o_(eng, fn, reads=(), writes=(), dma=False, tag=""):
        _sink[0](eng, fn, reads=reads, writes=writes, dma=dma, tag=tag)

    def collect_ops(body):
        lst = []

        def cap(eng, fn, reads=(), writes=(), dma=False, tag=""):
            lst.append((eng, fn, tuple(reads), tuple(writes), dma, tag))
        _sink[0] = cap
        try:
            body()
        finally:
            _sink[0] = bld.op
        return lst

    def emit_rr(*lists):
        for i in range(max(len(l) for l in lists)):
            for lst in lists:
                if i < len(lst):
                    eng, fn, rds, wrs, dma, tag = lst[i]
                    bld.op(eng, fn, reads=rds, writes=wrs, dma=dma, tag=tag)

    def emit_seq(lst):
        for eng, fn, rds, wrs, dma, tag in lst:
            bld.op(eng, fn, reads=rds, writes=wrs, dma=dma, tag=tag)
    state = {"pk": 0, "w": 0}
    SRING = [0, 1, 4]
    _sl = _slopes()
    _s8 = np.array([np.float32(ml_dtypes.bfloat16(-8.0 * x)) for x in _sl])
    slope_eff_py = (-_s8 / 8.0).astype(np.float64)

    def wslot():
        s_ = state["w"]
        state["w"] = (s_ + 1) % 4
        return s_

    def cload(dst, src, res, eng="sync"):
        o_(eng, lambda e, a=dst, b_=src: e.dma_start(out=a, in_=b_),
           writes=(res,), dma=True, tag="cload")

    cload(IDENT[:, :], ident_c, "IDENT")
    cload(IDENT32[:, :], identf_c, "IDENT32")
    cload(ONES16[:, :], ones16_c, "ONES16")
    cload(ONE16[:, :], one16c, "ONE16")
    cload(EXP16[:, :], exp16_c, "EXP16")
    cload(MASKN[:, :], maskneg_c, "MASKN")
    cload(ROWS32[32:33, :], rows32_c, "ROWC")
    cload(GBC[:, :], gbc_c, "GBC")
    cload(BVH[:, :], bvh_c, "BVH")
    cload(BIC[:, :], bi_cols, "BIC")
    cload(BOC[:, :], bo_cols, "BOC")
    cload(IOCF[:, :], iotac, "IOCF")
    cload(CMTX[:, 0:2048], g2b2b, "G2B2", eng="gpsimd")
    cload(WRT[:, :].rearrange("p (c j) -> p c j", j=12),
          wroute.rearrange("(c p) j -> p c j", p=128), "WRT", eng="gpsimd")
    cload(QS[64:66, 0:512], qj_c, "QSj0")
    cload(QS[64:66, 512:1024], qj_c, "QSj1")
    o_("vector", lambda e: e.memset(ONESB[:, :], 1.0), writes=("ONESB",))
    o_("vector", lambda e: e.memset(OUTS[64:65, 512:576], 1.0),
       writes=("RRBonesF",))
    o_("vector", lambda e: e.memset(OTS[64:65, 512:640], 1.0),
       writes=("RRBones",))
    va_ones = V_AR[:, :].rearrange("p (t h x) -> p t h x", h=8, x=65)[:, :, :, 64:65]
    o_("vector", lambda e: e.memset(va_ones, 1.0), writes=("VAones",))

    x0r = x0.rearrange("(n p) d -> n p d", p=128)
    for tt in range(32):
        sl = tt % 2
        tok = XB[:, sl * 4096:sl * 4096 + 1024]
        o_("gpsimd", lambda e, a=tok, b_=x0r[tt, :, :]: e.dma_start(out=a, in_=b_),
           writes=(f"XB{sl}",), dma=True, tag="a0ld")
        for dc in range(NCH):
            bank = PT7 if dc % 2 == 0 else PT8
            bres = "PT7" if dc % 2 == 0 else "PT8"
            w0 = (dc // 2) * 128
            o_("tensor",
               lambda e, o2=bank[:, w0:w0 + 128],
                      i2=tok[:, dc * 128:(dc + 1) * 128]:
                   e.transpose(o2, i2, IDENT[:, :]),
               reads=(f"XB{sl}", "IDENT"), writes=(bres,), tag="a0tp")
            o_("scalar",
               lambda e, o2=K_AR[:, dc * S + tt * 128:dc * S + tt * 128 + 128],
                      i2=bank[:, w0:w0 + 128]:
                   e.copy(o2, i2),
               reads=(bres,), writes=("K_AR",), tag="a0ev")
    for dc in range(NCH):
        o_("sync",
           lambda e, a=x0sp[dc * 128:(dc + 1) * 128, :],
                  b_=K_AR[:, dc * S:(dc + 1) * S]:
               e.dma_start(out=a, in_=b_),
           reads=("K_AR",), writes=("x0sp",), dma=True, tag="a0sp")

    def load_weight(dram_ap, c0, c1, slot, tag="wld"):
        n = c1 - c0
        src = dram_ap.rearrange("(c p) j -> p c j", p=128)[:, :, c0:c1]
        dst = W_AR[:, slot * 4096:slot * 4096 + 8 * n].rearrange(
            "p (c j) -> p c j", j=n)
        o_("gpsimd", lambda e, a=dst, b_=src: e.dma_start(out=a, in_=b_),
           writes=(f"W{slot}",), dma=True, tag=tag)

    def load_xblock(xsp, xres, b_, slot, tag="xld"):
        src = xsp.rearrange("(c p) t -> p c t", p=128)[:, :, b_ * TB:(b_ + 1) * TB]
        dst = XB[:, slot * 4096:(slot + 1) * 4096].rearrange(
            "p (c t) -> p c t", t=TB)
        o_("sync", lambda e, a=dst, b2=src: e.dma_start(out=a, in_=b2),
           reads=(xres,), writes=(f"XB{slot}",), dma=True, tag=tag)

    PRE = []
    PRE.append(("vector", lambda e: e.memset(XGL[:, :], 0.0),
                (), ("XGL",), False, "xgz"))
    _zq = [("sync",
            lambda e, z0=zt: e.dma_start(out=xg[z0 * 128:(z0 + 1) * 128, :],
                                         in_=XGL[:, :]),
            ("XGL",), ("xg",), True, "xgz") for zt in range(60)]
    _cq = []
    for _srcw, _w, _tag in ((w1, None, "w1"), (w2, w2b, "w2b")):
        for _ei in range(NE):
            for _q in range(16):
                if _tag == "w1":
                    _dst, _dres = (w1b0, "w1b0") if _q < 8 else (w1b1, "w1b1")
                    _qq = _q % 8
                else:
                    _dst, _dres, _qq = _w, _tag, _q
                _cq.append(("gpsimd",
                            lambda e, s0=_srcw, e0=_ei, q0=_q: e.dma_start(
                                out=PCS[:, :],
                                in_=s0[e0 * 128:(e0 + 1) * 128,
                                       q0 * 1024:(q0 + 1) * 1024]),
                            (), ("PCS",), True, "pcin"))
                _cq.append(("sync",
                            lambda e, d2=_dst, e0=_ei, q0=_qq: e.dma_start(
                                out=d2[e0 * 128:(e0 + 1) * 128,
                                       q0 * 1024:(q0 + 1) * 1024],
                                in_=PCS[:, :]),
                            ("PCS",), (_dres,), True, "pcout"))
    while _zq or _cq:
        if _zq:
            PRE.append(_zq.pop(0))
        for _ in range(9):
            if _cq:
                PRE.append(_cq.pop(0))

    def drain_pre(n):
        for _ in range(n):
            if not PRE:
                return
            eng, fn, reads, writes, dma, tag = PRE.pop(0)
            o_(eng, fn, reads=reads, writes=writes, dma=dma, tag=tag)

    def ln_rows(supb, supr, sqpb, sqpr, tag="ln"):
        o_("vector", lambda e: e.tensor_scalar_mul(mean_r, supb[0:1, :], 1.0 / D),
           reads=(supr,), writes=("mean",), tag=tag)
        o_("vector", lambda e: e.tensor_scalar_mul(m2_r, sqpb[64:65, :], 1.0 / D),
           reads=(sqpr,), writes=("m2",), tag=tag)
        o_("vector", lambda e: e.tensor_tensor(var_r, mean_r, mean_r, op=ALU.mult),
           reads=("mean",), writes=("var",), tag=tag)
        o_("vector", lambda e: e.tensor_tensor(var_r, m2_r, var_r, op=ALU.subtract),
           reads=("m2", "var"), writes=("var",), tag=tag)
        o_("vector", lambda e: e.tensor_scalar_add(var_r, var_r, LN_EPS),
           reads=("var",), writes=("var",), tag=tag)
        o_("scalar", lambda e: e.activation(var_r, var_r, AF.Ln),
           reads=("var",), writes=("var",), tag=tag)
        o_("scalar", lambda e: e.activation(arow_r, var_r, AF.Exp, scale=-0.5),
           reads=("var",), writes=("arow",), tag=tag)
        o_("vector", lambda e: e.tensor_tensor(brow_r, mean_r, arow_r, op=ALU.mult),
           reads=("mean", "arow"), writes=("brow",), tag=tag)
        o_("vector", lambda e: e.tensor_scalar_mul(brow_r, brow_r, -1.0),
           reads=("brow",), writes=("brow",), tag=tag)

    def emit_ln_norm(b_, RSRC, rfmt, gb, xout, xoutres,
                     agb, agrA, agrB, supb, supr, sqpb, sqpr):
        o_("tensor",
           lambda e: e.matmul(supb[:, 0:512], ones128, arow_r,
                              start=True, stop=True),
           reads=("arow", "ROWC"), writes=(supr,), tag="lnag")
        o_("scalar", lambda e: e.copy(agb[:, 0:512], supb[:, 0:512]),
           reads=(supr,), writes=(agrA,), tag="lnev")
        o_("tensor",
           lambda e: e.matmul(sqpb[:, 0:512], ones128, brow_r,
                              start=True, stop=True),
           reads=("brow", "ROWC"), writes=(sqpr,), tag="lnbg")
        o_("scalar", lambda e: e.copy(agb[:, 512:1024], sqpb[:, 0:512]),
           reads=(sqpr,), writes=(agrB,), tag="lnev")
        for dc in range(NCH):
            o_("vector",
               lambda e, d0=dc:
                   e.tensor_tensor(RSRC[:, d0 * 512:(d0 + 1) * 512],
                                   RSRC[:, d0 * 512:(d0 + 1) * 512],
                                   agb[:, 0:512], op=ALU.mult),
               reads=(agrA,), writes=(rfmt.format(dc),), tag="lnmul")
            o_("vector",
               lambda e, d0=dc:
                   e.tensor_tensor(RSRC[:, d0 * 512:(d0 + 1) * 512],
                                   RSRC[:, d0 * 512:(d0 + 1) * 512],
                                   agb[:, 512:1024], op=ALU.add),
               reads=(agrB,), writes=(rfmt.format(dc),), tag="lnadd")
            if gb is not None:
                go, bo_ = gb
                o_("vector",
                   lambda e, d0=dc, g0=go, b0=bo_:
                       e.tensor_scalar(RSRC[:, d0 * 512:(d0 + 1) * 512],
                                       RSRC[:, d0 * 512:(d0 + 1) * 512],
                                       GBC[:, g0 + d0:g0 + d0 + 1],
                                       GBC[:, b0 + d0:b0 + d0 + 1],
                                       op0=ALU.mult, op1=ALU.add),
                   reads=("GBC",), writes=(rfmt.format(dc),), tag="lngb")
        if xout is not None:
            src = RSRC[:, 0:4096].rearrange("p (c t) -> p c t", t=TB)
            dst = xout.rearrange("(c p) t -> p c t", p=128)[:, :, b_ * TB:(b_ + 1) * TB]
            o_("sync", lambda e, a=dst, b2=src: e.dma_start(out=a, in_=b2),
               reads=tuple(rfmt.format(dc) for dc in range(NCH)),
               writes=(xoutres,), dma=True, tag="xspill")

    def emit_routing(b_, tt4, rbk, rbr, rb0):
        tile = 4 * b_ + tt4
        r_ = lambda a, b2: RTS[:, rb0 + a:rb0 + b2]
        G12 = r_(0, 12); E12 = r_(12, 24)
        S4 = r_(24, 25); MX4 = r_(25, 26); R4 = r_(26, 27)
        GP = r_(27, 28); OH4 = r_(28, 32); CUM = r_(32, 33)
        TMP = r_(33, 34); PS4 = r_(36, 40); PM4 = r_(40, 44)
        IE4 = r_(44, 48); PM1 = r_(48, 49); PS1 = r_(49, 50)
        RPS = r_(50, 51); EP = r_(51, 52); WC = r_(52, 53)
        T4A = r_(56, 60); T4B = r_(60, 64)
        RT = f"RTS{rb0 // 64}"

        def v(fn, reads=(RT,), writes=(RT,)):
            o_("vector", fn, reads=reads, writes=writes, tag="rt")

        v(lambda e, t0=tt4: e.tensor_tensor(
            G12, rbk[:, t0 * 12:(t0 + 1) * 12], BRB[:, :], op=ALU.add),
          reads=(rbr, "BRB"))
        o_("scalar", lambda e: e.activation(E12, G12, AF.Exp),
           reads=(RT,), writes=(RT,), tag="rt")
        v(lambda e: e.reduce_sum(S4, E12[:, 0:4], axis=AX.X))
        v(lambda e: e.reduce_max(MX4, E12[:, 0:4], axis=AX.X))
        o_("scalar", lambda e: e.activation(R4, S4, AF.Ln),
           reads=(RT,), writes=(RT,), tag="rt")
        o_("scalar", lambda e: e.activation(R4, R4, AF.Exp, scale=-1.0),
           reads=(RT,), writes=(RT,), tag="rt")
        v(lambda e: e.tensor_tensor(GP, MX4, R4, op=ALU.mult))
        v(lambda e: e.tensor_scalar(OH4, E12[:, 0:4], MX4, None, op0=ALU.is_ge))
        v(lambda e: e.tensor_copy(CUM, OH4[:, 0:1]))
        for g in range(1, 4):
            v(lambda e: e.tensor_scalar(TMP, CUM, -1.0, 1.0,
                                        op0=ALU.mult, op1=ALU.add))
            v(lambda e, g_=g: e.tensor_tensor(OH4[:, g_:g_ + 1],
                                              OH4[:, g_:g_ + 1], TMP, op=ALU.mult))
            if g < 3:
                v(lambda e, g_=g: e.tensor_tensor(CUM, CUM, OH4[:, g_:g_ + 1],
                                                  op=ALU.add))
        Ev = E12[:, 4:12].rearrange("p (g two) -> p g two", two=2)
        v(lambda e: e.tensor_tensor(PS4, Ev[:, :, 0], Ev[:, :, 1], op=ALU.add))
        v(lambda e: e.tensor_tensor(PM4, Ev[:, :, 0], Ev[:, :, 1], op=ALU.max))
        v(lambda e: e.tensor_tensor(IE4, Ev[:, :, 0], Ev[:, :, 1], op=ALU.is_ge))
        v(lambda e: e.tensor_tensor(T4A, OH4, PM4, op=ALU.mult))
        v(lambda e: e.reduce_sum(PM1, T4A, axis=AX.X))
        v(lambda e: e.tensor_tensor(T4A, OH4, PS4, op=ALU.mult))
        v(lambda e: e.reduce_sum(PS1, T4A, axis=AX.X))
        o_("scalar", lambda e: e.activation(RPS, PS1, AF.Ln),
           reads=(RT,), writes=(RT,), tag="rt")
        o_("scalar", lambda e: e.activation(RPS, RPS, AF.Exp, scale=-1.0),
           reads=(RT,), writes=(RT,), tag="rt")
        v(lambda e: e.tensor_tensor(EP, PM1, RPS, op=ALU.mult))
        v(lambda e: e.tensor_tensor(WC, GP, EP, op=ALU.mult))
        cview = CMB[:, tile * 8:(tile + 1) * 8].rearrange("p (g two) -> p g two",
                                                          two=2)
        v(lambda e: e.tensor_tensor(T4A, OH4, IE4, op=ALU.mult))
        v(lambda e: e.tensor_scalar(T4A, T4A, WC, None, op0=ALU.mult))
        v(lambda e, cv=cview: e.tensor_copy(cv[:, :, 0], T4A),
          reads=(RT,), writes=("CMB",))
        v(lambda e: e.tensor_scalar(T4B, IE4, -1.0, 1.0, op0=ALU.mult, op1=ALU.add))
        v(lambda e: e.tensor_tensor(T4B, OH4, T4B, op=ALU.mult))
        v(lambda e: e.tensor_scalar(T4B, T4B, WC, None, op0=ALU.mult))
        v(lambda e, cv=cview: e.tensor_copy(cv[:, :, 1], T4B),
          reads=(RT,), writes=("CMB",))
        v(lambda e, ti=tile: e.tensor_copy(WCTS[:, ti:ti + 1], WC),
          reads=(RT,), writes=("WCTS",))

    def attention(ai, xsp, xres_in, xout, xoutres, WQd, WKd, WVd, WOd,
                  alibi, causal, bias_qkv, bias_o, gb, routing,
                  stop_after=None, tokspill=None):
        kext = 66 if alibi else 64
        pend_tail = [None]
        o_("sync", lambda e, a0=ai: e.dma_start(out=ALIBB[:, :], in_=hcc[a0]),
           writes=("ALIBB",), dma=True, tag="hcld")
        for half in (0, 1):
            wsl_k = wslot()
            load_weight(WKd[0], WKd[1] + half * 512, WKd[1] + half * 512 + 512,
                        wsl_k, "wk")
            wsl_v = wslot()
            load_weight(WVd[0], WVd[1] + half * 512, WVd[1] + half * 512 + 512,
                        wsl_v, "wv")
            if alibi:
                for hh in range(8):
                    h = 8 * half + hh
                    o_("sync",
                       lambda e, h0=h, t0=hh:
                           e.dma_start(out=K_AR[64:66, t0 * S:(t0 + 1) * S],
                                       in_=sl2[h0]),
                       writes=("K_AR",), dma=True, tag="sl2")
            for b_ in range(NB):
                xsl = b_ % 2
                load_xblock(xsp, xres_in, b_, xsl)
                for jjl in range(4):
                    jc = 4 * half + jjl
                    pk = state["pk"] % 2
                    state["pk"] += 1
                    for dc in range(NCH):
                        o_("tensor",
                           lambda e, p0=pk, w0=wsl_k, j0=jjl, d0=dc, x0_=xsl,
                                  st=(dc == 0), sp=(dc == NCH - 1):
                               e.matmul(P[p0][:, :],
                                        W_AR[:, w0 * 4096 + d0 * 512 + j0 * 128:
                                             w0 * 4096 + d0 * 512 + j0 * 128 + 128],
                                        XB[:, x0_ * 4096 + d0 * 512:
                                           x0_ * 4096 + (d0 + 1) * 512],
                                        start=st, stop=sp),
                           reads=(f"W{wsl_k}", f"XB{xsl}"), writes=(f"P{pk}",),
                           tag="kproj")
                    ssl = jjl % 2
                    if bias_qkv:
                        o_("scalar",
                           lambda e, p0=pk, s0=ssl, j0=jc:
                               e.activation(STG[:, s0 * 512:(s0 + 1) * 512],
                                            P[p0][:, :], AF.Identity,
                                            bias=BIC[:, 8 + j0:9 + j0]),
                           reads=(f"P{pk}", "BIC"), writes=(f"STG{ssl}",), tag="kev")
                    else:
                        o_("scalar",
                           lambda e, p0=pk, s0=ssl:
                               e.copy(STG[:, s0 * 512:(s0 + 1) * 512], P[p0][:, :]),
                           reads=(f"P{pk}",), writes=(f"STG{ssl}",), tag="kev")
                    o_("scalar",
                       lambda e, s0=ssl: e.activation(
                           SQ[:, s0 * 512:(s0 + 1) * 512],
                           STG[:, s0 * 512:(s0 + 1) * 512], AF.Square),
                       reads=(f"STG{ssl}",), writes=(f"SQ{ssl}",), tag="ksq")
                    o_("tensor",
                       lambda e, j0=jc, s0=ssl, st=(jjl == 0), sp=(jjl == 3):
                           e.matmul(P[4][0:16, :], ONES16[:, j0 * 16:(j0 + 1) * 16],
                                    SQ[:, s0 * 512:(s0 + 1) * 512],
                                    start=st, stop=sp),
                       reads=(f"SQ{ssl}", "ONES16"), writes=("P4",), tag="kstat")
                    he, ho = 2 * jjl, 2 * jjl + 1
                    if alibi:
                        o_("vector",
                           lambda e, h0=he, s0=ssl, b0=b_:
                               e.tensor_copy(K_AR[0:64, h0 * S + b0 * TB:
                                                  h0 * S + (b0 + 1) * TB],
                                             STG[0:64, s0 * 512:(s0 + 1) * 512]),
                           reads=(f"STG{ssl}",), writes=("K_AR",), tag="kmove")
                        o_("gpsimd",
                           lambda e, h0=ho, s0=ssl, b0=b_:
                               e.dma_start(out=K_AR[0:64, h0 * S + b0 * TB:
                                                   h0 * S + (b0 + 1) * TB],
                                           in_=STG[64:128, s0 * 512:(s0 + 1) * 512]),
                           reads=(f"STG{ssl}",), writes=("K_AR",), dma=True,
                           tag="kmove2")
                    else:
                        o_("vector",
                           lambda e, j0=jjl, s0=ssl, b0=b_:
                               e.tensor_copy(K_AR[:, j0 * S + b0 * TB:
                                                  j0 * S + (b0 + 1) * TB],
                                             STG[:, s0 * 512:(s0 + 1) * 512]),
                           reads=(f"STG{ssl}",), writes=("K_AR",), tag="kmove")
                o_("vector", lambda e: e.reduce_max(SC16[:, 2:3], P[4][0:16, :],
                                                    axis=AX.X),
                   reads=("P4",), writes=("SC16t",), tag="kmax")
                if b_ == 0:
                    o_("vector", lambda e: e.tensor_copy(SC16[:, 1:2], SC16[:, 2:3]),
                       reads=("SC16t",), writes=("SC16k",), tag="kmax2")
                else:
                    o_("vector", lambda e: e.tensor_tensor(
                        SC16[:, 1:2], SC16[:, 1:2], SC16[:, 2:3], op=ALU.max),
                       reads=("SC16t", "SC16k"), writes=("SC16k",), tag="kmax2")
            for b_ in range(NB):
                xsl = b_ % 2
                load_xblock(xsp, xres_in, b_, xsl, tag="xldv")
                for tt4 in range(4):
                    t_abs = 4 * b_ + tt4
                    pk = state["pk"] % 2
                    state["pk"] += 1
                    for dc in range(NCH):
                        o_("tensor",
                           lambda e, p0=pk, w0=wsl_v, t0=tt4, d0=dc, x0_=xsl,
                                  st=(dc == 0), sp=(dc == NCH - 1):
                               e.matmul(P[p0][:, :],
                                        XB[:, x0_ * 4096 + d0 * 512 + t0 * 128:
                                           x0_ * 4096 + d0 * 512 + t0 * 128 + 128],
                                        W_AR[:, w0 * 4096 + d0 * 512:
                                             w0 * 4096 + (d0 + 1) * 512],
                                        start=st, stop=sp),
                           reads=(f"W{wsl_v}", f"XB{xsl}"), writes=(f"P{pk}",),
                           tag="vproj")
                    vview = V_AR[:, t_abs * 520:t_abs * 520 + 8 * 65] \
                        .rearrange("p (h x) -> p h x", x=65)[:, :, 0:64]
                    o_("scalar",
                       lambda e, p0=pk, vv=vview:
                           e.copy(vv, P[p0][:, :].rearrange(
                               "p (h x) -> p h x", x=64)),
                       reads=(f"P{pk}",), writes=("V_AR",), tag="vev")
            wsl_q = wslot()
            load_weight(WQd[0], WQd[1] + half * 512, WQd[1] + half * 512 + 512,
                        wsl_q, "wq")
            for b_ in range(NB):
                xsl = b_ % 2
                load_xblock(xsp, xres_in, b_, xsl, tag="xldq")
                for jjl in range(4):
                    jc = 4 * half + jjl
                    pk = state["pk"] % 2
                    state["pk"] += 1
                    for dc in range(NCH):
                        o_("tensor",
                           lambda e, p0=pk, w0=wsl_q, j0=jjl, d0=dc, x0_=xsl,
                                  st=(dc == 0), sp=(dc == NCH - 1):
                               e.matmul(P[p0][:, :],
                                        W_AR[:, w0 * 4096 + d0 * 512 + j0 * 128:
                                             w0 * 4096 + d0 * 512 + j0 * 128 + 128],
                                        XB[:, x0_ * 4096 + d0 * 512:
                                           x0_ * 4096 + (d0 + 1) * 512],
                                        start=st, stop=sp),
                           reads=(f"W{wsl_q}", f"XB{xsl}"), writes=(f"P{pk}",),
                           tag="qproj")
                    ssl = jjl % 2
                    if bias_qkv:
                        o_("scalar",
                           lambda e, p0=pk, s0=ssl, j0=jc:
                               e.activation(STG[:, s0 * 512:(s0 + 1) * 512],
                                            P[p0][:, :], AF.Identity,
                                            bias=BIC[:, j0:j0 + 1]),
                           reads=(f"P{pk}", "BIC"), writes=(f"STG{ssl}",), tag="qev")
                    else:
                        o_("scalar",
                           lambda e, p0=pk, s0=ssl:
                               e.copy(STG[:, s0 * 512:(s0 + 1) * 512], P[p0][:, :]),
                           reads=(f"P{pk}",), writes=(f"STG{ssl}",), tag="qev")
                    o_("scalar",
                       lambda e, s0=ssl: e.activation(
                           SQ[:, s0 * 512:(s0 + 1) * 512],
                           STG[:, s0 * 512:(s0 + 1) * 512], AF.Square),
                       reads=(f"STG{ssl}",), writes=(f"SQ{ssl}",), tag="qsq")
                    o_("tensor",
                       lambda e, j0=jc, s0=ssl, st=(jjl == 0), sp=(jjl == 3):
                           e.matmul(P[5][0:16, :], ONES16[:, j0 * 16:(j0 + 1) * 16],
                                    SQ[:, s0 * 512:(s0 + 1) * 512],
                                    start=st, stop=sp),
                       reads=(f"SQ{ssl}", "ONES16"), writes=("P5",), tag="qstat")
                    he, ho = 2 * jc, 2 * jc + 1
                    o_("sync",
                       lambda e, h0=he, s0=ssl, b0=b_:
                           e.dma_start(out=qsp[h0, :, b0 * TB:(b0 + 1) * TB],
                                       in_=STG[0:64, s0 * 512:(s0 + 1) * 512]),
                       reads=(f"STG{ssl}",), writes=("qsp",), dma=True, tag="qspill")
                    o_("sync",
                       lambda e, h0=ho, s0=ssl, b0=b_:
                           e.dma_start(out=qsp[h0, :, b0 * TB:(b0 + 1) * TB],
                                       in_=STG[64:128, s0 * 512:(s0 + 1) * 512]),
                       reads=(f"STG{ssl}",), writes=("qsp",), dma=True, tag="qspill")
                o_("vector", lambda e: e.reduce_max(SC16[:, 3:4], P[5][0:16, :],
                                                    axis=AX.X),
                   reads=("P5",), writes=("SC16s",), tag="qmax")
                if b_ == 0:
                    o_("vector", lambda e: e.tensor_copy(SC16[:, 0:1], SC16[:, 3:4]),
                       reads=("SC16s",), writes=("SC16q",), tag="qmax2")
                else:
                    o_("vector", lambda e: e.tensor_tensor(
                        SC16[:, 0:1], SC16[:, 0:1], SC16[:, 3:4], op=ALU.max),
                       reads=("SC16s", "SC16q"), writes=("SC16q",), tag="qmax2")
            o_("vector", lambda e: e.tensor_tensor(SC16[:, 4:5], SC16[:, 0:1],
                                                   SC16[:, 1:2], op=ALU.mult),
               reads=("SC16q", "SC16k"), writes=("SC16p",), tag="bfin")
            o_("scalar", lambda e: e.activation(SC16[:, 5:6], SC16[:, 4:5],
                                                AF.Ln),
               reads=("SC16p",), writes=("SC16l",), tag="bfin")
            o_("scalar", lambda e: e.activation(SC16[:, 6:7], SC16[:, 5:6],
                                                AF.Exp, scale=0.5),
               reads=("SC16l",), writes=("SC16e",), tag="bfin")
            o_("vector", lambda e: e.tensor_scalar(SC16[:, 7:8], SC16[:, 6:7],
                                                   -0.1251250, -0.5,
                                                   op0=ALU.mult, op1=ALU.add),
               reads=("SC16e",), writes=("SC16b",), tag="bfin")
            o_("vector", lambda e: e.tensor_scalar(BCT[:, :], ONE16[:, :],
                                                   SC16[0:16, 7:8], None,
                                                   op0=ALU.mult),
               reads=("SC16b", "ONE16"), writes=("BCT",), tag="bfin")
            c0b = half * 8 * 36
            c1b = (half + 1) * 8 * 36
            o_("tensor",
               lambda e, a0=c0b, a1=c1b: e.matmul(P[5][:, 0:a1 - a0],
                                                  BCT[:, :], EXP16[:, a0:a1],
                                                  start=True, stop=True),
               reads=("BCT", "EXP16"), writes=("P5",), tag="bbc")
            o_("vector",
               lambda e, a0=c0b, a1=c1b: e.tensor_tensor(
                   ALIBB[:, a0:a1], ALIBB[:, a0:a1], P[5][:, 0:a1 - a0],
                   op=ALU.add),
               reads=("P5", "ALIBB"), writes=("ALIBB",), tag="bbc")
            if stop_after == "kv":
                continue
            if not alibi:
                for b_ in range(NB):
                    for pj in range(4):
                        h_e = 8 * half + 2 * pj
                        h_o = h_e + 1
                        pcnt = half * NB * 4 + b_ * 4 + pj
                        qsl = pcnt % 2
                        drain_pre(1)
                        o_("sync",
                           lambda e, h0=h_e, b0=b_, q0=qsl:
                               e.dma_start(out=QS[0:64, q0 * 512:(q0 + 1) * 512],
                                           in_=qsp[h0, :, b0 * TB:(b0 + 1) * TB]),
                           reads=("qsp",), writes=(f"QS{qsl}",), dma=True,
                           tag="qld")
                        o_("sync",
                           lambda e, h0=h_o, b0=b_, q0=qsl:
                               e.dma_start(out=QS[64:128, q0 * 512:(q0 + 1) * 512],
                                           in_=qsp[h0, :, b0 * TB:(b0 + 1) * TB]),
                           reads=("qsp",), writes=(f"QS{qsl}",), dma=True,
                           tag="qld")
                        LAGP = 2

                        def emit_smm(i, pj=pj, qsl=qsl):
                            for (sb_, base, tp) in (
                                    ([0, 1][i % 2], 0, (0, 0)),
                                    ([4, 5][i % 2], 64, (64, 0))):
                                o_("tensor",
                                   lambda e, p0=sb_, j0=pj, c0=i, q0=qsl,
                                          b0=base, t0=tp:
                                       e.matmul(P[p0][:, :],
                                                K_AR[b0:b0 + 64,
                                                     j0 * S + c0 * 128:
                                                     j0 * S + c0 * 128 + 128],
                                                QS[b0:b0 + 64,
                                                   q0 * 512:(q0 + 1) * 512],
                                                start=True, stop=True,
                                                tile_position=t0),
                                   reads=("K_AR", f"QS{qsl}"),
                                   writes=(f"P{sb_}",), tag="smm")

                        def emit_exp(i, h_e=h_e, h_o=h_o):
                            for (sb_, base, hq) in (
                                    ([0, 1][i % 2], 0, h_e),
                                    ([4, 5][i % 2], 64, h_o)):
                                dcol = hq * 36 + 32
                                psl = (2 * i + (0 if base == 0 else 1)) % 4
                                o_("scalar",
                                   lambda e, p0=sb_, s0=psl, cc=dcol:
                                       e.activation(
                                           PTB[:, s0 * 512:(s0 + 1) * 512],
                                           P[p0][:, :], AF.Exp,
                                           bias=ALIBB[:, cc:cc + 1],
                                           scale=0.125),
                                   reads=(f"P{sb_}", "ALIBB"),
                                   writes=(f"PTB{psl}",), tag="exp")

                        def emit_avp(i, pj=pj):
                            for (av, hh2, par) in ((2, 2 * pj, 0),
                                                   (3, 2 * pj + 1, 1)):
                                psl = (2 * i + par) % 4
                                o_("tensor",
                                   lambda e, a0=av, c0=i, hx=hh2, s0=psl,
                                          st=(i == 0), sp=(i == 31):
                                       e.matmul(P[a0][0:65, :],
                                                V_AR[:, c0 * 520 + hx * 65:
                                                     c0 * 520 + hx * 65 + 65],
                                                PTB[:, s0 * 512:(s0 + 1) * 512],
                                                start=st, stop=sp),
                                   reads=("V_AR", f"PTB{psl}"),
                                   writes=(f"P{av}",), tag="avmm")

                        for i in range(32):
                            emit_smm(i)
                            if i == 2 and pend_tail[0] is not None:
                                pend_tail[0]()
                                pend_tail[0] = None
                            if i >= LAGP:
                                emit_avp(i - LAGP)
                            emit_exp(i)
                        for i in range(32 - LAGP, 32):
                            emit_avp(i)
                        if pend_tail[0] is not None:
                            pend_tail[0]()
                            pend_tail[0] = None

                        def mk_tailp(h_e=h_e, h_o=h_o, b_=b_):
                            def tail():
                                for (av, hq, osl) in ((2, h_e, 0), (3, h_o, 1)):
                                    o_("vector",
                                       lambda e, a0=av: e.reciprocal(
                                           OUTS[64:65, 0:512], P[a0][64:65, :]),
                                       reads=(f"P{av}",), writes=("rrow",),
                                       tag="recip")
                                    o_("tensor",
                                       lambda e: e.matmul(P[5][0:64, :],
                                                          OUTS[64:65, 512:576],
                                                          OUTS[64:65, 0:512],
                                                          start=True, stop=True),
                                       reads=("rrow", "RRBonesF"), writes=("P5",),
                                       tag="rbmm")
                                    o_("scalar",
                                       lambda e: e.copy(RB[0:64, :],
                                                        P[5][0:64, :]),
                                       reads=("P5",), writes=("RB",), tag="rbev")
                                    o_("vector",
                                       lambda e, a0=av, s0=osl:
                                           e.tensor_tensor(
                                               OTS[0:64,
                                                   s0 * 512:(s0 + 1) * 512],
                                               P[a0][0:64, :], RB[0:64, :],
                                               op=ALU.mult),
                                       reads=(f"P{av}", "RB"),
                                       writes=(f"OTS{osl}",), tag="onorm")
                                    if bias_qkv:
                                        o_("vector",
                                           lambda e, s0=osl, h0=hq:
                                               e.tensor_scalar(
                                                   OTS[0:64,
                                                       s0 * 512:(s0 + 1) * 512],
                                                   OTS[0:64,
                                                       s0 * 512:(s0 + 1) * 512],
                                                   BVH[:, h0:h0 + 1], None,
                                                   op0=ALU.add),
                                           reads=("BVH",),
                                           writes=(f"OTS{osl}",), tag="obias")
                                    o_("gpsimd",
                                       lambda e, h0=hq, b0=b_, s0=osl:
                                           e.dma_start(
                                               out=osp[64 * h0:64 * h0 + 64,
                                                       b0 * TB:(b0 + 1) * TB],
                                               in_=OTS[0:64,
                                                       s0 * 512:(s0 + 1) * 512]),
                                       reads=(f"OTS{osl}",), writes=("osp",),
                                       dma=True, tag="ospill")
                            return tail
                        pend_tail[0] = mk_tailp()
            else:
              for b_ in range(NB):
                nch = 4 * b_ + 4 if causal else 32
                for hp in range(4):
                    drain_pre(8)
                    hlists = []
                    pair_tails = []
                    for hh in (2 * hp, 2 * hp + 1):
                        h = 8 * half + hh
                        hcnt = half * NB * 8 + b_ * 8 + hh
                        qsl = hcnt % 2
                        avb = 2 + (hcnt % 2)
                        sbk = 0 if hh % 2 == 0 else 4
                        rbk = 1 if hh % 2 == 0 else 5
                        pt0 = 0 if hh % 2 == 0 else 2
                        if causal:
                            dwin = int(110.0 / max(slope_eff_py[h], 1e-9))
                            cmin = max(0, (512 * b_ - 127 - dwin + 127) // 128)
                        else:
                            cmin = 0
                        chunks = list(range(cmin, nch))
                        LAG = 2

                        def av_op(i, chunks=chunks, avb=avb, hh=hh, pt0=pt0):
                            c = chunks[i]
                            psl = pt0 + i % 2
                            o_("tensor",
                               lambda e, a0=avb, c0=c, hx=hh, s0=psl,
                                      st=(i == 0), sp=(i == len(chunks) - 1):
                                   e.matmul(P[a0][0:65, :],
                                            V_AR[:, c0 * 520 + hx * 65:
                                                 c0 * 520 + hx * 65 + 65],
                                            PTB[:, s0 * 512:(s0 + 1) * 512],
                                            start=st, stop=sp),
                               reads=("V_AR", f"PTB{psl}"),
                               writes=(f"P{avb}",), tag="avmm")

                        def body(hh=hh, h=h, qsl=qsl, sbk=sbk, pt0=pt0,
                                 b2=b_, chunks=chunks, av_op=av_op):
                            o_("sync",
                               lambda e, h0=h, b0=b2, q0=qsl:
                                   e.dma_start(
                                       out=QS[0:64, q0 * 512:(q0 + 1) * 512],
                                       in_=qsp[h0, :, b0 * TB:(b0 + 1) * TB]),
                               reads=("qsp",), writes=(f"QS{qsl}",), dma=True,
                               tag="qld")
                            for i in range(len(chunks)):
                                c = chunks[i]
                                psl = pt0 + i % 2
                                o_("tensor",
                                   lambda e, p0=sbk, h0=hh, c0=c, q0=qsl:
                                       e.matmul(P[p0][:, :],
                                                K_AR[0:kext,
                                                     h0 * S + c0 * 128:
                                                     h0 * S + c0 * 128 + 128],
                                                QS[0:kext,
                                                   q0 * 512:(q0 + 1) * 512],
                                                start=True, stop=True),
                                   reads=("K_AR", f"QS{qsl}", "QSj0", "QSj1"),
                                   writes=(f"P{sbk}",), tag="smm")
                                if causal and c >= 4 * b2:
                                    off = 384 - 128 * (c - 4 * b2)
                                    o_("vector",
                                       lambda e, p0=sbk, f0=off:
                                           e.tensor_tensor(
                                               P[p0][:, :], P[p0][:, :],
                                               MASKN[:, f0:f0 + 512],
                                               op=ALU.add),
                                       reads=(f"P{sbk}", "MASKN"),
                                       writes=(f"P{sbk}",), tag="mask")
                                if i >= LAG:
                                    av_op(i - LAG)
                                dcol = h * 36 + ((c - 4 * b2 + 32)
                                                 if causal else 32)
                                o_("scalar",
                                   lambda e, p0=sbk, s0=psl, cc=dcol:
                                       e.activation(
                                           PTB[:, s0 * 512:(s0 + 1) * 512],
                                           P[p0][:, :], AF.Exp,
                                           bias=ALIBB[:, cc:cc + 1],
                                           scale=0.125),
                                   reads=(f"P{sbk}", "ALIBB"),
                                   writes=(f"PTB{psl}",), tag="exp")
                            for i in range(max(0, len(chunks) - LAG),
                                           len(chunks)):
                                av_op(i)
                        hlists.append(collect_ops(body))

                        def mk_tail(avb=avb, rbk=rbk, h=h, b2=b_,
                                    osl=hcnt % 2):
                            def tail():
                                o_("scalar",
                                   lambda e, a0=avb: e.activation(
                                       OUTS[64:65, 0:512], P[a0][64:65, :],
                                       AF.Ln),
                                   reads=(f"P{avb}",), writes=("rrow",),
                                   tag="recip")
                                o_("scalar",
                                   lambda e: e.activation(
                                       OUTS[64:65, 0:512],
                                       OUTS[64:65, 0:512],
                                       AF.Exp, scale=-1.0),
                                   reads=("rrow",), writes=("rrow",),
                                   tag="recip")
                                o_("tensor",
                                   lambda e, r0=rbk: e.matmul(
                                       P[r0][0:64, :],
                                       OUTS[64:65, 512:576],
                                       OUTS[64:65, 0:512],
                                       start=True, stop=True),
                                   reads=("rrow", "RRBonesF"),
                                   writes=(f"P{rbk}",), tag="rbmm")
                                o_("scalar",
                                   lambda e, r0=rbk: e.copy(RB[0:64, :],
                                                            P[r0][0:64, :]),
                                   reads=(f"P{rbk}",), writes=("RB",),
                                   tag="rbev")
                                o_("vector",
                                   lambda e, a0=avb, s0=osl:
                                       e.tensor_tensor(
                                           OTS[0:64,
                                               s0 * 512:(s0 + 1) * 512],
                                           P[a0][0:64, :], RB[0:64, :],
                                           op=ALU.mult),
                                   reads=(f"P{avb}", "RB"),
                                   writes=(f"OTS{osl}",), tag="onorm")
                                o_("gpsimd",
                                   lambda e, h0=h, b0=b2, s0=osl:
                                       e.dma_start(
                                           out=osp[64 * h0:64 * h0 + 64,
                                                   b0 * TB:(b0 + 1) * TB],
                                           in_=OTS[0:64,
                                                   s0 * 512:(s0 + 1) * 512]),
                                   reads=(f"OTS{osl}",), writes=("osp",),
                                   dma=True, tag="ospill")
                            return tail
                        pair_tails.append(mk_tail())
                    la, lb = hlists
                    merged = []
                    for i in range(max(len(la), len(lb))):
                        if i < len(la):
                            merged.append(la[i])
                        if i < len(lb):
                            merged.append(lb[i])
                    emit_seq(merged[:6])
                    if pend_tail[0] is not None:
                        pend_tail[0]()
                        pend_tail[0] = None
                    emit_seq(merged[6:])

                    def mk_pair(tails=tuple(pair_tails)):
                        def both():
                            for t2 in tails:
                                t2()
                        return both
                    pend_tail[0] = mk_pair()
        if pend_tail[0] is not None:
            pend_tail[0]()
            pend_tail[0] = None
        if stop_after in ("kv", "qloop"):
            return
        o_("vector", lambda e: e.memset(SC16[:, 5:6], 0.0),
           writes=("K_AR",) + tuple(f"RLB{dc}" for dc in range(NCH)) +
                  ("AGBC", "AGBD", "SQB0", "SQB1",
                   "XBB0", "XBB1", "TKB0", "TKB1", "XBB0", "XBB1",
                   "TKB0", "TKB1"),
           tag="opfence")
        wsl_o1 = wslot()
        load_weight(WOd[0], WOd[1], WOd[1] + 512, wsl_o1, "wo1")
        wsl_o2 = wslot()
        load_weight(WOd[0], WOd[1] + 512, WOd[1] + 1024, wsl_o2, "wo2")
        ARNB = K_AR[:, 0:4096]
        AGBB_ = K_AR[:, 4096:5120]
        SQB = K_AR[:, 5120:6144]
        XBB = K_AR[:, 6144:14336]
        TKB = (K_AR[:, 14336:15376], K_AR[:, 15376:16416])

        def load_oblock(xsp2, xres2, b2, dst, dres, tag):
            src2 = xsp2.rearrange("(c p) t -> p c t",
                                  p=128)[:, :, b2 * TB:(b2 + 1) * TB]
            d2 = dst.rearrange("p (c t) -> p c t", t=TB)
            o_("sync", lambda e, a=d2, s3=src2: e.dma_start(out=a, in_=s3),
               reads=(xres2,), writes=(dres,), dma=True, tag=tag)

        def o_block(b_):
            par = b_ % 2
            if par == 0:
                ARN, afmt = RLN, "RLN{}"
                agb, agrA, agrB = AGB, "AGBA", "AGBB"
                sqb, sqf = SQ, "SQ{}"
                supb, supr = P[4], "P4"
                sqpb, sqpr = P[5], "P5"
                xbs, x0r, x1r = XB, "XB0", "XB1"
                pko = 0
                tstg = ((STG, ("STG0", "STG1")), (SQT, ("SQT",)))
                rb0 = 0
            else:
                ARN, afmt = ARNB, "RLB{}"
                agb, agrA, agrB = AGBB_, "AGBC", "AGBD"
                sqb, sqf = SQB, "SQB{}"
                supb, supr = P[2], "P2"
                sqpb, sqpr = P[3], "P3"
                xbs, x0r, x1r = XBB, "XBB0", "XBB1"
                pko = 1
                tstg = ((TKB[0], ("TKB0",)), (TKB[1], ("TKB1",)))
                rb0 = 64
            load_oblock(osp, "osp", b_, xbs[:, 0:4096], x0r, "old")
            load_oblock(xsp, xres_in, b_, xbs[:, 4096:8192], x1r, "xres")
            for dc in range(NCH):
                wsl_o = wsl_o1 if dc < 4 else wsl_o2
                dcl = dc % 4
                for kc in range(NCH):
                    o_("tensor",
                       lambda e, p0=pko, w0=wsl_o, d0=dcl, k0=kc, x2=xbs,
                              st=(kc == 0), sp=(kc == NCH - 1):
                           e.matmul(P[p0][:, :],
                                    W_AR[:, w0 * 4096 + k0 * 512 + d0 * 128:
                                         w0 * 4096 + k0 * 512 + d0 * 128 + 128],
                                    x2[:, k0 * 512:(k0 + 1) * 512],
                                    start=st, stop=sp),
                       reads=(f"W{wsl_o}", x0r), writes=(f"P{pko}",),
                       tag="oproj")
                if bias_o:
                    o_("vector",
                       lambda e, p0=pko, d0=dc, a2=ARN, x2=xbs:
                           e.scalar_tensor_tensor(
                               a2[:, d0 * 512:(d0 + 1) * 512], P[p0][:, :],
                               BOC[:, d0:d0 + 1],
                               x2[:, 4096 + d0 * 512:4096 + (d0 + 1) * 512],
                               op0=ALU.add, op1=ALU.add),
                       reads=(f"P{pko}", x1r, "BOC"),
                       writes=(afmt.format(dc),), tag="resid")
                else:
                    o_("vector",
                       lambda e, p0=pko, d0=dc, a2=ARN, x2=xbs:
                           e.tensor_tensor(a2[:, d0 * 512:(d0 + 1) * 512],
                                           P[p0][:, :],
                                           x2[:, 4096 + d0 * 512:
                                              4096 + (d0 + 1) * 512],
                                           op=ALU.add),
                       reads=(f"P{pko}", x1r), writes=(afmt.format(dc),),
                       tag="resid")
                o_("scalar",
                   lambda e, d0=dc, s0=dc % 2, a2=ARN, s2=sqb: e.activation(
                       s2[:, s0 * 512:(s0 + 1) * 512],
                       a2[:, d0 * 512:(d0 + 1) * 512], AF.Square),
                   reads=(afmt.format(dc),), writes=(sqf.format(dc % 2),),
                   tag="osq")
                o_("tensor",
                   lambda e, d0=dc, a2=ARN, pb=supb,
                          st=(dc == 0), sp=(dc == NCH - 1):
                       e.matmul(pb[0:1, :], ONESB[:, 0:1],
                                a2[:, d0 * 512:(d0 + 1) * 512],
                                start=st, stop=sp),
                   reads=(afmt.format(dc), "ONESB"), writes=(supr,),
                   tag="osum")
                o_("tensor",
                   lambda e, s0=dc % 2, s2=sqb, pb=sqpb,
                          st=(dc == 0), sp=(dc == NCH - 1):
                       e.matmul(pb[64:65, :], ONESB[:, 0:1],
                                s2[:, s0 * 512:(s0 + 1) * 512],
                                start=st, stop=sp),
                   reads=(sqf.format(dc % 2), "ONESB"),
                   writes=(sqpr,), tag="osumsq")
            ln_rows(supb, supr, sqpb, sqpr)
            emit_ln_norm(b_, ARN, afmt, gb, xout, xoutres,
                         agb, agrA, agrB, supb, supr, sqpb, sqpr)
            if routing:
                for tt4 in range(4):
                    for dc in range(NCH):
                        o_("tensor",
                           lambda e, t0=tt4, d0=dc, a2=ARN, pb=sqpb,
                                  st=(dc == 0), sp=(dc == NCH - 1):
                               e.matmul(pb[:, t0 * 12:(t0 + 1) * 12],
                                        a2[:, d0 * 512 + t0 * 128:
                                            d0 * 512 + t0 * 128 + 128],
                                        WRT[:, d0 * 12:(d0 + 1) * 12],
                                        start=st, stop=sp),
                           reads=(afmt.format(dc), "WRT"),
                           writes=(sqpr,), tag="rtmm")
                for tt4 in range(4):
                    emit_routing(b_, tt4, sqpb, sqpr, rb0)
            if tokspill is not None:
                tokd, tokres, with_wc = tokspill
                twidth = 1040 if with_wc else 1024
                for tt4 in range(4):
                    stg, swr = tstg[tt4 % 2]
                    for w in range(8):
                        bank = PT7 if w % 2 == 0 else PT8
                        bres = "PT7" if w % 2 == 0 else "PT8"
                        wcol = (w // 2) * 128
                        o_("tensor",
                           lambda e, w0=w, t0=tt4, k2=bank, c2=wcol, a2=ARN:
                               e.transpose(k2[:, c2:c2 + 128],
                                           a2[:, w0 * 512 + t0 * 128:
                                               w0 * 512 + t0 * 128 + 128],
                                           IDENT[:, :]),
                           reads=(afmt.format(w), "IDENT"), writes=(bres,),
                           tag="ttp")
                        o_("scalar",
                           lambda e, w0=w, k2=bank, c2=wcol, s2=stg:
                               e.copy(s2[:, w0 * 128:(w0 + 1) * 128],
                                      k2[:, c2:c2 + 128]),
                           reads=(bres,), writes=swr, tag="tte")
                    if with_wc:
                        o_("vector",
                           lambda e, s2=stg, ti=4 * b_ + tt4:
                               e.tensor_copy(s2[:, 1024:1025],
                                             WCTS[:, ti:ti + 1]),
                           reads=("WCTS",), writes=swr, tag="ttwc")
                        o_("vector",
                           lambda e, s2=stg: e.memset(s2[:, 1025:1040], 0.0),
                           writes=swr, tag="ttwc")
                    o_("sync",
                       lambda e, s2=stg, t0=tt4, b0=b_, tw=twidth:
                           e.dma_start(out=tokd[(4 * b0 + t0) * 128:
                                                (4 * b0 + t0) * 128 + 128,
                                                0:tw],
                                       in_=s2[:, 0:tw]),
                       reads=swr, writes=(tokres,), dma=True, tag="tts")

        SEG0 = {"old", "xres", "oproj", "resid", "osq", "osum", "osumsq"}
        SEG1 = {"ln", "lnag", "lnev", "lnbg"}
        SEG3 = {"ttp", "tte", "ttwc", "tts"}

        def segs(lst):
            s0, s1, s2, s3 = [], [], [], []
            for op in lst:
                tg = op[5]
                if tg in SEG0:
                    s0.append(op)
                elif tg in SEG1:
                    s1.append(op)
                elif tg in SEG3:
                    s3.append(op)
                else:
                    s2.append(op)
            return s0, s1, s2, s3

        def rr2(la2, lb2):
            def chunks(lst):
                out_, cur = [], []
                for op in lst:
                    cur.append(op)
                    if op[5] == "tts":
                        out_.append(cur)
                        cur = []
                if cur:
                    out_.append(cur)
                return out_
            ca, cb = chunks(la2), chunks(lb2)
            for i in range(max(len(ca), len(cb))):
                if i < len(ca):
                    emit_seq(ca[i])
                if i < len(cb):
                    emit_seq(cb[i])

        for bp in range(0, NB, 2):
            s0a, s1a, s2a, s3a = segs(collect_ops(lambda: o_block(bp)))
            s0b, s1b, s2b, s3b = segs(collect_ops(lambda: o_block(bp + 1)))
            emit_rr(s0a, s0b)
            emit_seq(s1a)
            emit_rr(s2a, s1b + s2b)
            rr2(s3a, s3b)

    NT = 15

    def Hap(fc):
        if fc < 8:
            return RLN[:, fc * 512:(fc + 1) * 512]
        return K_AR[:, 12288 + (fc - 8) * 512:12288 + (fc - 7) * 512]

    def emit_moe():
        drain_pre(10 ** 9)
        yres = tuple(f"Y{t}" for t in range(NT))
        o_("vector", lambda e: e.memset(SC16[:, 2:3], 0.0),
           reads=("K_AR", "V_AR", "RB"),
           writes=yres + tuple(f"RLN{dc}" for dc in range(NCH)) +
                  tuple(f"RLB{dc}" for dc in range(NCH)) +
                  tuple(f"H{fc}" for fc in range(16)) +
                  ("AGBC", "AGBD", "SQB0", "SQB1",
                   "XBB0", "XBB1", "TKB0", "TKB1",
                   "K_AR", "V_AR", "RB", "W1F0", "W1F1", "W2F",
                   "W0", "W1", "W2", "W3", "XGT0", "XGT1", "WCROW",
                   "ONE8", "CUM8", "YGT"),
           tag="fence")
        for b_ in range(NB):
            cb = PT7 if b_ % 2 == 0 else PT8
            cbres = "PT7" if b_ % 2 == 0 else "PT8"
            for tt4 in range(4):
                tile = 4 * b_ + tt4
                o_("tensor",
                   lambda e, t0=tt4, ti=tile, k2=cb:
                       e.transpose(k2[0:8, t0 * 128:(t0 + 1) * 128],
                                   CMB[:, ti * 8:(ti + 1) * 8], IDENT[:, :]),
                   reads=("CMB", "IDENT"), writes=(cbres,), tag="ctp")
            o_("scalar",
               lambda e, b0=b_, k2=cb:
                   e.copy(CMTR[:, b0 * 512:(b0 + 1) * 512], k2[0:8, 0:512]),
               reads=(cbres,), writes=(f"CMT{b_}",), tag="ctev")
        rfr = tuple(f"CMT{b2}" for b2 in range(NB))
        o_("vector", lambda e: e.tensor_scalar(ONE8, CMTR[:, 0:S], 0.0, None,
                                               op0=ALU.is_gt),
           reads=rfr, writes=("ONE8",), tag="rf")
        o_("vector", lambda e: e.tensor_tensor_scan(CUM8, ONE8, ONE8, 0.0,
                                                    op0=ALU.add,
                                                    op1=ALU.bypass),
           reads=("ONE8",), writes=("CUM8",), tag="rf")
        o_("tensor", lambda e: e.matmul(P[4][0:1, 0:128], CUM8[:, S - 1:S],
                                        IDENT32[0:8, 0:128],
                                        start=True, stop=True),
           reads=("CUM8", "IDENT32"), writes=("P4",), tag="rfmm")
        o_("scalar", lambda e: e.copy(mr(0, 8), P[4][0:1, 0:8]),
           reads=("P4",), writes=("MROWc",), tag="rf")
        o_("vector", lambda e: e.memset(mr(8, 16), 0.0),
           writes=("MROWt",), tag="rf")
        for k_ in range(NT):
            o_("vector",
               lambda e, k0=k_: e.tensor_scalar(mr(56, 64),
                                                mr(0, 8),
                                                512.0 * k0, None,
                                                op0=ALU.is_gt),
               reads=("MROWc",), writes=("MROWx",), tag="rf")
            o_("vector",
               lambda e: e.tensor_tensor(mr(8, 16),
                                         mr(8, 16),
                                         mr(56, 64), op=ALU.add),
               reads=("MROWx",), writes=("MROWt",), tag="rf")
        o_("vector", lambda e: e.tensor_tensor_scan(mr(16, 24),
                                                    mr(8, 16),
                                                    mr(8, 16), 0.0,
                                                    op0=ALU.add,
                                                    op1=ALU.bypass),
           reads=("MROWt",), writes=("MROWi",), tag="rf")
        o_("vector", lambda e: e.tensor_tensor(mr(24, 32),
                                               mr(16, 24),
                                               mr(8, 16),
                                               op=ALU.subtract),
           reads=("MROWi", "MROWt"), writes=("MROWs",), tag="rf")
        o_("vector", lambda e: e.tensor_scalar(mr(0, 8),
                                               mr(24, 32), 512.0,
                                               None, op0=ALU.mult),
           reads=("MROWs",), writes=("MROWb",), tag="rf")
        o_("vector", lambda e: e.memset(mr(40, 56), -1.0),
           writes=("MROWe",), tag="rf")
        for e_i in range(NE):
            o_("vector",
               lambda e, e0=e_i: e.tensor_scalar(
                   mr(56, 72), iot16_r,
                   mr(24 + e0, 25 + e0), None, op0=ALU.is_ge),
               reads=("MROWs", "ROWC"), writes=("MROWx",), tag="rf")
            o_("vector",
               lambda e: e.tensor_tensor(mr(40, 56),
                                         mr(40, 56),
                                         mr(56, 72), op=ALU.add),
               reads=("MROWx",), writes=("MROWe",), tag="rf")
        o_("tensor", lambda e: e.matmul(P[4][0:8, 128:129], mr(0, 8),
                                        one1_r, start=True, stop=True),
           reads=("MROWb", "ROWC"), writes=("P4",), tag="rfmm")
        o_("scalar", lambda e: e.copy(BASEC[0:8, 0:1], P[4][0:8, 128:129]),
           reads=("P4",), writes=("BASECr",), tag="rf")
        o_("vector", lambda e: e.tensor_scalar(CUM8, CUM8, BASEC[0:8, 0:1],
                                               -1.0, op0=ALU.add,
                                               op1=ALU.add),
           reads=("BASECr", "CUM8"), writes=("CUM8",), tag="rf")
        o_("vector", lambda e: e.tensor_tensor(CUM8, CUM8, ONE8, op=ALU.mult),
           reads=("ONE8", "CUM8"), writes=("CUM8",), tag="rf")
        o_("vector", lambda e: e.memset(ONES8F[:, 0:1], 1.0),
           writes=("O8F",), tag="rf")
        for ch in range(8):
            o_("tensor",
               lambda e, c0=ch: e.matmul(P[5][0:1, 0:512], ONES8F[0:8, 0:1],
                                         CUM8[:, c0 * 512:(c0 + 1) * 512],
                                         start=True, stop=True),
               reads=("CUM8", "O8F"), writes=("P5",), tag="rfmm")
            o_("vector",
               lambda e: e.tensor_copy(IRV[0:1, :], P[5][0:1, 0:512]),
               reads=("P5",), writes=("IROW",), tag="rf")
            o_("sync",
               lambda e, c0=ch: e.dma_start(
                   out=ib1[0:1, c0 * 512:(c0 + 1) * 512],
                   in_=IRV[0:1, :]),
               reads=("IROW",), writes=("ib1",), dma=True, tag="rf")

        def _reload(e):
            with nc.allow_non_contiguous_dma(reason="16KB index repartition"):
                return e.dma_start(out=DESTC[:, 0:32],
                                   in_=ib1.rearrange("o (t p) -> p t o",
                                                     p=128))
        o_("gpsimd", _reload, reads=("ib1",), writes=("DESTC",), dma=True,
           tag="rfr")
        o_("tensor", lambda e: e.matmul(P[4][:, 256:272], ones128,
                                        mr(40, 56),
                                        start=True, stop=True),
           reads=("MROWe", "ROWC"), writes=("P4",), tag="rfmm")
        o_("vector", lambda e: e.tensor_scalar(EIDF[:, 0:16], P[4][:, 256:272],
                                               128.0, None, op0=ALU.mult),
           reads=("P4",), writes=("EIDF",), tag="rf")
        o_("vector", lambda e: e.tensor_scalar(EIDF[:, 0:16], EIDF[:, 0:16],
                                               IOCF[:, 0:1], None,
                                               op0=ALU.add),
           reads=("IOCF", "EIDF"), writes=("EIDF",), tag="rf")
        o_("vector", lambda e: e.tensor_copy(EIDCI[:, 0:16], EIDF[:, 0:16]),
           reads=("EIDF",), writes=("EIDCI",), tag="rf")
        for tt in range(32):
            xsl = tt % 2
            o_("sync",
               lambda e, t0=tt, x0_=xsl * 4096:
                   e.dma_start(out=XB[:, x0_:x0_ + 1040],
                               in_=x2tok[t0 * 128:(t0 + 1) * 128, :]),
               reads=("x2tok",), writes=(f"XB{xsl}",), dma=True, tag="scl")
            o_("gpsimd",
               lambda e, t0=tt, x0_=xsl * 4096:
                   e.indirect_dma_start(
                       out=xg,
                       out_offset=IndirectOffsetOnAxis(DESTC[:, t0:t0 + 1], 0),
                       in_=XB[:, x0_:x0_ + 1040], in_offset=None),
               reads=("DESTC", f"XB{xsl}"), writes=("xg",), dma=True,
               tag="sct")
        o_("vector", lambda e: e.memset(SC16[:, 3:4], 0.0),
           writes=("ONE8", "CUM8", "W1F0", "W1F1", "XB0", "XB1",
                   "XGT0", "XGT1") +
                  tuple(f"H{fc}" for fc in range(16)),
           tag="fence2")
        STAGE = ((XGL, ("XGL",)), (STG, ("STG0", "STG1")),
                 (SQT, ("SQT",)), (PTB, ("PTB0", "PTB1", "PTB2")))

        def emit_xgl(j):
            for s in range(4):
                stg, swr = STAGE[s]
                o_("sync",
                   lambda e, j0=j, s0=s, s2=stg:
                       e.dma_start(out=s2[:, 0:1040],
                                   in_=xg[j0 * 512 + s0 * 128:
                                          j0 * 512 + s0 * 128 + 128, :]),
                   reads=("xg",), writes=swr, dma=True, tag="xgl")

        emit_xgl(0)
        for j in range(NT):
            xpar = j % 2
            xga = XB[:, xpar * 4096:(xpar + 1) * 4096]
            xgr = f"XGT{xpar}"
            o_("gpsimd",
               lambda e, j0=j: e.indirect_dma_start(
                   out=V_AR[:, 0:8192], out_offset=None, in_=w1b0,
                   in_offset=IndirectOffsetOnAxis(EIDCI[:, j0:j0 + 1], 0)),
               reads=("EIDCI", "w1b0"), writes=("W1F0",), dma=True, tag="w1g")
            o_("gpsimd",
               lambda e, j0=j: e.indirect_dma_start(
                   out=V_AR[:, 8192:16384], out_offset=None, in_=w1b1,
                   in_offset=IndirectOffsetOnAxis(EIDCI[:, j0:j0 + 1], 0)),
               reads=("EIDCI", "w1b1"), writes=("W1F1",), dma=True, tag="w1g")
            o_("gpsimd",
               lambda e, j0=j: e.indirect_dma_start(
                   out=W_AR[:, 0:16384], out_offset=None, in_=w2b,
                   in_offset=IndirectOffsetOnAxis(EIDCI[:, j0:j0 + 1], 0)),
               reads=("EIDCI", "w2b"), writes=("W2F",), dma=True, tag="w2g")
            o_("gpsimd",
               lambda e, j0=j: e.indirect_dma_start(
                   out=B1T[:, 0:24], out_offset=None, in_=b12r,
                   in_offset=IndirectOffsetOnAxis(EIDCI[:, j0:j0 + 1], 0)),
               reads=("EIDCI",), writes=("B1T",), dma=True, tag="bg")
            for sct in range(4):
                stg, swr = STAGE[sct]
                for w in range(9):
                    bank = PT7 if w % 2 == 0 else PT8
                    bres = "PT7" if w % 2 == 0 else "PT8"
                    wcol = (w // 2) * 128
                    if w < 8:
                        o_("tensor",
                           lambda e, w0=w, k2=bank, c2=wcol, s2=stg:
                               e.transpose(k2[:, c2:c2 + 128],
                                           s2[:, w0 * 128:(w0 + 1) * 128],
                                           IDENT[:, :]),
                           reads=swr + ("IDENT",), writes=(bres,), tag="xgt")
                        o_("scalar",
                           lambda e, w0=w, s0=sct, k2=bank, c2=wcol, x2=xga:
                               e.copy(x2[:, w0 * 512 + s0 * 128:
                                         w0 * 512 + s0 * 128 + 128],
                                      k2[:, c2:c2 + 128]),
                           reads=(bres,), writes=(xgr,), tag="xge")
                    else:
                        o_("tensor",
                           lambda e, s2=stg: e.matmul(P[4][0:1, 384:512],
                                                      s2[:, 1024:1025],
                                                      IDENT[:, :],
                                                      start=True, stop=True),
                           reads=swr + ("IDENT",), writes=("P4",),
                           tag="xgt")
                        o_("scalar",
                           lambda e, s0=sct:
                               e.copy(OTS[64:65, s0 * 128:(s0 + 1) * 128],
                                      P[4][0:1, 384:512]),
                           reads=("P4",), writes=("WCROW",), tag="wce")
            if j + 1 < NT:
                emit_xgl(j + 1)
            o_("tensor", lambda e: e.matmul(P[5][:, :], OTS[64:65, 512:640],
                                            OTS[64:65, 0:512],
                                            start=True, stop=True),
               reads=("WCROW", "RRBones"), writes=("P5",), tag="cbmm")
            o_("scalar", lambda e: e.copy(RB[:, :], P[5][:, :]),
               reads=("P5",), writes=("RB",), tag="cbev")
            for fc in range(16):
                pk = state["pk"] % 2
                state["pk"] += 1
                fh, fcl = fc // 8, fc % 8
                for dc in range(NCH):
                    o_("tensor",
                       lambda e, p0=pk, h0=fh, f0=fcl, d0=dc, x2=xga,
                              st=(dc == 0), sp=(dc == NCH - 1):
                           e.matmul(P[p0][:, :],
                                    V_AR[:, h0 * 8192 + d0 * 1024 + f0 * 128:
                                         h0 * 8192 + d0 * 1024 +
                                         f0 * 128 + 128],
                                    x2[:, d0 * 512:(d0 + 1) * 512],
                                    start=st, stop=sp),
                       reads=(f"W1F{fh}", xgr), writes=(f"P{pk}",),
                       tag="hmm")
                o_("scalar",
                   lambda e, p0=pk, f0=fc:
                       e.activation(Hap(f0), P[p0][:, :], AF.Relu,
                                    bias=B1T[:, f0:f0 + 1]),
                   reads=(f"P{pk}", "B1T"), writes=(f"H{fc}",), tag="relu")
            for dc2 in range(NCH):
                pk = state["pk"] % 2
                state["pk"] += 1
                for fc in range(16):
                    o_("tensor",
                       lambda e, p0=pk, f0=fc, d0=dc2,
                              st=(fc == 0), sp=(fc == 15):
                           e.matmul(P[p0][:, :],
                                    W_AR[:, f0 * 1024 + d0 * 128:
                                         f0 * 1024 + d0 * 128 + 128],
                                    Hap(f0), start=st, stop=sp),
                       reads=("W2F", f"H{fc}"), writes=(f"P{pk}",), tag="ymm")
                o_("scalar",
                   lambda e, p0=pk, d0=dc2:
                       e.activation(K_AR[:, d0 * 512:(d0 + 1) * 512],
                                    P[p0][:, :], AF.Identity,
                                    bias=B1T[:, 16 + d0:17 + d0]),
                   reads=(f"P{pk}", "B1T"), writes=("YGT",), tag="yev")
                o_("vector",
                   lambda e, d0=dc2:
                       e.tensor_tensor(K_AR[:, d0 * 512:(d0 + 1) * 512],
                                       K_AR[:, d0 * 512:(d0 + 1) * 512],
                                       RB[:, :], op=ALU.mult),
                   reads=("RB", "YGT"), writes=("YGT",), tag="yscale")
            for sct in range(4):
                for w in range(8):
                    bank = PT7 if w % 2 == 0 else PT8
                    bres = "PT7" if w % 2 == 0 else "PT8"
                    wcol = (w // 2) * 128
                    o_("tensor",
                       lambda e, w0=w, s0=sct, k2=bank, c2=wcol:
                           e.transpose(k2[:, c2:c2 + 128],
                                       K_AR[:, w0 * 512 + s0 * 128:
                                            w0 * 512 + s0 * 128 + 128],
                                       IDENT[:, :]),
                       reads=("YGT", "IDENT"), writes=(bres,), tag="ygt")
                    o_("scalar",
                       lambda e, w0=w, s0=sct, k2=bank, c2=wcol:
                           e.copy(K_AR[:, 4096 + s0 * 1024 + w0 * 128:
                                       4096 + s0 * 1024 + w0 * 128 + 128],
                                  k2[:, c2:c2 + 128]),
                       reads=(bres,), writes=(f"YTK{sct}",), tag="yge")
                o_("sync",
                   lambda e, j0=j, s0=sct:
                       e.dma_start(out=yg[j0 * 512 + s0 * 128:
                                          j0 * 512 + s0 * 128 + 128, :],
                                   in_=K_AR[:, 4096 + s0 * 1024:
                                            4096 + (s0 + 1) * 1024]),
                   reads=(f"YTK{sct}",), writes=(f"Y{j}",), dma=True,
                   tag="ygs")

    def emit_final():
        o_("vector", lambda e: e.memset(SC16[:, 4:5], 0.0),
           writes=("XB0", "XB1", "XGT0", "XGT1", "rrow", "OUTS",
                   "RRBonesF", "IROW", "XF0", "XF1", "XF2", "XF3"),
           tag="fence3")
        FSQ = ((SQT, ("SQT",)), (STG, ("STG0", "STG1")),
               (XGL, ("XGL",)), (PTB, ("PTB0", "PTB1")))

        def tile_ops(tt):
            ops = []

            def a_(eng, fn, reads=(), writes=(), dma=False, tag="f"):
                ops.append((eng, fn, tuple(reads), tuple(writes), dma, tag))

            q = tt % 4
            xo = q * 2048
            mb = q * 8
            sfx = str(q)
            sq, sqr = FSQ[q]
            xbr = f"XF{q}"

            def lncols(tag):
                a_("vector",
                   lambda e: e.tensor_scalar(MVC[:, mb + 2:mb + 3],
                                             MVC[:, mb:mb + 1],
                                             1.0 / D, None, op0=ALU.mult),
                   (f"MV0{sfx}",), (f"MV2{sfx}",), False, tag)
                a_("vector",
                   lambda e: e.tensor_scalar(MVC[:, mb + 3:mb + 4],
                                             MVC[:, mb + 1:mb + 2],
                                             1.0 / D, None, op0=ALU.mult),
                   (f"MV1{sfx}",), (f"MV3{sfx}",), False, tag)
                a_("vector",
                   lambda e: e.tensor_tensor(MVC[:, mb + 6:mb + 7],
                                             MVC[:, mb + 2:mb + 3],
                                             MVC[:, mb + 2:mb + 3],
                                             op=ALU.mult),
                   (f"MV2{sfx}",), (f"MV6{sfx}",), False, tag)
                a_("vector",
                   lambda e: e.tensor_tensor(MVC[:, mb + 3:mb + 4],
                                             MVC[:, mb + 3:mb + 4],
                                             MVC[:, mb + 6:mb + 7],
                                             op=ALU.subtract),
                   (f"MV6{sfx}", f"MV3{sfx}"), (f"MV3{sfx}",), False, tag)
                a_("vector",
                   lambda e: e.tensor_scalar_add(MVC[:, mb + 3:mb + 4],
                                                 MVC[:, mb + 3:mb + 4],
                                                 LN_EPS),
                   (f"MV3{sfx}",), (f"MV3{sfx}",), False, tag)
                a_("scalar",
                   lambda e: e.activation(MVC[:, mb + 3:mb + 4],
                                          MVC[:, mb + 3:mb + 4], AF.Ln),
                   (f"MV3{sfx}",), (f"MV3{sfx}",), False, tag)
                a_("scalar",
                   lambda e: e.activation(MVC[:, mb + 4:mb + 5],
                                          MVC[:, mb + 3:mb + 4], AF.Exp,
                                          scale=-0.5),
                   (f"MV3{sfx}",), (f"MV4{sfx}",), False, tag)
                a_("vector",
                   lambda e: e.tensor_tensor(MVC[:, mb + 5:mb + 6],
                                             MVC[:, mb + 2:mb + 3],
                                             MVC[:, mb + 4:mb + 5],
                                             op=ALU.mult),
                   (f"MV2{sfx}", f"MV4{sfx}"), (f"MV5{sfx}",), False, tag)
                a_("vector",
                   lambda e: e.tensor_scalar(MVC[:, mb + 5:mb + 6],
                                             MVC[:, mb + 5:mb + 6],
                                             -1.0, None, op0=ALU.mult),
                   (f"MV5{sfx}",), (f"MV5{sfx}",), False, tag)

            a_("gpsimd",
               lambda e: e.indirect_dma_start(
                   out=XB[:, xo:xo + 1024], out_offset=None, in_=yg,
                   in_offset=IndirectOffsetOnAxis(DESTC[:, tt:tt + 1], 0)),
               ("DESTC",) + tuple(f"Y{j2}" for j2 in range(NT)),
               (xbr,), True, "ygr")
            a_("sync",
               lambda e: e.dma_start(out=XB[:, xo + 1024:xo + 2048],
                                     in_=x2tok[tt * 128:(tt + 1) * 128,
                                               0:1024]),
               ("x2tok",), (xbr,), True, "f2ld")
            a_("vector",
               lambda e: e.tensor_tensor(XB[:, xo:xo + 1024],
                                         XB[:, xo:xo + 1024],
                                         XB[:, xo + 1024:xo + 2048],
                                         op=ALU.add),
               (xbr,), (xbr,), False, "r3")
            a_("sync",
               lambda e: e.dma_start(out=XB[:, xo + 1024:xo + 2048],
                                     in_=x1tok[tt * 128:(tt + 1) * 128, :]),
               ("x1tok", xbr), (xbr,), True, "f1ld")
            a_("vector",
               lambda e: e.reduce_sum(MVC[:, mb:mb + 1],
                                      XB[:, xo:xo + 1024], axis=AX.X),
               (xbr,), (f"MV0{sfx}",), False, "f3ln")
            a_("scalar",
               lambda e: e.activation(sq[:, 0:1024], XB[:, xo:xo + 1024],
                                      AF.Square),
               (xbr,), sqr, False, "f3sq")
            a_("vector",
               lambda e: e.reduce_sum(MVC[:, mb + 1:mb + 2], sq[:, 0:1024],
                                      axis=AX.X),
               sqr, (f"MV1{sfx}",), False, "f3ln")
            lncols("f3ln")
            a_("vector",
               lambda e: e.tensor_scalar(XB[:, xo:xo + 1024],
                                         XB[:, xo:xo + 1024],
                                         MVC[:, mb + 4:mb + 5],
                                         MVC[:, mb + 5:mb + 6],
                                         op0=ALU.mult, op1=ALU.add),
               (f"MV4{sfx}", f"MV5{sfx}"), (xbr,), False, "f3n1")
            a_("vector",
               lambda e: e.tensor_tensor(XB[:, xo:xo + 1024],
                                         XB[:, xo:xo + 1024],
                                         CMTX[:, 0:1024], op=ALU.mult),
               ("G2B2",), (xbr,), False, "f3n2")
            a_("vector",
               lambda e: e.tensor_tensor(XB[:, xo:xo + 1024],
                                         XB[:, xo:xo + 1024],
                                         CMTX[:, 1024:2048], op=ALU.add),
               ("G2B2",), (xbr,), False, "f3n3")
            a_("vector",
               lambda e: e.tensor_tensor(XB[:, xo:xo + 1024],
                                         XB[:, xo:xo + 1024],
                                         XB[:, xo + 1024:xo + 2048],
                                         op=ALU.add),
               (xbr,), (xbr,), False, "r4")
            a_("vector",
               lambda e: e.reduce_sum(MVC[:, mb:mb + 1],
                                      XB[:, xo:xo + 1024], axis=AX.X),
               (xbr,), (f"MV0{sfx}",), False, "f4ln")
            a_("scalar",
               lambda e: e.activation(sq[:, 0:1024], XB[:, xo:xo + 1024],
                                      AF.Square),
               (xbr,), sqr, False, "f4sq")
            a_("vector",
               lambda e: e.reduce_sum(MVC[:, mb + 1:mb + 2], sq[:, 0:1024],
                                      axis=AX.X),
               sqr, (f"MV1{sfx}",), False, "f4ln")
            lncols("f4ln")
            a_("vector",
               lambda e: e.tensor_scalar(OUTS[:, :], XB[:, xo:xo + 1024],
                                         MVC[:, mb + 4:mb + 5],
                                         MVC[:, mb + 5:mb + 6],
                                         op0=ALU.mult, op1=ALU.add),
               (f"MV4{sfx}", f"MV5{sfx}", xbr), ("OUTS",), False, "fnorm")
            a_("sync",
               lambda e: e.dma_start(out=out[tt * 128:(tt + 1) * 128, :],
                                     in_=OUTS[:, :]),
               ("OUTS",), ("out",), True, "ostore")
            return ops

        for t0 in range(0, 32, 4):
            tls = [tile_ops(t0 + i) for i in range(4)]
            emit_rr(*[tl[:-2] for tl in tls])
            for tl in tls:
                emit_seq(tl[-2:])

    if upto != "a0":
        attention(0, x0sp, "x0sp", x1sp, "x1_out", (qw, 0), (kw, 0), (vw, 0),
                  (ow, 0), alibi=True, causal=True, bias_qkv=False,
                  bias_o=False, gb=None, routing=False,
                  stop_after=upto if upto in ("kv", "qloop") else None,
                  tokspill=(x1tok, "x1tok", False))
    if upto in ("x2", "all"):
        o_("tensor", lambda e: e.matmul(P[5][:, 0:12], ones128, brt_row,
                                        start=True, stop=True),
           reads=("ROWC",), writes=("P5",), tag="brt")
        o_("scalar", lambda e: e.copy(BRB[:, :], P[5][:, 0:12]),
           reads=("P5",), writes=("BRB",), tag="brt")
        x2o = x2sp if upto == "x2" else None
        attention(1, x1sp, "x1_out", x2o, "x2_out", (wiT, 0), (wiT, 1024),
                  (wiT, 2048), (woT, 0), alibi=False, causal=False,
                  bias_qkv=True, bias_o=True, gb=(0, 8), routing=True,
                  tokspill=(x2tok, "x2tok", True))
    if upto == "all":
        emit_moe()
        emit_final()
        sink_reads = ("out", "x1_out")
    elif upto == "x2":
        sink_reads = ("x1_out", "x2_out", "x1tok", "x2tok")
    elif upto == "a0":
        sink_reads = ("x0sp",)
    elif upto == "kv":
        sink_reads = ("qsp", "x0sp", "K_AR")
    elif upto == "qloop":
        sink_reads = ("osp", "qsp", "x0sp")
    else:
        sink_reads = ("x1_out",)
    o_("sync", lambda e: e.nop(), reads=sink_reads, tag="sink")

    bld.emit(nc)
    es.close()
    return nc, bld


def host_inputs(inputs, core):
    f32 = lambda a: np.ascontiguousarray(np.asarray(a, dtype=np.float32))
    bf16 = lambda a: np.ascontiguousarray(np.asarray(a).astype(ml_dtypes.bfloat16))
    x = f32(inputs["L0_attn_x"])[core]
    sl = _slopes()
    s8 = np.array([np.float32(ml_dtypes.bfloat16(-8.0 * s)) for s in sl],
                  dtype=np.float32)
    slope_eff = (-s8 / 8.0).astype(np.float64)

    d = {"x0": x}
    d["qw"] = f32(inputs["L0_attn_q_w"])
    d["kw"] = f32(inputs["L0_attn_k_w"])
    d["vw"] = f32(inputs["L0_attn_v_w"])
    d["ow"] = f32(inputs["L0_attn_o_w"])
    d["wiT"] = f32(np.asarray(inputs["L0_ffn_in_proj_w"], np.float32).T)
    d["woT"] = f32(np.asarray(inputs["L0_ffn_out_proj_w"], np.float32).T)
    bi = f32(inputs["L0_ffn_in_proj_b"])
    d["bi_cols"] = f32(bi.reshape(24, 128).T)
    bo = f32(inputs["L0_ffn_out_proj_b"])
    d["bo_cols"] = f32(bo.reshape(8, 128).T)
    rows32 = np.zeros((1, 3688), np.float32)
    rows32[0, 0:512] = 1.0
    bg = f32(inputs["L0_ffn_b_group"])
    be = f32(inputs["L0_ffn_b_expert"]).reshape(-1)
    rows32[0, 2048:2052] = bg
    rows32[0, 2052:2060] = be
    rows32[0, 3600:3616] = np.arange(16, dtype=np.float32)
    d["rows32_c"] = rows32
    gbc = np.zeros((128, 32), np.float32)
    gbc[:, 0:8] = f32(inputs["L0_ffn_ln1_g"]).reshape(8, 128).T
    gbc[:, 8:16] = f32(inputs["L0_ffn_ln1_b"]).reshape(8, 128).T
    gbc[:, 16:24] = f32(inputs["L0_ffn_ln2_g"]).reshape(8, 128).T
    gbc[:, 24:32] = f32(inputs["L0_ffn_ln2_b"]).reshape(8, 128).T
    d["gbc_c"] = gbc
    d["bvh_c"] = np.ascontiguousarray(bi[2048:3072].reshape(16, 64).T)
    wg = f32(inputs["L0_ffn_w_group"])
    we = f32(inputs["L0_ffn_w_expert"])
    d["wroute"] = np.ascontiguousarray(
        np.concatenate([wg, we.transpose(1, 0, 2).reshape(D, 8)], axis=1))
    w1a = f32(inputs["L0_ffn_w1"])
    d["w1"] = np.ascontiguousarray(
        w1a.reshape(NE, 8, 128, 2, 1024).transpose(0, 2, 3, 1, 4)
        .reshape(NE * 128, 16384))
    w2a = f32(inputs["L0_ffn_w2"])
    d["w2"] = np.ascontiguousarray(
        w2a.reshape(NE, 16, 128, 1024).transpose(0, 2, 1, 3)
        .reshape(NE * 128, 16384))
    b1 = f32(inputs["L0_ffn_b1"])
    b2a = f32(inputs["L0_ffn_b2"])
    b12 = np.zeros((NE, 128, 24), np.float32)
    b12[:, :, 0:16] = b1.reshape(NE, 16, 128).transpose(0, 2, 1)
    b12[:, :, 16:24] = b2a.reshape(NE, 8, 128).transpose(0, 2, 1)
    d["b12r"] = np.ascontiguousarray(b12.reshape(NE * 128, 24))
    gb2 = np.zeros((128, 2048), np.float32)
    gb2[:, 0:1024] = f32(inputs["L0_ffn_ln2_g"]).reshape(-1)[None, :]
    gb2[:, 1024:2048] = f32(inputs["L0_ffn_ln2_b"]).reshape(-1)[None, :]
    d["g2b2b"] = gb2
    d["iotac"] = np.arange(128, dtype=np.float32).reshape(128, 1)
    d["identf_c"] = np.eye(128, dtype=np.float32)
    d["ident_c"] = np.eye(128, dtype=ml_dtypes.bfloat16)
    o16 = np.zeros((128, 128), np.float32)
    for jc in range(8):
        for r in range(128):
            o16[r, jc * 16 + 2 * jc + (r // 64)] = 1.0
    d["ones16_c"] = bf16(o16)
    d["one16c"] = bf16(np.ones((16, 128), np.float32))
    ex16 = np.zeros((16, 576), np.float32)
    for h in range(16):
        ex16[h, h * 36:(h + 1) * 36] = 1.0
    d["exp16_c"] = bf16(ex16)
    hcc = np.zeros((2, 128, 576), np.float32)
    ii = np.arange(128, dtype=np.float64)
    for h in range(16):
        for di in range(36):
            delta = di - 32
            hcc[0, :, h * 36 + di] = (slope_eff[h] * (ii + 128.0 * delta)
                                      ).astype(np.float32)
    d["hcc"] = hcc
    sl2 = np.zeros((NH, 2, S), np.float32)
    for h in range(16):
        sl2[h, :, :] = s8[h]
    d["sl2"] = bf16(sl2)
    j = np.arange(512, dtype=np.float32)
    qj = np.stack([16.0 * np.floor(j / 16.0), j - 16.0 * np.floor(j / 16.0)])
    d["qj_c"] = bf16(qj)
    mk = np.zeros((128, 896), np.float32)
    im = np.arange(128)[:, None]
    mm = np.arange(896)[None, :]
    mk[mm < im + 384] = NEG
    d["maskneg_c"] = mk
    return d


_NC_CACHE = {}


def kernel(**inputs):
    if "nc" not in _NC_CACHE:
        _NC_CACHE["nc"] = build_program("all")
    nc, _ = _NC_CACHE["nc"]
    in_maps = [host_inputs(inputs, c) for c in range(8)]
    res = run_bass_kernel_spmd(nc, in_maps, core_ids=list(range(8)))
    outs = [np.asarray(r["out"], dtype=np.float32) for r in res.results]
    return np.stack(outs, axis=0)


if __name__ == "__main__":
    nc, bld = build_program("all")
    from collections import Counter
    cnt = Counter(o.eng for o in bld.ops)
    print("ops per engine:", dict(cnt), "total", len(bld.ops))
    print("sem domains:", bld.domains)
    print("sem finals:", bld.counters)


# version 14
# speedup vs baseline: 1.0522x; 1.0522x over previous
import numpy as np
from contextlib import ExitStack

import concourse.bass as bass
import concourse.mybir as mybir
from concourse.bass_utils import run_bass_kernel_spmd
from concourse.bass_core_rust import IndirectOffsetOnAxis
import ml_dtypes

F32 = mybir.dt.float32
BF16 = mybir.dt.bfloat16
AF = mybir.ActivationFunctionType
ALU = mybir.AluOpType
AX = mybir.AxisListType

S = 4096
D = 1024
NB = 8
TB = 512
NCH = 8
NH = 16
FF = 2048
NE = 8
NEG = -1.0e30
LN_EPS = 1e-5

ENGINES = ("tensor", "vector", "scalar", "gpsimd", "sync")


class Op:
    __slots__ = ("eng", "fn", "reads", "writes", "dma", "tag", "deps",
                 "inc_idx", "waits", "sub")

    def __init__(self, eng, fn, reads, writes, dma, tag):
        self.eng = eng
        self.fn = fn
        self.reads = tuple(reads)
        self.writes = tuple(writes)
        self.dma = dma
        self.tag = tag


class Builder:
    def __init__(self):
        self.ops = []

    def op(self, eng, fn, reads=(), writes=(), dma=False, tag=""):
        self.ops.append(Op(eng, fn, reads, writes, dma, tag))

    KSUB = 6

    def resolve(self):
        dma_rank = {}
        for i, o in enumerate(self.ops):
            if o.dma:
                r = dma_rank.get(o.eng, 0)
                dma_rank[o.eng] = r + 1
                o.sub = r % self.KSUB
                o.inc_idx = r // self.KSUB + 1
            else:
                o.sub = 0

        def dom_of(o):
            if o.dma:
                return (o.eng, "d", o.sub)
            return (o.eng, "c", 0)

        last_w = {}
        readers = {}
        for i, o in enumerate(self.ops):
            deps = set()
            for r in o.reads:
                if r in last_w:
                    deps.add(last_w[r])
            for w in o.writes:
                if w in last_w:
                    deps.add(last_w[w])
                deps.update(readers.get(w, ()))
            for w in o.writes:
                last_w[w] = i
                readers[w] = []
            for r in o.reads:
                readers.setdefault(r, []).append(i)
            byd = {}
            for j in deps:
                p = self.ops[j]
                if (p.eng == o.eng and not p.dma and not o.dma
                        and o.eng == "tensor"):
                    continue
                dom = dom_of(p)
                if j > byd.get(dom, -1):
                    byd[dom] = j
            o.deps = set(byd.values())
        need_inc = set()
        for i, o in enumerate(self.ops):
            need_inc |= o.deps
        counters = {}
        for i, o in enumerate(self.ops):
            dom = dom_of(o)
            if o.dma:
                counters[dom] = o.inc_idx
            elif i in need_inc:
                counters[dom] = counters.get(dom, 0) + 1
                o.inc_idx = counters[dom]
            else:
                o.inc_idx = None
        emitted = {}
        for o in self.ops:
            waits = {}
            if o.dma and o.inc_idx > 1:
                waits[dom_of(o)] = (o.inc_idx - 1) * 16
            for j in o.deps:
                p = self.ops[j]
                dom = dom_of(p)
                v = p.inc_idx * (16 if p.dma else 1)
                if v > waits.get(dom, 0):
                    waits[dom] = v
            em = emitted.setdefault(o.eng, {})
            fin = []
            for dom, v in sorted(waits.items(), key=lambda kv: str(kv[0])):
                if v > em.get(dom, 0):
                    em[dom] = v
                    fin.append((dom, v))
            o.waits = fin
        self.domains = sorted(counters, key=str)
        self.counters = counters
        self._dom_of = dom_of

    def emit(self, nc):
        self.resolve()
        es = ExitStack()
        sems = {dom: es.enter_context(
                    nc.semaphore(f"sm_{dom[0][:2]}{dom[1]}{dom[2]}"))
                for dom in self.domains}
        by_eng = {e: [] for e in ENGINES}
        for o in self.ops:
            by_eng[o.eng].append(o)
        dom_of = self._dom_of

        def runner(lst):
            def body(eng):
                for o in lst:
                    for dom, v in o.waits:
                        eng.wait_ge(sems[dom], v)
                    ins = o.fn(eng)
                    if o.inc_idx is not None:
                        ins.then_inc(sems[dom_of(o)], 16 if o.dma else 1)
            return body

        block = es.enter_context(nc.Block())
        with es:
            block.tensor(runner(by_eng["tensor"]))
            block.vector(runner(by_eng["vector"]))
            block.scalar(runner(by_eng["scalar"]))
            block.gpsimd(runner(by_eng["gpsimd"]))
            block.sync(runner(by_eng["sync"]))


def _slopes():
    ratio = 2.0 ** (-8.0 / NH)
    return np.array([ratio ** (h + 1) for h in range(NH)], dtype=np.float64)


def build_program(upto="all"):
    nc = bass.Bass("TRN2", target_bir_lowering=False, debug=False)

    def inp(name, shape, dt=F32):
        return nc.dram_tensor(name, list(shape), dt, kind="ExternalInput").ap()

    x0 = inp("x0", (S, D))
    qw = inp("qw", (D, D))
    kw = inp("kw", (D, D))
    vw = inp("vw", (D, D))
    ow = inp("ow", (D, D))
    wiT = inp("wiT", (D, 3 * D))
    woT = inp("woT", (D, D))
    bi_cols = inp("bi_cols", (128, 24))
    bo_cols = inp("bo_cols", (128, 8))
    rows32_c = inp("rows32_c", (1, 3688))
    gbc_c = inp("gbc_c", (128, 32))
    bvh_c = inp("bvh_c", (64, 16))
    wroute = inp("wroute", (D, 12))
    w1 = inp("w1", (NE * 128, 16384))
    w2 = inp("w2", (NE * 128, 16384))
    b12r = inp("b12r", (NE * 128, 24))
    g2b2b = inp("g2b2b", (128, 2 * D))
    iotac = inp("iotac", (128, 1))
    ident_c = inp("ident_c", (128, 128), BF16)
    identf_c = inp("identf_c", (128, 128))
    ones16_c = inp("ones16_c", (128, 128), BF16)
    one16c = inp("one16c", (16, 128), BF16)
    exp16_c = inp("exp16_c", (16, 576), BF16)
    hcc = inp("hcc", (2, 128, 576))
    sl2 = inp("sl2", (NH, 2, S), BF16)
    qj_c = inp("qj_c", (2, 512), BF16)
    maskneg_c = inp("maskneg_c", (128, 896))

    out = nc.dram_tensor("out", [S, D], F32, kind="ExternalOutput").ap()
    x1sp = nc.dram_tensor("x1sp", [D, S], BF16, kind="ExternalOutput").ap()
    x2sp = nc.dram_tensor("x2sp", [D, S], BF16, kind="ExternalOutput").ap()
    w1b0 = nc.dram_tensor("w1b0", [NE * 128, 8192], BF16, kind="Internal").ap()
    w1b1 = nc.dram_tensor("w1b1", [NE * 128, 8192], BF16, kind="Internal").ap()
    w2b = nc.dram_tensor("w2b", [NE * 128, 16384], BF16, kind="Internal").ap()
    x1tok = nc.dram_tensor("x1tok", [S, D], BF16, kind="Internal").ap()
    x2tok = nc.dram_tensor("x2tok", [S, 1040], BF16, kind="Internal").ap()
    xg = nc.dram_tensor("xg", [15 * 512, 1040], BF16, kind="Internal").ap()
    yg = nc.dram_tensor("yg", [15 * 512, D], BF16, kind="Internal").ap()
    ib1 = nc.dram_tensor("ib1", [1, S], mybir.dt.int32, kind="Internal").ap()
    dbg_kind = "ExternalOutput" if upto in ("a0", "kv", "qloop") else "Internal"
    x0sp = nc.dram_tensor("x0sp", [D, S], BF16, kind=dbg_kind).ap()
    qsp = nc.dram_tensor("qsp", [NH, 64, S], BF16, kind=dbg_kind).ap()
    osp = nc.dram_tensor("osp", [D, S], BF16, kind=dbg_kind).ap()

    es = ExitStack()

    def sbuf(name, shape, dt=BF16):
        return es.enter_context(nc.sbuf_tensor(name, list(shape), dt))

    K_AR = sbuf("K_AR", (128, 8 * S))
    V_AR = sbuf("V_AR", (128, 32 * 520))
    XB = sbuf("XB", (128, 2 * 4096))
    W_AR = sbuf("W_AR", (128, 4 * 4096))
    QS = sbuf("QS", (128, 2 * 512))
    PTB = sbuf("PTB", (128, 4 * 512))
    STG = sbuf("STG", (128, 1040))
    SQ = sbuf("SQ", (128, 2 * 512))
    RB = sbuf("RB", (128, 512))
    OTS = sbuf("OTS", (128, 2 * 512))
    RLN = sbuf("RLN", (128, 8 * 512))
    CMB = sbuf("CMB", (128, 32 * 8))
    CMTX = sbuf("CMTX", (128, 2048))
    AGB = sbuf("AGB", (128, 1024))
    BRB = sbuf("BRB", (128, 12), F32)
    IDENT = sbuf("IDENT", (128, 128))
    ONES16 = sbuf("ONES16", (128, 128))
    ONE16 = sbuf("ONE16", (16, 128))
    EXP16 = sbuf("EXP16", (16, 576))
    ONESB = sbuf("ONESB", (128, 2))
    ALIBB = sbuf("ALIBB", (128, 576), F32)
    MASKN = sbuf("MASKN", (128, 896), F32)
    ROWS32 = sbuf("ROWS32", (128, 3688), F32)
    GBC = sbuf("GBC", (128, 32), F32)
    BVH = sbuf("BVH", (64, 16), F32)
    BCT = sbuf("BCT", (16, 128))
    BIC = sbuf("BIC", (128, 24), F32)
    BOC = sbuf("BOC", (128, 8), F32)
    WRT = sbuf("WRT", (128, 8 * 12))
    RTS = sbuf("RTS", (128, 128), F32)
    SC16 = sbuf("SC16", (16, 8), F32)
    OUTS = sbuf("OUTS", (128, 1024), F32)
    IDENT32 = sbuf("IDENT32", (128, 128), F32)
    DESTC = sbuf("DESTC", (128, 32), mybir.dt.int32)
    EIDF = sbuf("EIDF", (128, 16), F32)
    EIDCI = sbuf("EIDCI", (128, 16), mybir.dt.int32)
    IOCF = sbuf("IOCF", (128, 1), F32)
    B1T = sbuf("B1T", (128, 24), F32)
    BASEC = sbuf("BASEC", (128, 1), F32)
    ONES8F = sbuf("ONES8F", (8, 4), F32)
    XGL = sbuf("XGL", (128, 1040))
    SQT = sbuf("SQT", (128, 1040))
    MVC = sbuf("MVC", (128, 32), F32)
    WCTS = sbuf("WCTS", (128, 32), F32)
    PCS = sbuf("PCS", (128, 1024))
    V32 = V_AR[:, 0:16640].bitcast(F32)
    ONE8 = V32[0:8, 0:S]
    CUM8 = V32[0:8, S:2 * S]

    P = [es.enter_context(nc.psum_tensor(f"P{i}", [128, 512], F32))
         for i in range(6)]
    PT7 = es.enter_context(nc.psum_tensor("PT7", [128, 1024], BF16))
    PT8 = es.enter_context(nc.psum_tensor("PT8", [128, 1024], BF16))

    ones512_row = ROWS32[32:33, 0:512]
    ones128 = ROWS32[32:33, 0:128]
    ones64_row = ROWS32[32:33, 0:64]
    arow_r = ROWS32[32:33, 512:1024]
    brow_r = ROWS32[32:33, 1024:1536]
    rrow_r = ROWS32[32:33, 1536:2048]
    brt_row = ROWS32[32:33, 2048:2060]
    mean_r = ROWS32[32:33, 2064:2576]
    m2_r = ROWS32[32:33, 2576:3088]
    var_r = ROWS32[32:33, 3088:3600]
    iot16_r = ROWS32[32:33, 3600:3616]
    one1_r = ROWS32[32:33, 0:1]
    CMTR = K_AR[0:8, 8192:12288]

    def mr(a, b):
        return ROWS32[32:33, 3616 + a:3616 + b]
    IRV = OUTS[:, 512:1024].bitcast(mybir.dt.int32)

    bld = Builder()
    _sink = [bld.op]

    def o_(eng, fn, reads=(), writes=(), dma=False, tag=""):
        _sink[0](eng, fn, reads=reads, writes=writes, dma=dma, tag=tag)

    def collect_ops(body):
        lst = []

        def cap(eng, fn, reads=(), writes=(), dma=False, tag=""):
            lst.append((eng, fn, tuple(reads), tuple(writes), dma, tag))
        _sink[0] = cap
        try:
            body()
        finally:
            _sink[0] = bld.op
        return lst

    def emit_rr(*lists):
        for i in range(max(len(l) for l in lists)):
            for lst in lists:
                if i < len(lst):
                    eng, fn, rds, wrs, dma, tag = lst[i]
                    bld.op(eng, fn, reads=rds, writes=wrs, dma=dma, tag=tag)

    def emit_seq(lst):
        for eng, fn, rds, wrs, dma, tag in lst:
            bld.op(eng, fn, reads=rds, writes=wrs, dma=dma, tag=tag)
    state = {"pk": 0, "w": 0}
    SRING = [0, 1, 4]
    _sl = _slopes()
    _s8 = np.array([np.float32(ml_dtypes.bfloat16(-8.0 * x)) for x in _sl])
    slope_eff_py = (-_s8 / 8.0).astype(np.float64)

    def wslot():
        s_ = state["w"]
        state["w"] = (s_ + 1) % 4
        return s_

    def cload(dst, src, res, eng="sync"):
        o_(eng, lambda e, a=dst, b_=src: e.dma_start(out=a, in_=b_),
           writes=(res,), dma=True, tag="cload")

    cload(IDENT[:, :], ident_c, "IDENT")
    cload(IDENT32[:, :], identf_c, "IDENT32")
    cload(ONES16[:, :], ones16_c, "ONES16")
    cload(ONE16[:, :], one16c, "ONE16")
    cload(EXP16[:, :], exp16_c, "EXP16")
    cload(MASKN[:, :], maskneg_c, "MASKN")
    cload(ROWS32[32:33, :], rows32_c, "ROWC")
    cload(GBC[:, :], gbc_c, "GBC")
    cload(BVH[:, :], bvh_c, "BVH")
    cload(BIC[:, :], bi_cols, "BIC")
    cload(BOC[:, :], bo_cols, "BOC")
    cload(IOCF[:, :], iotac, "IOCF")
    cload(CMTX[:, 0:2048], g2b2b, "G2B2", eng="gpsimd")
    cload(WRT[:, :].rearrange("p (c j) -> p c j", j=12),
          wroute.rearrange("(c p) j -> p c j", p=128), "WRT", eng="gpsimd")
    cload(QS[64:66, 0:512], qj_c, "QSj0")
    cload(QS[64:66, 512:1024], qj_c, "QSj1")
    o_("vector", lambda e: e.memset(ONESB[:, :], 1.0), writes=("ONESB",))
    o_("vector", lambda e: e.memset(OUTS[64:65, 512:576], 1.0),
       writes=("RRBonesF",))
    o_("vector", lambda e: e.memset(OTS[64:65, 512:640], 1.0),
       writes=("RRBones",))
    va_ones = V_AR[:, :].rearrange("p (t h x) -> p t h x", h=8, x=65)[:, :, :, 64:65]
    o_("vector", lambda e: e.memset(va_ones, 1.0), writes=("VAones",))

    x0r = x0.rearrange("(n p) d -> n p d", p=128)
    for tt in range(32):
        sl = tt % 2
        tok = XB[:, sl * 4096:sl * 4096 + 1024]
        o_("gpsimd", lambda e, a=tok, b_=x0r[tt, :, :]: e.dma_start(out=a, in_=b_),
           writes=(f"XB{sl}",), dma=True, tag="a0ld")
        for dc in range(NCH):
            bank = PT7 if dc % 2 == 0 else PT8
            bres = "PT7" if dc % 2 == 0 else "PT8"
            w0 = (dc // 2) * 128
            o_("tensor",
               lambda e, o2=bank[:, w0:w0 + 128],
                      i2=tok[:, dc * 128:(dc + 1) * 128]:
                   e.transpose(o2, i2, IDENT[:, :]),
               reads=(f"XB{sl}", "IDENT"), writes=(bres,), tag="a0tp")
            o_("scalar",
               lambda e, o2=K_AR[:, dc * S + tt * 128:dc * S + tt * 128 + 128],
                      i2=bank[:, w0:w0 + 128]:
                   e.copy(o2, i2),
               reads=(bres,), writes=("K_AR",), tag="a0ev")
    for dc in range(NCH):
        o_("sync",
           lambda e, a=x0sp[dc * 128:(dc + 1) * 128, :],
                  b_=K_AR[:, dc * S:(dc + 1) * S]:
               e.dma_start(out=a, in_=b_),
           reads=("K_AR",), writes=("x0sp",), dma=True, tag="a0sp")

    def load_weight(dram_ap, c0, c1, slot, tag="wld"):
        n = c1 - c0
        src = dram_ap.rearrange("(c p) j -> p c j", p=128)[:, :, c0:c1]
        dst = W_AR[:, slot * 4096:slot * 4096 + 8 * n].rearrange(
            "p (c j) -> p c j", j=n)
        o_("gpsimd", lambda e, a=dst, b_=src: e.dma_start(out=a, in_=b_),
           writes=(f"W{slot}",), dma=True, tag=tag)

    def load_xblock(xsp, xres, b_, slot, tag="xld"):
        src = xsp.rearrange("(c p) t -> p c t", p=128)[:, :, b_ * TB:(b_ + 1) * TB]
        dst = XB[:, slot * 4096:(slot + 1) * 4096].rearrange(
            "p (c t) -> p c t", t=TB)
        o_("sync", lambda e, a=dst, b2=src: e.dma_start(out=a, in_=b2),
           reads=(xres,), writes=(f"XB{slot}",), dma=True, tag=tag)

    PRE = []
    PRE.append(("vector", lambda e: e.memset(XGL[:, :], 0.0),
                (), ("XGL",), False, "xgz"))
    _zq = [("sync",
            lambda e, z0=zt: e.dma_start(out=xg[z0 * 128:(z0 + 1) * 128, :],
                                         in_=XGL[:, :]),
            ("XGL",), ("xg",), True, "xgz") for zt in range(60)]
    _cq = []
    for _srcw, _w, _tag in ((w1, None, "w1"), (w2, w2b, "w2b")):
        for _ei in range(NE):
            for _q in range(16):
                if _tag == "w1":
                    _dst, _dres = (w1b0, "w1b0") if _q < 8 else (w1b1, "w1b1")
                    _qq = _q % 8
                else:
                    _dst, _dres, _qq = _w, _tag, _q
                _cq.append(("gpsimd",
                            lambda e, s0=_srcw, e0=_ei, q0=_q: e.dma_start(
                                out=PCS[:, :],
                                in_=s0[e0 * 128:(e0 + 1) * 128,
                                       q0 * 1024:(q0 + 1) * 1024]),
                            (), ("PCS",), True, "pcin"))
                _cq.append(("sync",
                            lambda e, d2=_dst, e0=_ei, q0=_qq: e.dma_start(
                                out=d2[e0 * 128:(e0 + 1) * 128,
                                       q0 * 1024:(q0 + 1) * 1024],
                                in_=PCS[:, :]),
                            ("PCS",), (_dres,), True, "pcout"))
    while _zq or _cq:
        if _zq:
            PRE.append(_zq.pop(0))
        for _ in range(9):
            if _cq:
                PRE.append(_cq.pop(0))

    def drain_pre(n):
        for _ in range(n):
            if not PRE:
                return
            eng, fn, reads, writes, dma, tag = PRE.pop(0)
            o_(eng, fn, reads=reads, writes=writes, dma=dma, tag=tag)

    def ln_rows(supb, supr, sqpb, sqpr, tag="ln"):
        o_("vector", lambda e: e.tensor_scalar_mul(mean_r, supb[0:1, :], 1.0 / D),
           reads=(supr,), writes=("mean",), tag=tag)
        o_("vector", lambda e: e.tensor_scalar_mul(m2_r, sqpb[64:65, :], 1.0 / D),
           reads=(sqpr,), writes=("m2",), tag=tag)
        o_("vector", lambda e: e.tensor_tensor(var_r, mean_r, mean_r, op=ALU.mult),
           reads=("mean",), writes=("var",), tag=tag)
        o_("vector", lambda e: e.tensor_tensor(var_r, m2_r, var_r, op=ALU.subtract),
           reads=("m2", "var"), writes=("var",), tag=tag)
        o_("vector", lambda e: e.tensor_scalar_add(var_r, var_r, LN_EPS),
           reads=("var",), writes=("var",), tag=tag)
        o_("scalar", lambda e: e.activation(var_r, var_r, AF.Ln),
           reads=("var",), writes=("var",), tag=tag)
        o_("scalar", lambda e: e.activation(arow_r, var_r, AF.Exp, scale=-0.5),
           reads=("var",), writes=("arow",), tag=tag)
        o_("vector", lambda e: e.tensor_tensor(brow_r, mean_r, arow_r, op=ALU.mult),
           reads=("mean", "arow"), writes=("brow",), tag=tag)
        o_("vector", lambda e: e.tensor_scalar_mul(brow_r, brow_r, -1.0),
           reads=("brow",), writes=("brow",), tag=tag)

    def emit_ln_norm(b_, RSRC, rfmt, gb, xout, xoutres,
                     agb, agrA, agrB, supb, supr, sqpb, sqpr):
        o_("tensor",
           lambda e: e.matmul(supb[:, 0:512], ones128, arow_r,
                              start=True, stop=True),
           reads=("arow", "ROWC"), writes=(supr,), tag="lnag")
        o_("scalar", lambda e: e.copy(agb[:, 0:512], supb[:, 0:512]),
           reads=(supr,), writes=(agrA,), tag="lnev")
        o_("tensor",
           lambda e: e.matmul(sqpb[:, 0:512], ones128, brow_r,
                              start=True, stop=True),
           reads=("brow", "ROWC"), writes=(sqpr,), tag="lnbg")
        o_("scalar", lambda e: e.copy(agb[:, 512:1024], sqpb[:, 0:512]),
           reads=(sqpr,), writes=(agrB,), tag="lnev")
        for dc in range(NCH):
            o_("vector",
               lambda e, d0=dc:
                   e.tensor_tensor(RSRC[:, d0 * 512:(d0 + 1) * 512],
                                   RSRC[:, d0 * 512:(d0 + 1) * 512],
                                   agb[:, 0:512], op=ALU.mult),
               reads=(agrA,), writes=(rfmt.format(dc),), tag="lnmul")
            o_("vector",
               lambda e, d0=dc:
                   e.tensor_tensor(RSRC[:, d0 * 512:(d0 + 1) * 512],
                                   RSRC[:, d0 * 512:(d0 + 1) * 512],
                                   agb[:, 512:1024], op=ALU.add),
               reads=(agrB,), writes=(rfmt.format(dc),), tag="lnadd")
            if gb is not None:
                go, bo_ = gb
                o_("vector",
                   lambda e, d0=dc, g0=go, b0=bo_:
                       e.tensor_scalar(RSRC[:, d0 * 512:(d0 + 1) * 512],
                                       RSRC[:, d0 * 512:(d0 + 1) * 512],
                                       GBC[:, g0 + d0:g0 + d0 + 1],
                                       GBC[:, b0 + d0:b0 + d0 + 1],
                                       op0=ALU.mult, op1=ALU.add),
                   reads=("GBC",), writes=(rfmt.format(dc),), tag="lngb")
        if xout is not None:
            src = RSRC[:, 0:4096].rearrange("p (c t) -> p c t", t=TB)
            dst = xout.rearrange("(c p) t -> p c t", p=128)[:, :, b_ * TB:(b_ + 1) * TB]
            o_("sync", lambda e, a=dst, b2=src: e.dma_start(out=a, in_=b2),
               reads=tuple(rfmt.format(dc) for dc in range(NCH)),
               writes=(xoutres,), dma=True, tag="xspill")

    def emit_routing(b_, tt4, rbk, rbr, rb0):
        tile = 4 * b_ + tt4
        r_ = lambda a, b2: RTS[:, rb0 + a:rb0 + b2]
        G12 = r_(0, 12); E12 = r_(12, 24)
        S4 = r_(24, 25); MX4 = r_(25, 26); R4 = r_(26, 27)
        GP = r_(27, 28); OH4 = r_(28, 32); CUM = r_(32, 33)
        TMP = r_(33, 34); PS4 = r_(36, 40); PM4 = r_(40, 44)
        IE4 = r_(44, 48); PM1 = r_(48, 49); PS1 = r_(49, 50)
        RPS = r_(50, 51); EP = r_(51, 52); WC = r_(52, 53)
        T4A = r_(56, 60); T4B = r_(60, 64)
        RT = f"RTS{rb0 // 64}"

        def v(fn, reads=(RT,), writes=(RT,)):
            o_("vector", fn, reads=reads, writes=writes, tag="rt")

        v(lambda e, t0=tt4: e.tensor_tensor(
            G12, rbk[:, t0 * 12:(t0 + 1) * 12], BRB[:, :], op=ALU.add),
          reads=(rbr, "BRB"))
        o_("scalar", lambda e: e.activation(E12, G12, AF.Exp),
           reads=(RT,), writes=(RT,), tag="rt")
        v(lambda e: e.reduce_sum(S4, E12[:, 0:4], axis=AX.X))
        v(lambda e: e.reduce_max(MX4, E12[:, 0:4], axis=AX.X))
        o_("scalar", lambda e: e.activation(R4, S4, AF.Ln),
           reads=(RT,), writes=(RT,), tag="rt")
        o_("scalar", lambda e: e.activation(R4, R4, AF.Exp, scale=-1.0),
           reads=(RT,), writes=(RT,), tag="rt")
        v(lambda e: e.tensor_tensor(GP, MX4, R4, op=ALU.mult))
        v(lambda e: e.tensor_scalar(OH4, E12[:, 0:4], MX4, None, op0=ALU.is_ge))
        v(lambda e: e.tensor_copy(CUM, OH4[:, 0:1]))
        for g in range(1, 4):
            v(lambda e: e.tensor_scalar(TMP, CUM, -1.0, 1.0,
                                        op0=ALU.mult, op1=ALU.add))
            v(lambda e, g_=g: e.tensor_tensor(OH4[:, g_:g_ + 1],
                                              OH4[:, g_:g_ + 1], TMP, op=ALU.mult))
            if g < 3:
                v(lambda e, g_=g: e.tensor_tensor(CUM, CUM, OH4[:, g_:g_ + 1],
                                                  op=ALU.add))
        Ev = E12[:, 4:12].rearrange("p (g two) -> p g two", two=2)
        v(lambda e: e.tensor_tensor(PS4, Ev[:, :, 0], Ev[:, :, 1], op=ALU.add))
        v(lambda e: e.tensor_tensor(PM4, Ev[:, :, 0], Ev[:, :, 1], op=ALU.max))
        v(lambda e: e.tensor_tensor(IE4, Ev[:, :, 0], Ev[:, :, 1], op=ALU.is_ge))
        v(lambda e: e.tensor_tensor(T4A, OH4, PM4, op=ALU.mult))
        v(lambda e: e.reduce_sum(PM1, T4A, axis=AX.X))
        v(lambda e: e.tensor_tensor(T4A, OH4, PS4, op=ALU.mult))
        v(lambda e: e.reduce_sum(PS1, T4A, axis=AX.X))
        o_("scalar", lambda e: e.activation(RPS, PS1, AF.Ln),
           reads=(RT,), writes=(RT,), tag="rt")
        o_("scalar", lambda e: e.activation(RPS, RPS, AF.Exp, scale=-1.0),
           reads=(RT,), writes=(RT,), tag="rt")
        v(lambda e: e.tensor_tensor(EP, PM1, RPS, op=ALU.mult))
        v(lambda e: e.tensor_tensor(WC, GP, EP, op=ALU.mult))
        cview = CMB[:, tile * 8:(tile + 1) * 8].rearrange("p (g two) -> p g two",
                                                          two=2)
        v(lambda e: e.tensor_tensor(T4A, OH4, IE4, op=ALU.mult))
        v(lambda e: e.tensor_scalar(T4A, T4A, WC, None, op0=ALU.mult))
        v(lambda e, cv=cview: e.tensor_copy(cv[:, :, 0], T4A),
          reads=(RT,), writes=("CMB",))
        v(lambda e: e.tensor_scalar(T4B, IE4, -1.0, 1.0, op0=ALU.mult, op1=ALU.add))
        v(lambda e: e.tensor_tensor(T4B, OH4, T4B, op=ALU.mult))
        v(lambda e: e.tensor_scalar(T4B, T4B, WC, None, op0=ALU.mult))
        v(lambda e, cv=cview: e.tensor_copy(cv[:, :, 1], T4B),
          reads=(RT,), writes=("CMB",))
        v(lambda e, ti=tile: e.tensor_copy(WCTS[:, ti:ti + 1], WC),
          reads=(RT,), writes=("WCTS",))

    def attention(ai, xsp, xres_in, xout, xoutres, WQd, WKd, WVd, WOd,
                  alibi, causal, bias_qkv, bias_o, gb, routing,
                  stop_after=None, tokspill=None):
        kext = 66 if alibi else 64
        pend_tail = [None]
        o_("sync", lambda e, a0=ai: e.dma_start(out=ALIBB[:, :], in_=hcc[a0]),
           writes=("ALIBB",), dma=True, tag="hcld")
        for half in (0, 1):
            wsl_k = wslot()
            load_weight(WKd[0], WKd[1] + half * 512, WKd[1] + half * 512 + 512,
                        wsl_k, "wk")
            wsl_v = wslot()
            load_weight(WVd[0], WVd[1] + half * 512, WVd[1] + half * 512 + 512,
                        wsl_v, "wv")
            if alibi:
                for hh in range(8):
                    h = 8 * half + hh
                    o_("sync",
                       lambda e, h0=h, t0=hh:
                           e.dma_start(out=K_AR[64:66, t0 * S:(t0 + 1) * S],
                                       in_=sl2[h0]),
                       writes=("K_AR",), dma=True, tag="sl2")
            for b_ in range(NB):
                xsl = b_ % 2
                load_xblock(xsp, xres_in, b_, xsl)
                for jjl in range(4):
                    jc = 4 * half + jjl
                    pk = state["pk"] % 2
                    state["pk"] += 1
                    for dc in range(NCH):
                        o_("tensor",
                           lambda e, p0=pk, w0=wsl_k, j0=jjl, d0=dc, x0_=xsl,
                                  st=(dc == 0), sp=(dc == NCH - 1):
                               e.matmul(P[p0][:, :],
                                        W_AR[:, w0 * 4096 + d0 * 512 + j0 * 128:
                                             w0 * 4096 + d0 * 512 + j0 * 128 + 128],
                                        XB[:, x0_ * 4096 + d0 * 512:
                                           x0_ * 4096 + (d0 + 1) * 512],
                                        start=st, stop=sp),
                           reads=(f"W{wsl_k}", f"XB{xsl}"), writes=(f"P{pk}",),
                           tag="kproj")
                    ssl = jjl % 2
                    if bias_qkv:
                        o_("scalar",
                           lambda e, p0=pk, s0=ssl, j0=jc:
                               e.activation(STG[:, s0 * 512:(s0 + 1) * 512],
                                            P[p0][:, :], AF.Identity,
                                            bias=BIC[:, 8 + j0:9 + j0]),
                           reads=(f"P{pk}", "BIC"), writes=(f"STG{ssl}",), tag="kev")
                    else:
                        o_("scalar",
                           lambda e, p0=pk, s0=ssl:
                               e.copy(STG[:, s0 * 512:(s0 + 1) * 512], P[p0][:, :]),
                           reads=(f"P{pk}",), writes=(f"STG{ssl}",), tag="kev")
                    o_("scalar",
                       lambda e, s0=ssl: e.activation(
                           SQ[:, s0 * 512:(s0 + 1) * 512],
                           STG[:, s0 * 512:(s0 + 1) * 512], AF.Square),
                       reads=(f"STG{ssl}",), writes=(f"SQ{ssl}",), tag="ksq")
                    o_("tensor",
                       lambda e, j0=jc, s0=ssl, st=(jjl == 0), sp=(jjl == 3):
                           e.matmul(P[4][0:16, :], ONES16[:, j0 * 16:(j0 + 1) * 16],
                                    SQ[:, s0 * 512:(s0 + 1) * 512],
                                    start=st, stop=sp),
                       reads=(f"SQ{ssl}", "ONES16"), writes=("P4",), tag="kstat")
                    he, ho = 2 * jjl, 2 * jjl + 1
                    if alibi:
                        o_("vector",
                           lambda e, h0=he, s0=ssl, b0=b_:
                               e.tensor_copy(K_AR[0:64, h0 * S + b0 * TB:
                                                  h0 * S + (b0 + 1) * TB],
                                             STG[0:64, s0 * 512:(s0 + 1) * 512]),
                           reads=(f"STG{ssl}",), writes=("K_AR",), tag="kmove")
                        o_("gpsimd",
                           lambda e, h0=ho, s0=ssl, b0=b_:
                               e.dma_start(out=K_AR[0:64, h0 * S + b0 * TB:
                                                   h0 * S + (b0 + 1) * TB],
                                           in_=STG[64:128, s0 * 512:(s0 + 1) * 512]),
                           reads=(f"STG{ssl}",), writes=("K_AR",), dma=True,
                           tag="kmove2")
                    else:
                        o_("vector",
                           lambda e, j0=jjl, s0=ssl, b0=b_:
                               e.tensor_copy(K_AR[:, j0 * S + b0 * TB:
                                                  j0 * S + (b0 + 1) * TB],
                                             STG[:, s0 * 512:(s0 + 1) * 512]),
                           reads=(f"STG{ssl}",), writes=("K_AR",), tag="kmove")
                o_("vector", lambda e: e.reduce_max(SC16[:, 2:3], P[4][0:16, :],
                                                    axis=AX.X),
                   reads=("P4",), writes=("SC16t",), tag="kmax")
                if b_ == 0:
                    o_("vector", lambda e: e.tensor_copy(SC16[:, 1:2], SC16[:, 2:3]),
                       reads=("SC16t",), writes=("SC16k",), tag="kmax2")
                else:
                    o_("vector", lambda e: e.tensor_tensor(
                        SC16[:, 1:2], SC16[:, 1:2], SC16[:, 2:3], op=ALU.max),
                       reads=("SC16t", "SC16k"), writes=("SC16k",), tag="kmax2")
            for b_ in range(NB):
                xsl = b_ % 2
                load_xblock(xsp, xres_in, b_, xsl, tag="xldv")
                for tt4 in range(4):
                    t_abs = 4 * b_ + tt4
                    pk = state["pk"] % 2
                    state["pk"] += 1
                    for dc in range(NCH):
                        o_("tensor",
                           lambda e, p0=pk, w0=wsl_v, t0=tt4, d0=dc, x0_=xsl,
                                  st=(dc == 0), sp=(dc == NCH - 1):
                               e.matmul(P[p0][:, :],
                                        XB[:, x0_ * 4096 + d0 * 512 + t0 * 128:
                                           x0_ * 4096 + d0 * 512 + t0 * 128 + 128],
                                        W_AR[:, w0 * 4096 + d0 * 512:
                                             w0 * 4096 + (d0 + 1) * 512],
                                        start=st, stop=sp),
                           reads=(f"W{wsl_v}", f"XB{xsl}"), writes=(f"P{pk}",),
                           tag="vproj")
                    vview = V_AR[:, t_abs * 520:t_abs * 520 + 8 * 65] \
                        .rearrange("p (h x) -> p h x", x=65)[:, :, 0:64]
                    o_("scalar",
                       lambda e, p0=pk, vv=vview:
                           e.copy(vv, P[p0][:, :].rearrange(
                               "p (h x) -> p h x", x=64)),
                       reads=(f"P{pk}",), writes=("V_AR",), tag="vev")
            wsl_q = wslot()
            load_weight(WQd[0], WQd[1] + half * 512, WQd[1] + half * 512 + 512,
                        wsl_q, "wq")
            for b_ in range(NB):
                xsl = b_ % 2
                load_xblock(xsp, xres_in, b_, xsl, tag="xldq")
                for jjl in range(4):
                    jc = 4 * half + jjl
                    pk = state["pk"] % 2
                    state["pk"] += 1
                    for dc in range(NCH):
                        o_("tensor",
                           lambda e, p0=pk, w0=wsl_q, j0=jjl, d0=dc, x0_=xsl,
                                  st=(dc == 0), sp=(dc == NCH - 1):
                               e.matmul(P[p0][:, :],
                                        W_AR[:, w0 * 4096 + d0 * 512 + j0 * 128:
                                             w0 * 4096 + d0 * 512 + j0 * 128 + 128],
                                        XB[:, x0_ * 4096 + d0 * 512:
                                           x0_ * 4096 + (d0 + 1) * 512],
                                        start=st, stop=sp),
                           reads=(f"W{wsl_q}", f"XB{xsl}"), writes=(f"P{pk}",),
                           tag="qproj")
                    ssl = jjl % 2
                    if bias_qkv:
                        o_("scalar",
                           lambda e, p0=pk, s0=ssl, j0=jc:
                               e.activation(STG[:, s0 * 512:(s0 + 1) * 512],
                                            P[p0][:, :], AF.Identity,
                                            bias=BIC[:, j0:j0 + 1]),
                           reads=(f"P{pk}", "BIC"), writes=(f"STG{ssl}",), tag="qev")
                    else:
                        o_("scalar",
                           lambda e, p0=pk, s0=ssl:
                               e.copy(STG[:, s0 * 512:(s0 + 1) * 512], P[p0][:, :]),
                           reads=(f"P{pk}",), writes=(f"STG{ssl}",), tag="qev")
                    o_("scalar",
                       lambda e, s0=ssl: e.activation(
                           SQ[:, s0 * 512:(s0 + 1) * 512],
                           STG[:, s0 * 512:(s0 + 1) * 512], AF.Square),
                       reads=(f"STG{ssl}",), writes=(f"SQ{ssl}",), tag="qsq")
                    o_("tensor",
                       lambda e, j0=jc, s0=ssl, st=(jjl == 0), sp=(jjl == 3):
                           e.matmul(P[5][0:16, :], ONES16[:, j0 * 16:(j0 + 1) * 16],
                                    SQ[:, s0 * 512:(s0 + 1) * 512],
                                    start=st, stop=sp),
                       reads=(f"SQ{ssl}", "ONES16"), writes=("P5",), tag="qstat")
                    he, ho = 2 * jc, 2 * jc + 1
                    o_("sync",
                       lambda e, h0=he, s0=ssl, b0=b_:
                           e.dma_start(out=qsp[h0, :, b0 * TB:(b0 + 1) * TB],
                                       in_=STG[0:64, s0 * 512:(s0 + 1) * 512]),
                       reads=(f"STG{ssl}",), writes=("qsp",), dma=True, tag="qspill")
                    o_("sync",
                       lambda e, h0=ho, s0=ssl, b0=b_:
                           e.dma_start(out=qsp[h0, :, b0 * TB:(b0 + 1) * TB],
                                       in_=STG[64:128, s0 * 512:(s0 + 1) * 512]),
                       reads=(f"STG{ssl}",), writes=("qsp",), dma=True, tag="qspill")
                o_("vector", lambda e: e.reduce_max(SC16[:, 3:4], P[5][0:16, :],
                                                    axis=AX.X),
                   reads=("P5",), writes=("SC16s",), tag="qmax")
                if b_ == 0:
                    o_("vector", lambda e: e.tensor_copy(SC16[:, 0:1], SC16[:, 3:4]),
                       reads=("SC16s",), writes=("SC16q",), tag="qmax2")
                else:
                    o_("vector", lambda e: e.tensor_tensor(
                        SC16[:, 0:1], SC16[:, 0:1], SC16[:, 3:4], op=ALU.max),
                       reads=("SC16s", "SC16q"), writes=("SC16q",), tag="qmax2")
            o_("vector", lambda e: e.tensor_tensor(SC16[:, 4:5], SC16[:, 0:1],
                                                   SC16[:, 1:2], op=ALU.mult),
               reads=("SC16q", "SC16k"), writes=("SC16p",), tag="bfin")
            o_("scalar", lambda e: e.activation(SC16[:, 5:6], SC16[:, 4:5],
                                                AF.Ln),
               reads=("SC16p",), writes=("SC16l",), tag="bfin")
            o_("scalar", lambda e: e.activation(SC16[:, 6:7], SC16[:, 5:6],
                                                AF.Exp, scale=0.5),
               reads=("SC16l",), writes=("SC16e",), tag="bfin")
            o_("vector", lambda e: e.tensor_scalar(SC16[:, 7:8], SC16[:, 6:7],
                                                   -0.1251250, -0.5,
                                                   op0=ALU.mult, op1=ALU.add),
               reads=("SC16e",), writes=("SC16b",), tag="bfin")
            o_("vector", lambda e: e.tensor_scalar(BCT[:, :], ONE16[:, :],
                                                   SC16[0:16, 7:8], None,
                                                   op0=ALU.mult),
               reads=("SC16b", "ONE16"), writes=("BCT",), tag="bfin")
            c0b = half * 8 * 36
            c1b = (half + 1) * 8 * 36
            o_("tensor",
               lambda e, a0=c0b, a1=c1b: e.matmul(P[5][:, 0:a1 - a0],
                                                  BCT[:, :], EXP16[:, a0:a1],
                                                  start=True, stop=True),
               reads=("BCT", "EXP16"), writes=("P5",), tag="bbc")
            o_("vector",
               lambda e, a0=c0b, a1=c1b: e.tensor_tensor(
                   ALIBB[:, a0:a1], ALIBB[:, a0:a1], P[5][:, 0:a1 - a0],
                   op=ALU.add),
               reads=("P5", "ALIBB"), writes=("ALIBB",), tag="bbc")
            if stop_after == "kv":
                continue
            if not alibi:
                for b_ in range(NB):
                    for pj in range(4):
                        h_e = 8 * half + 2 * pj
                        h_o = h_e + 1
                        pcnt = half * NB * 4 + b_ * 4 + pj
                        qsl = pcnt % 2
                        drain_pre(1)
                        o_("sync",
                           lambda e, h0=h_e, b0=b_, q0=qsl:
                               e.dma_start(out=QS[0:64, q0 * 512:(q0 + 1) * 512],
                                           in_=qsp[h0, :, b0 * TB:(b0 + 1) * TB]),
                           reads=("qsp",), writes=(f"QS{qsl}",), dma=True,
                           tag="qld")
                        o_("sync",
                           lambda e, h0=h_o, b0=b_, q0=qsl:
                               e.dma_start(out=QS[64:128, q0 * 512:(q0 + 1) * 512],
                                           in_=qsp[h0, :, b0 * TB:(b0 + 1) * TB]),
                           reads=("qsp",), writes=(f"QS{qsl}",), dma=True,
                           tag="qld")
                        LAGP = 2

                        def emit_smm(i, pj=pj, qsl=qsl):
                            for (sb_, base, tp) in (
                                    ([0, 1][i % 2], 0, (0, 0)),
                                    ([4, 5][i % 2], 64, (64, 0))):
                                o_("tensor",
                                   lambda e, p0=sb_, j0=pj, c0=i, q0=qsl,
                                          b0=base, t0=tp:
                                       e.matmul(P[p0][:, :],
                                                K_AR[b0:b0 + 64,
                                                     j0 * S + c0 * 128:
                                                     j0 * S + c0 * 128 + 128],
                                                QS[b0:b0 + 64,
                                                   q0 * 512:(q0 + 1) * 512],
                                                start=True, stop=True,
                                                tile_position=t0),
                                   reads=("K_AR", f"QS{qsl}"),
                                   writes=(f"P{sb_}",), tag="smm")

                        def emit_exp(i, h_e=h_e, h_o=h_o):
                            for (sb_, base, hq) in (
                                    ([0, 1][i % 2], 0, h_e),
                                    ([4, 5][i % 2], 64, h_o)):
                                dcol = hq * 36 + 32
                                psl = (2 * i + (0 if base == 0 else 1)) % 4
                                o_("scalar",
                                   lambda e, p0=sb_, s0=psl, cc=dcol:
                                       e.activation(
                                           PTB[:, s0 * 512:(s0 + 1) * 512],
                                           P[p0][:, :], AF.Exp,
                                           bias=ALIBB[:, cc:cc + 1],
                                           scale=0.125),
                                   reads=(f"P{sb_}", "ALIBB"),
                                   writes=(f"PTB{psl}",), tag="exp")

                        def emit_avp(i, pj=pj):
                            for (av, hh2, par) in ((2, 2 * pj, 0),
                                                   (3, 2 * pj + 1, 1)):
                                psl = (2 * i + par) % 4
                                o_("tensor",
                                   lambda e, a0=av, c0=i, hx=hh2, s0=psl,
                                          st=(i == 0), sp=(i == 31):
                                       e.matmul(P[a0][0:65, :],
                                                V_AR[:, c0 * 520 + hx * 65:
                                                     c0 * 520 + hx * 65 + 65],
                                                PTB[:, s0 * 512:(s0 + 1) * 512],
                                                start=st, stop=sp),
                                   reads=("V_AR", f"PTB{psl}"),
                                   writes=(f"P{av}",), tag="avmm")

                        for i in range(32):
                            emit_smm(i)
                            if i == 2 and pend_tail[0] is not None:
                                pend_tail[0]()
                                pend_tail[0] = None
                            if i >= LAGP:
                                emit_avp(i - LAGP)
                            emit_exp(i)
                        for i in range(32 - LAGP, 32):
                            emit_avp(i)
                        if pend_tail[0] is not None:
                            pend_tail[0]()
                            pend_tail[0] = None

                        def mk_tailp(h_e=h_e, h_o=h_o, b_=b_):
                            def tail():
                                for (av, hq, osl) in ((2, h_e, 0), (3, h_o, 1)):
                                    o_("vector",
                                       lambda e, a0=av: e.reciprocal(
                                           OUTS[64:65, 0:512], P[a0][64:65, :]),
                                       reads=(f"P{av}",), writes=("rrow",),
                                       tag="recip")
                                    o_("tensor",
                                       lambda e: e.matmul(P[5][0:64, :],
                                                          OUTS[64:65, 512:576],
                                                          OUTS[64:65, 0:512],
                                                          start=True, stop=True),
                                       reads=("rrow", "RRBonesF"), writes=("P5",),
                                       tag="rbmm")
                                    o_("scalar",
                                       lambda e: e.copy(RB[0:64, :],
                                                        P[5][0:64, :]),
                                       reads=("P5",), writes=("RB",), tag="rbev")
                                    o_("vector",
                                       lambda e, a0=av, s0=osl:
                                           e.tensor_tensor(
                                               OTS[0:64,
                                                   s0 * 512:(s0 + 1) * 512],
                                               P[a0][0:64, :], RB[0:64, :],
                                               op=ALU.mult),
                                       reads=(f"P{av}", "RB"),
                                       writes=(f"OTS{osl}",), tag="onorm")
                                    if bias_qkv:
                                        o_("vector",
                                           lambda e, s0=osl, h0=hq:
                                               e.tensor_scalar(
                                                   OTS[0:64,
                                                       s0 * 512:(s0 + 1) * 512],
                                                   OTS[0:64,
                                                       s0 * 512:(s0 + 1) * 512],
                                                   BVH[:, h0:h0 + 1], None,
                                                   op0=ALU.add),
                                           reads=("BVH",),
                                           writes=(f"OTS{osl}",), tag="obias")
                                    o_("gpsimd",
                                       lambda e, h0=hq, b0=b_, s0=osl:
                                           e.dma_start(
                                               out=osp[64 * h0:64 * h0 + 64,
                                                       b0 * TB:(b0 + 1) * TB],
                                               in_=OTS[0:64,
                                                       s0 * 512:(s0 + 1) * 512]),
                                       reads=(f"OTS{osl}",), writes=("osp",),
                                       dma=True, tag="ospill")
                            return tail
                        pend_tail[0] = mk_tailp()
            else:
              for b_ in range(NB):
                nch = 4 * b_ + 4 if causal else 32
                for hp in range(4):
                    drain_pre(8)
                    hlists = []
                    pair_tails = []
                    for hh in (2 * hp, 2 * hp + 1):
                        h = 8 * half + hh
                        hcnt = half * NB * 8 + b_ * 8 + hh
                        qsl = hcnt % 2
                        avb = 2 + (hcnt % 2)
                        sbks = (0, 1) if hh % 2 == 0 else (4, 5)
                        rbk = 1 if hh % 2 == 0 else 5
                        pt0 = 0 if hh % 2 == 0 else 2
                        if causal:
                            dwin = int(110.0 / max(slope_eff_py[h], 1e-9))
                            cmin = max(0, (512 * b_ - 127 - dwin + 127) // 128)
                        else:
                            cmin = 0
                        chunks = list(range(cmin, nch))
                        LAG = 2

                        def av_op(i, chunks=chunks, avb=avb, hh=hh, pt0=pt0):
                            c = chunks[i]
                            psl = pt0 + i % 2
                            o_("tensor",
                               lambda e, a0=avb, c0=c, hx=hh, s0=psl,
                                      st=(i == 0), sp=(i == len(chunks) - 1):
                                   e.matmul(P[a0][0:65, :],
                                            V_AR[:, c0 * 520 + hx * 65:
                                                 c0 * 520 + hx * 65 + 65],
                                            PTB[:, s0 * 512:(s0 + 1) * 512],
                                            start=st, stop=sp),
                               reads=("V_AR", f"PTB{psl}"),
                               writes=(f"P{avb}",), tag="avmm")

                        def body(hh=hh, h=h, qsl=qsl, sbks=sbks, pt0=pt0,
                                 b2=b_, chunks=chunks, av_op=av_op):
                            o_("sync",
                               lambda e, h0=h, b0=b2, q0=qsl:
                                   e.dma_start(
                                       out=QS[0:64, q0 * 512:(q0 + 1) * 512],
                                       in_=qsp[h0, :, b0 * TB:(b0 + 1) * TB]),
                               reads=("qsp",), writes=(f"QS{qsl}",), dma=True,
                               tag="qld")
                            for i in range(len(chunks)):
                                c = chunks[i]
                                psl = pt0 + i % 2
                                sbk = sbks[i % 2]
                                o_("tensor",
                                   lambda e, p0=sbk, h0=hh, c0=c, q0=qsl:
                                       e.matmul(P[p0][:, :],
                                                K_AR[0:kext,
                                                     h0 * S + c0 * 128:
                                                     h0 * S + c0 * 128 + 128],
                                                QS[0:kext,
                                                   q0 * 512:(q0 + 1) * 512],
                                                start=True, stop=True),
                                   reads=("K_AR", f"QS{qsl}", "QSj0", "QSj1"),
                                   writes=(f"P{sbk}",), tag="smm")
                                if causal and c >= 4 * b2:
                                    off = 384 - 128 * (c - 4 * b2)
                                    o_("vector",
                                       lambda e, p0=sbk, f0=off:
                                           e.tensor_tensor(
                                               P[p0][:, :], P[p0][:, :],
                                               MASKN[:, f0:f0 + 512],
                                               op=ALU.add),
                                       reads=(f"P{sbk}", "MASKN"),
                                       writes=(f"P{sbk}",), tag="mask")
                                if i >= LAG:
                                    av_op(i - LAG)
                                dcol = h * 36 + ((c - 4 * b2 + 32)
                                                 if causal else 32)
                                o_("scalar",
                                   lambda e, p0=sbk, s0=psl, cc=dcol:
                                       e.activation(
                                           PTB[:, s0 * 512:(s0 + 1) * 512],
                                           P[p0][:, :], AF.Exp,
                                           bias=ALIBB[:, cc:cc + 1],
                                           scale=0.125),
                                   reads=(f"P{sbk}", "ALIBB"),
                                   writes=(f"PTB{psl}",), tag="exp")
                            for i in range(max(0, len(chunks) - LAG),
                                           len(chunks)):
                                av_op(i)
                        hlists.append(collect_ops(body))

                        def mk_tail(avb=avb, rbk=rbk, h=h, b2=b_,
                                    osl=hcnt % 2):
                            def tail():
                                o_("scalar",
                                   lambda e, a0=avb: e.activation(
                                       OUTS[64:65, 0:512], P[a0][64:65, :],
                                       AF.Ln),
                                   reads=(f"P{avb}",), writes=("rrow",),
                                   tag="recip")
                                o_("scalar",
                                   lambda e: e.activation(
                                       OUTS[64:65, 0:512],
                                       OUTS[64:65, 0:512],
                                       AF.Exp, scale=-1.0),
                                   reads=("rrow",), writes=("rrow",),
                                   tag="recip")
                                o_("tensor",
                                   lambda e, r0=rbk: e.matmul(
                                       P[r0][0:64, :],
                                       OUTS[64:65, 512:576],
                                       OUTS[64:65, 0:512],
                                       start=True, stop=True),
                                   reads=("rrow", "RRBonesF"),
                                   writes=(f"P{rbk}",), tag="rbmm")
                                o_("scalar",
                                   lambda e, r0=rbk: e.copy(RB[0:64, :],
                                                            P[r0][0:64, :]),
                                   reads=(f"P{rbk}",), writes=("RB",),
                                   tag="rbev")
                                o_("vector",
                                   lambda e, a0=avb, s0=osl:
                                       e.tensor_tensor(
                                           OTS[0:64,
                                               s0 * 512:(s0 + 1) * 512],
                                           P[a0][0:64, :], RB[0:64, :],
                                           op=ALU.mult),
                                   reads=(f"P{avb}", "RB"),
                                   writes=(f"OTS{osl}",), tag="onorm")
                                o_("gpsimd",
                                   lambda e, h0=h, b0=b2, s0=osl:
                                       e.dma_start(
                                           out=osp[64 * h0:64 * h0 + 64,
                                                   b0 * TB:(b0 + 1) * TB],
                                           in_=OTS[0:64,
                                                   s0 * 512:(s0 + 1) * 512]),
                                   reads=(f"OTS{osl}",), writes=("osp",),
                                   dma=True, tag="ospill")
                            return tail
                        pair_tails.append(mk_tail())
                    la, lb = hlists
                    merged = []
                    for i in range(max(len(la), len(lb))):
                        if i < len(la):
                            merged.append(la[i])
                        if i < len(lb):
                            merged.append(lb[i])
                    emit_seq(merged[:6])
                    if pend_tail[0] is not None:
                        pend_tail[0]()
                        pend_tail[0] = None
                    emit_seq(merged[6:])

                    def mk_pair(tails=tuple(pair_tails)):
                        def both():
                            for t2 in tails:
                                t2()
                        return both
                    pend_tail[0] = mk_pair()
        if pend_tail[0] is not None:
            pend_tail[0]()
            pend_tail[0] = None
        if stop_after in ("kv", "qloop"):
            return
        o_("vector", lambda e: e.memset(SC16[:, 5:6], 0.0),
           writes=("K_AR",) + tuple(f"RLB{dc}" for dc in range(NCH)) +
                  ("AGBC", "AGBD", "SQB0", "SQB1",
                   "XBB0", "XBB1", "TKB0", "TKB1", "XBB0", "XBB1",
                   "TKB0", "TKB1"),
           tag="opfence")
        wsl_o1 = wslot()
        load_weight(WOd[0], WOd[1], WOd[1] + 512, wsl_o1, "wo1")
        wsl_o2 = wslot()
        load_weight(WOd[0], WOd[1] + 512, WOd[1] + 1024, wsl_o2, "wo2")
        ARNB = K_AR[:, 0:4096]
        AGBB_ = K_AR[:, 4096:5120]
        SQB = K_AR[:, 5120:6144]
        XBB = K_AR[:, 6144:14336]
        TKB = (K_AR[:, 14336:15376], K_AR[:, 15376:16416])

        def load_oblock(xsp2, xres2, b2, dst, dres, tag):
            src2 = xsp2.rearrange("(c p) t -> p c t",
                                  p=128)[:, :, b2 * TB:(b2 + 1) * TB]
            d2 = dst.rearrange("p (c t) -> p c t", t=TB)
            o_("sync", lambda e, a=d2, s3=src2: e.dma_start(out=a, in_=s3),
               reads=(xres2,), writes=(dres,), dma=True, tag=tag)

        def o_block(b_):
            par = b_ % 2
            if par == 0:
                ARN, afmt = RLN, "RLN{}"
                agb, agrA, agrB = AGB, "AGBA", "AGBB"
                sqb, sqf = SQ, "SQ{}"
                supb, supr = P[4], "P4"
                sqpb, sqpr = P[5], "P5"
                xbs, x0r, x1r = XB, "XB0", "XB1"
                pko = 0
                tstg = ((STG, ("STG0", "STG1")), (SQT, ("SQT",)))
                rb0 = 0
            else:
                ARN, afmt = ARNB, "RLB{}"
                agb, agrA, agrB = AGBB_, "AGBC", "AGBD"
                sqb, sqf = SQB, "SQB{}"
                supb, supr = P[2], "P2"
                sqpb, sqpr = P[3], "P3"
                xbs, x0r, x1r = XBB, "XBB0", "XBB1"
                pko = 1
                tstg = ((TKB[0], ("TKB0",)), (TKB[1], ("TKB1",)))
                rb0 = 64
            load_oblock(osp, "osp", b_, xbs[:, 0:4096], x0r, "old")
            load_oblock(xsp, xres_in, b_, xbs[:, 4096:8192], x1r, "xres")
            for dc in range(NCH):
                wsl_o = wsl_o1 if dc < 4 else wsl_o2
                dcl = dc % 4
                for kc in range(NCH):
                    o_("tensor",
                       lambda e, p0=pko, w0=wsl_o, d0=dcl, k0=kc, x2=xbs,
                              st=(kc == 0), sp=(kc == NCH - 1):
                           e.matmul(P[p0][:, :],
                                    W_AR[:, w0 * 4096 + k0 * 512 + d0 * 128:
                                         w0 * 4096 + k0 * 512 + d0 * 128 + 128],
                                    x2[:, k0 * 512:(k0 + 1) * 512],
                                    start=st, stop=sp),
                       reads=(f"W{wsl_o}", x0r), writes=(f"P{pko}",),
                       tag="oproj")
                if bias_o:
                    o_("vector",
                       lambda e, p0=pko, d0=dc, a2=ARN, x2=xbs:
                           e.scalar_tensor_tensor(
                               a2[:, d0 * 512:(d0 + 1) * 512], P[p0][:, :],
                               BOC[:, d0:d0 + 1],
                               x2[:, 4096 + d0 * 512:4096 + (d0 + 1) * 512],
                               op0=ALU.add, op1=ALU.add),
                       reads=(f"P{pko}", x1r, "BOC"),
                       writes=(afmt.format(dc),), tag="resid")
                else:
                    o_("vector",
                       lambda e, p0=pko, d0=dc, a2=ARN, x2=xbs:
                           e.tensor_tensor(a2[:, d0 * 512:(d0 + 1) * 512],
                                           P[p0][:, :],
                                           x2[:, 4096 + d0 * 512:
                                              4096 + (d0 + 1) * 512],
                                           op=ALU.add),
                       reads=(f"P{pko}", x1r), writes=(afmt.format(dc),),
                       tag="resid")
                o_("scalar",
                   lambda e, d0=dc, s0=dc % 2, a2=ARN, s2=sqb: e.activation(
                       s2[:, s0 * 512:(s0 + 1) * 512],
                       a2[:, d0 * 512:(d0 + 1) * 512], AF.Square),
                   reads=(afmt.format(dc),), writes=(sqf.format(dc % 2),),
                   tag="osq")
                o_("tensor",
                   lambda e, d0=dc, a2=ARN, pb=supb,
                          st=(dc == 0), sp=(dc == NCH - 1):
                       e.matmul(pb[0:1, :], ONESB[:, 0:1],
                                a2[:, d0 * 512:(d0 + 1) * 512],
                                start=st, stop=sp),
                   reads=(afmt.format(dc), "ONESB"), writes=(supr,),
                   tag="osum")
                o_("tensor",
                   lambda e, s0=dc % 2, s2=sqb, pb=sqpb,
                          st=(dc == 0), sp=(dc == NCH - 1):
                       e.matmul(pb[64:65, :], ONESB[:, 0:1],
                                s2[:, s0 * 512:(s0 + 1) * 512],
                                start=st, stop=sp),
                   reads=(sqf.format(dc % 2), "ONESB"),
                   writes=(sqpr,), tag="osumsq")
            ln_rows(supb, supr, sqpb, sqpr)
            emit_ln_norm(b_, ARN, afmt, gb, xout, xoutres,
                         agb, agrA, agrB, supb, supr, sqpb, sqpr)
            if routing:
                for tt4 in range(4):
                    for dc in range(NCH):
                        o_("tensor",
                           lambda e, t0=tt4, d0=dc, a2=ARN, pb=sqpb,
                                  st=(dc == 0), sp=(dc == NCH - 1):
                               e.matmul(pb[:, t0 * 12:(t0 + 1) * 12],
                                        a2[:, d0 * 512 + t0 * 128:
                                            d0 * 512 + t0 * 128 + 128],
                                        WRT[:, d0 * 12:(d0 + 1) * 12],
                                        start=st, stop=sp),
                           reads=(afmt.format(dc), "WRT"),
                           writes=(sqpr,), tag="rtmm")
                for tt4 in range(4):
                    emit_routing(b_, tt4, sqpb, sqpr, rb0)
            if tokspill is not None:
                tokd, tokres, with_wc = tokspill
                twidth = 1040 if with_wc else 1024
                for tt4 in range(4):
                    stg, swr = tstg[tt4 % 2]
                    for w in range(8):
                        bank = PT7 if w % 2 == 0 else PT8
                        bres = "PT7" if w % 2 == 0 else "PT8"
                        wcol = (w // 2) * 128
                        o_("tensor",
                           lambda e, w0=w, t0=tt4, k2=bank, c2=wcol, a2=ARN:
                               e.transpose(k2[:, c2:c2 + 128],
                                           a2[:, w0 * 512 + t0 * 128:
                                               w0 * 512 + t0 * 128 + 128],
                                           IDENT[:, :]),
                           reads=(afmt.format(w), "IDENT"), writes=(bres,),
                           tag="ttp")
                        o_("scalar",
                           lambda e, w0=w, k2=bank, c2=wcol, s2=stg:
                               e.copy(s2[:, w0 * 128:(w0 + 1) * 128],
                                      k2[:, c2:c2 + 128]),
                           reads=(bres,), writes=swr, tag="tte")
                    if with_wc:
                        o_("vector",
                           lambda e, s2=stg, ti=4 * b_ + tt4:
                               e.tensor_copy(s2[:, 1024:1025],
                                             WCTS[:, ti:ti + 1]),
                           reads=("WCTS",), writes=swr, tag="ttwc")
                        o_("vector",
                           lambda e, s2=stg: e.memset(s2[:, 1025:1040], 0.0),
                           writes=swr, tag="ttwc")
                    o_("sync",
                       lambda e, s2=stg, t0=tt4, b0=b_, tw=twidth:
                           e.dma_start(out=tokd[(4 * b0 + t0) * 128:
                                                (4 * b0 + t0) * 128 + 128,
                                                0:tw],
                                       in_=s2[:, 0:tw]),
                       reads=swr, writes=(tokres,), dma=True, tag="tts")

        SEG0 = {"old", "xres", "oproj", "resid", "osq", "osum", "osumsq"}
        SEG1 = {"ln", "lnag", "lnev", "lnbg"}
        SEG3 = {"ttp", "tte", "ttwc", "tts"}

        def segs(lst):
            s0, s1, s2, s3 = [], [], [], []
            for op in lst:
                tg = op[5]
                if tg in SEG0:
                    s0.append(op)
                elif tg in SEG1:
                    s1.append(op)
                elif tg in SEG3:
                    s3.append(op)
                else:
                    s2.append(op)
            return s0, s1, s2, s3

        def rr2(la2, lb2):
            def chunks(lst):
                out_, cur = [], []
                for op in lst:
                    cur.append(op)
                    if op[5] == "tts":
                        out_.append(cur)
                        cur = []
                if cur:
                    out_.append(cur)
                return out_
            ca, cb = chunks(la2), chunks(lb2)
            for i in range(max(len(ca), len(cb))):
                if i < len(ca):
                    emit_seq(ca[i])
                if i < len(cb):
                    emit_seq(cb[i])

        for bp in range(0, NB, 2):
            s0a, s1a, s2a, s3a = segs(collect_ops(lambda: o_block(bp)))
            s0b, s1b, s2b, s3b = segs(collect_ops(lambda: o_block(bp + 1)))
            emit_rr(s0a, s0b)
            emit_seq(s1a)
            emit_rr(s2a, s1b + s2b)
            rr2(s3a, s3b)

    NT = 15

    def Hap(fc):
        if fc < 8:
            return RLN[:, fc * 512:(fc + 1) * 512]
        return K_AR[:, 12288 + (fc - 8) * 512:12288 + (fc - 7) * 512]

    def emit_moe():
        drain_pre(10 ** 9)
        yres = tuple(f"Y{t}" for t in range(NT))
        o_("vector", lambda e: e.memset(SC16[:, 2:3], 0.0),
           reads=("K_AR", "V_AR", "RB"),
           writes=yres + tuple(f"RLN{dc}" for dc in range(NCH)) +
                  tuple(f"RLB{dc}" for dc in range(NCH)) +
                  tuple(f"H{fc}" for fc in range(16)) +
                  ("AGBC", "AGBD", "SQB0", "SQB1",
                   "XBB0", "XBB1", "TKB0", "TKB1",
                   "K_AR", "V_AR", "RB", "W1F0", "W1F1", "W2F",
                   "W0", "W1", "W2", "W3", "XGT0", "XGT1", "WCROW",
                   "ONE8", "CUM8", "YGT"),
           tag="fence")
        for b_ in range(NB):
            cb = PT7 if b_ % 2 == 0 else PT8
            cbres = "PT7" if b_ % 2 == 0 else "PT8"
            for tt4 in range(4):
                tile = 4 * b_ + tt4
                o_("tensor",
                   lambda e, t0=tt4, ti=tile, k2=cb:
                       e.transpose(k2[0:8, t0 * 128:(t0 + 1) * 128],
                                   CMB[:, ti * 8:(ti + 1) * 8], IDENT[:, :]),
                   reads=("CMB", "IDENT"), writes=(cbres,), tag="ctp")
            o_("scalar",
               lambda e, b0=b_, k2=cb:
                   e.copy(CMTR[:, b0 * 512:(b0 + 1) * 512], k2[0:8, 0:512]),
               reads=(cbres,), writes=(f"CMT{b_}",), tag="ctev")
        rfr = tuple(f"CMT{b2}" for b2 in range(NB))
        o_("vector", lambda e: e.tensor_scalar(ONE8, CMTR[:, 0:S], 0.0, None,
                                               op0=ALU.is_gt),
           reads=rfr, writes=("ONE8",), tag="rf")
        o_("vector", lambda e: e.tensor_tensor_scan(CUM8, ONE8, ONE8, 0.0,
                                                    op0=ALU.add,
                                                    op1=ALU.bypass),
           reads=("ONE8",), writes=("CUM8",), tag="rf")
        o_("tensor", lambda e: e.matmul(P[4][0:1, 0:128], CUM8[:, S - 1:S],
                                        IDENT32[0:8, 0:128],
                                        start=True, stop=True),
           reads=("CUM8", "IDENT32"), writes=("P4",), tag="rfmm")
        o_("scalar", lambda e: e.copy(mr(0, 8), P[4][0:1, 0:8]),
           reads=("P4",), writes=("MROWc",), tag="rf")
        o_("vector", lambda e: e.memset(mr(8, 16), 0.0),
           writes=("MROWt",), tag="rf")
        for k_ in range(NT):
            o_("vector",
               lambda e, k0=k_: e.tensor_scalar(mr(56, 64),
                                                mr(0, 8),
                                                512.0 * k0, None,
                                                op0=ALU.is_gt),
               reads=("MROWc",), writes=("MROWx",), tag="rf")
            o_("vector",
               lambda e: e.tensor_tensor(mr(8, 16),
                                         mr(8, 16),
                                         mr(56, 64), op=ALU.add),
               reads=("MROWx",), writes=("MROWt",), tag="rf")
        o_("vector", lambda e: e.tensor_tensor_scan(mr(16, 24),
                                                    mr(8, 16),
                                                    mr(8, 16), 0.0,
                                                    op0=ALU.add,
                                                    op1=ALU.bypass),
           reads=("MROWt",), writes=("MROWi",), tag="rf")
        o_("vector", lambda e: e.tensor_tensor(mr(24, 32),
                                               mr(16, 24),
                                               mr(8, 16),
                                               op=ALU.subtract),
           reads=("MROWi", "MROWt"), writes=("MROWs",), tag="rf")
        o_("vector", lambda e: e.tensor_scalar(mr(0, 8),
                                               mr(24, 32), 512.0,
                                               None, op0=ALU.mult),
           reads=("MROWs",), writes=("MROWb",), tag="rf")
        o_("vector", lambda e: e.memset(mr(40, 56), -1.0),
           writes=("MROWe",), tag="rf")
        for e_i in range(NE):
            o_("vector",
               lambda e, e0=e_i: e.tensor_scalar(
                   mr(56, 72), iot16_r,
                   mr(24 + e0, 25 + e0), None, op0=ALU.is_ge),
               reads=("MROWs", "ROWC"), writes=("MROWx",), tag="rf")
            o_("vector",
               lambda e: e.tensor_tensor(mr(40, 56),
                                         mr(40, 56),
                                         mr(56, 72), op=ALU.add),
               reads=("MROWx",), writes=("MROWe",), tag="rf")
        o_("tensor", lambda e: e.matmul(P[4][0:8, 128:129], mr(0, 8),
                                        one1_r, start=True, stop=True),
           reads=("MROWb", "ROWC"), writes=("P4",), tag="rfmm")
        o_("scalar", lambda e: e.copy(BASEC[0:8, 0:1], P[4][0:8, 128:129]),
           reads=("P4",), writes=("BASECr",), tag="rf")
        o_("vector", lambda e: e.tensor_scalar(CUM8, CUM8, BASEC[0:8, 0:1],
                                               -1.0, op0=ALU.add,
                                               op1=ALU.add),
           reads=("BASECr", "CUM8"), writes=("CUM8",), tag="rf")
        o_("vector", lambda e: e.tensor_tensor(CUM8, CUM8, ONE8, op=ALU.mult),
           reads=("ONE8", "CUM8"), writes=("CUM8",), tag="rf")
        o_("vector", lambda e: e.memset(ONES8F[:, 0:1], 1.0),
           writes=("O8F",), tag="rf")
        for ch in range(8):
            o_("tensor",
               lambda e, c0=ch: e.matmul(P[5][0:1, 0:512], ONES8F[0:8, 0:1],
                                         CUM8[:, c0 * 512:(c0 + 1) * 512],
                                         start=True, stop=True),
               reads=("CUM8", "O8F"), writes=("P5",), tag="rfmm")
            o_("vector",
               lambda e: e.tensor_copy(IRV[0:1, :], P[5][0:1, 0:512]),
               reads=("P5",), writes=("IROW",), tag="rf")
            o_("sync",
               lambda e, c0=ch: e.dma_start(
                   out=ib1[0:1, c0 * 512:(c0 + 1) * 512],
                   in_=IRV[0:1, :]),
               reads=("IROW",), writes=("ib1",), dma=True, tag="rf")

        def _reload(e):
            with nc.allow_non_contiguous_dma(reason="16KB index repartition"):
                return e.dma_start(out=DESTC[:, 0:32],
                                   in_=ib1.rearrange("o (t p) -> p t o",
                                                     p=128))
        o_("gpsimd", _reload, reads=("ib1",), writes=("DESTC",), dma=True,
           tag="rfr")
        o_("tensor", lambda e: e.matmul(P[4][:, 256:272], ones128,
                                        mr(40, 56),
                                        start=True, stop=True),
           reads=("MROWe", "ROWC"), writes=("P4",), tag="rfmm")
        o_("vector", lambda e: e.tensor_scalar(EIDF[:, 0:16], P[4][:, 256:272],
                                               128.0, None, op0=ALU.mult),
           reads=("P4",), writes=("EIDF",), tag="rf")
        o_("vector", lambda e: e.tensor_scalar(EIDF[:, 0:16], EIDF[:, 0:16],
                                               IOCF[:, 0:1], None,
                                               op0=ALU.add),
           reads=("IOCF", "EIDF"), writes=("EIDF",), tag="rf")
        o_("vector", lambda e: e.tensor_copy(EIDCI[:, 0:16], EIDF[:, 0:16]),
           reads=("EIDF",), writes=("EIDCI",), tag="rf")
        for tt in range(32):
            xsl = tt % 2
            o_("sync",
               lambda e, t0=tt, x0_=xsl * 4096:
                   e.dma_start(out=XB[:, x0_:x0_ + 1040],
                               in_=x2tok[t0 * 128:(t0 + 1) * 128, :]),
               reads=("x2tok",), writes=(f"XB{xsl}",), dma=True, tag="scl")
            o_("gpsimd",
               lambda e, t0=tt, x0_=xsl * 4096:
                   e.indirect_dma_start(
                       out=xg,
                       out_offset=IndirectOffsetOnAxis(DESTC[:, t0:t0 + 1], 0),
                       in_=XB[:, x0_:x0_ + 1040], in_offset=None),
               reads=("DESTC", f"XB{xsl}"), writes=("xg",), dma=True,
               tag="sct")
        o_("vector", lambda e: e.memset(SC16[:, 3:4], 0.0),
           writes=("ONE8", "CUM8", "W1F0", "W1F1", "XB0", "XB1",
                   "XGT0", "XGT1") +
                  tuple(f"H{fc}" for fc in range(16)),
           tag="fence2")
        STAGE = ((XGL, ("XGL",)), (STG, ("STG0", "STG1")),
                 (SQT, ("SQT",)), (PTB, ("PTB0", "PTB1", "PTB2")))

        def emit_xgl(j):
            for s in range(4):
                stg, swr = STAGE[s]
                o_("sync",
                   lambda e, j0=j, s0=s, s2=stg:
                       e.dma_start(out=s2[:, 0:1040],
                                   in_=xg[j0 * 512 + s0 * 128:
                                          j0 * 512 + s0 * 128 + 128, :]),
                   reads=("xg",), writes=swr, dma=True, tag="xgl")

        emit_xgl(0)
        for j in range(NT):
            xpar = j % 2
            xga = XB[:, xpar * 4096:(xpar + 1) * 4096]
            xgr = f"XGT{xpar}"
            o_("gpsimd",
               lambda e, j0=j: e.indirect_dma_start(
                   out=V_AR[:, 0:8192], out_offset=None, in_=w1b0,
                   in_offset=IndirectOffsetOnAxis(EIDCI[:, j0:j0 + 1], 0)),
               reads=("EIDCI", "w1b0"), writes=("W1F0",), dma=True, tag="w1g")
            o_("gpsimd",
               lambda e, j0=j: e.indirect_dma_start(
                   out=V_AR[:, 8192:16384], out_offset=None, in_=w1b1,
                   in_offset=IndirectOffsetOnAxis(EIDCI[:, j0:j0 + 1], 0)),
               reads=("EIDCI", "w1b1"), writes=("W1F1",), dma=True, tag="w1g")
            o_("gpsimd",
               lambda e, j0=j: e.indirect_dma_start(
                   out=W_AR[:, 0:16384], out_offset=None, in_=w2b,
                   in_offset=IndirectOffsetOnAxis(EIDCI[:, j0:j0 + 1], 0)),
               reads=("EIDCI", "w2b"), writes=("W2F",), dma=True, tag="w2g")
            o_("gpsimd",
               lambda e, j0=j: e.indirect_dma_start(
                   out=B1T[:, 0:24], out_offset=None, in_=b12r,
                   in_offset=IndirectOffsetOnAxis(EIDCI[:, j0:j0 + 1], 0)),
               reads=("EIDCI",), writes=("B1T",), dma=True, tag="bg")
            for sct in range(4):
                stg, swr = STAGE[sct]
                for w in range(9):
                    bank = PT7 if w % 2 == 0 else PT8
                    bres = "PT7" if w % 2 == 0 else "PT8"
                    wcol = (w // 2) * 128
                    if w < 8:
                        o_("tensor",
                           lambda e, w0=w, k2=bank, c2=wcol, s2=stg:
                               e.transpose(k2[:, c2:c2 + 128],
                                           s2[:, w0 * 128:(w0 + 1) * 128],
                                           IDENT[:, :]),
                           reads=swr + ("IDENT",), writes=(bres,), tag="xgt")
                        o_("scalar",
                           lambda e, w0=w, s0=sct, k2=bank, c2=wcol, x2=xga:
                               e.copy(x2[:, w0 * 512 + s0 * 128:
                                         w0 * 512 + s0 * 128 + 128],
                                      k2[:, c2:c2 + 128]),
                           reads=(bres,), writes=(xgr,), tag="xge")
                    else:
                        o_("tensor",
                           lambda e, s2=stg: e.matmul(P[4][0:1, 384:512],
                                                      s2[:, 1024:1025],
                                                      IDENT[:, :],
                                                      start=True, stop=True),
                           reads=swr + ("IDENT",), writes=("P4",),
                           tag="xgt")
                        o_("scalar",
                           lambda e, s0=sct:
                               e.copy(OTS[64:65, s0 * 128:(s0 + 1) * 128],
                                      P[4][0:1, 384:512]),
                           reads=("P4",), writes=("WCROW",), tag="wce")
            if j + 1 < NT:
                emit_xgl(j + 1)
            o_("tensor", lambda e: e.matmul(P[5][:, :], OTS[64:65, 512:640],
                                            OTS[64:65, 0:512],
                                            start=True, stop=True),
               reads=("WCROW", "RRBones"), writes=("P5",), tag="cbmm")
            o_("scalar", lambda e: e.copy(RB[:, :], P[5][:, :]),
               reads=("P5",), writes=("RB",), tag="cbev")
            for fc in range(16):
                pk = state["pk"] % 2
                state["pk"] += 1
                fh, fcl = fc // 8, fc % 8
                for dc in range(NCH):
                    o_("tensor",
                       lambda e, p0=pk, h0=fh, f0=fcl, d0=dc, x2=xga,
                              st=(dc == 0), sp=(dc == NCH - 1):
                           e.matmul(P[p0][:, :],
                                    V_AR[:, h0 * 8192 + d0 * 1024 + f0 * 128:
                                         h0 * 8192 + d0 * 1024 +
                                         f0 * 128 + 128],
                                    x2[:, d0 * 512:(d0 + 1) * 512],
                                    start=st, stop=sp),
                       reads=(f"W1F{fh}", xgr), writes=(f"P{pk}",),
                       tag="hmm")
                o_("scalar",
                   lambda e, p0=pk, f0=fc:
                       e.activation(Hap(f0), P[p0][:, :], AF.Relu,
                                    bias=B1T[:, f0:f0 + 1]),
                   reads=(f"P{pk}", "B1T"), writes=(f"H{fc}",), tag="relu")
            for dc2 in range(NCH):
                pk = state["pk"] % 2
                state["pk"] += 1
                for fc in range(16):
                    o_("tensor",
                       lambda e, p0=pk, f0=fc, d0=dc2,
                              st=(fc == 0), sp=(fc == 15):
                           e.matmul(P[p0][:, :],
                                    W_AR[:, f0 * 1024 + d0 * 128:
                                         f0 * 1024 + d0 * 128 + 128],
                                    Hap(f0), start=st, stop=sp),
                       reads=("W2F", f"H{fc}"), writes=(f"P{pk}",), tag="ymm")
                o_("scalar",
                   lambda e, p0=pk, d0=dc2:
                       e.activation(K_AR[:, d0 * 512:(d0 + 1) * 512],
                                    P[p0][:, :], AF.Identity,
                                    bias=B1T[:, 16 + d0:17 + d0]),
                   reads=(f"P{pk}", "B1T"), writes=("YGT",), tag="yev")
                o_("vector",
                   lambda e, d0=dc2:
                       e.tensor_tensor(K_AR[:, d0 * 512:(d0 + 1) * 512],
                                       K_AR[:, d0 * 512:(d0 + 1) * 512],
                                       RB[:, :], op=ALU.mult),
                   reads=("RB", "YGT"), writes=("YGT",), tag="yscale")
            for sct in range(4):
                for w in range(8):
                    bank = PT7 if w % 2 == 0 else PT8
                    bres = "PT7" if w % 2 == 0 else "PT8"
                    wcol = (w // 2) * 128
                    o_("tensor",
                       lambda e, w0=w, s0=sct, k2=bank, c2=wcol:
                           e.transpose(k2[:, c2:c2 + 128],
                                       K_AR[:, w0 * 512 + s0 * 128:
                                            w0 * 512 + s0 * 128 + 128],
                                       IDENT[:, :]),
                       reads=("YGT", "IDENT"), writes=(bres,), tag="ygt")
                    o_("scalar",
                       lambda e, w0=w, s0=sct, k2=bank, c2=wcol:
                           e.copy(K_AR[:, 4096 + s0 * 1024 + w0 * 128:
                                       4096 + s0 * 1024 + w0 * 128 + 128],
                                  k2[:, c2:c2 + 128]),
                       reads=(bres,), writes=(f"YTK{sct}",), tag="yge")
                o_("sync",
                   lambda e, j0=j, s0=sct:
                       e.dma_start(out=yg[j0 * 512 + s0 * 128:
                                          j0 * 512 + s0 * 128 + 128, :],
                                   in_=K_AR[:, 4096 + s0 * 1024:
                                            4096 + (s0 + 1) * 1024]),
                   reads=(f"YTK{sct}",), writes=(f"Y{j}",), dma=True,
                   tag="ygs")

    def emit_final():
        o_("vector", lambda e: e.memset(SC16[:, 4:5], 0.0),
           writes=("XB0", "XB1", "XGT0", "XGT1", "rrow", "OUTS",
                   "RRBonesF", "IROW", "XF0", "XF1", "XF2", "XF3"),
           tag="fence3")
        FSQ = ((SQT, ("SQT",)), (STG, ("STG0", "STG1")),
               (XGL, ("XGL",)), (PTB, ("PTB0", "PTB1")))

        def tile_ops(tt):
            ops = []

            def a_(eng, fn, reads=(), writes=(), dma=False, tag="f"):
                ops.append((eng, fn, tuple(reads), tuple(writes), dma, tag))

            q = tt % 4
            xo = q * 2048
            mb = q * 8
            sfx = str(q)
            sq, sqr = FSQ[q]
            xbr = f"XF{q}"

            def lncols(tag):
                a_("vector",
                   lambda e: e.tensor_scalar(MVC[:, mb + 2:mb + 3],
                                             MVC[:, mb:mb + 1],
                                             1.0 / D, None, op0=ALU.mult),
                   (f"MV0{sfx}",), (f"MV2{sfx}",), False, tag)
                a_("vector",
                   lambda e: e.tensor_scalar(MVC[:, mb + 3:mb + 4],
                                             MVC[:, mb + 1:mb + 2],
                                             1.0 / D, None, op0=ALU.mult),
                   (f"MV1{sfx}",), (f"MV3{sfx}",), False, tag)
                a_("vector",
                   lambda e: e.tensor_tensor(MVC[:, mb + 6:mb + 7],
                                             MVC[:, mb + 2:mb + 3],
                                             MVC[:, mb + 2:mb + 3],
                                             op=ALU.mult),
                   (f"MV2{sfx}",), (f"MV6{sfx}",), False, tag)
                a_("vector",
                   lambda e: e.tensor_tensor(MVC[:, mb + 3:mb + 4],
                                             MVC[:, mb + 3:mb + 4],
                                             MVC[:, mb + 6:mb + 7],
                                             op=ALU.subtract),
                   (f"MV6{sfx}", f"MV3{sfx}"), (f"MV3{sfx}",), False, tag)
                a_("vector",
                   lambda e: e.tensor_scalar_add(MVC[:, mb + 3:mb + 4],
                                                 MVC[:, mb + 3:mb + 4],
                                                 LN_EPS),
                   (f"MV3{sfx}",), (f"MV3{sfx}",), False, tag)
                a_("scalar",
                   lambda e: e.activation(MVC[:, mb + 3:mb + 4],
                                          MVC[:, mb + 3:mb + 4], AF.Ln),
                   (f"MV3{sfx}",), (f"MV3{sfx}",), False, tag)
                a_("scalar",
                   lambda e: e.activation(MVC[:, mb + 4:mb + 5],
                                          MVC[:, mb + 3:mb + 4], AF.Exp,
                                          scale=-0.5),
                   (f"MV3{sfx}",), (f"MV4{sfx}",), False, tag)
                a_("vector",
                   lambda e: e.tensor_tensor(MVC[:, mb + 5:mb + 6],
                                             MVC[:, mb + 2:mb + 3],
                                             MVC[:, mb + 4:mb + 5],
                                             op=ALU.mult),
                   (f"MV2{sfx}", f"MV4{sfx}"), (f"MV5{sfx}",), False, tag)
                a_("vector",
                   lambda e: e.tensor_scalar(MVC[:, mb + 5:mb + 6],
                                             MVC[:, mb + 5:mb + 6],
                                             -1.0, None, op0=ALU.mult),
                   (f"MV5{sfx}",), (f"MV5{sfx}",), False, tag)

            a_("gpsimd",
               lambda e: e.indirect_dma_start(
                   out=XB[:, xo:xo + 1024], out_offset=None, in_=yg,
                   in_offset=IndirectOffsetOnAxis(DESTC[:, tt:tt + 1], 0)),
               ("DESTC",) + tuple(f"Y{j2}" for j2 in range(NT)),
               (xbr,), True, "ygr")
            a_("sync",
               lambda e: e.dma_start(out=XB[:, xo + 1024:xo + 2048],
                                     in_=x2tok[tt * 128:(tt + 1) * 128,
                                               0:1024]),
               ("x2tok",), (xbr,), True, "f2ld")
            a_("vector",
               lambda e: e.tensor_tensor(XB[:, xo:xo + 1024],
                                         XB[:, xo:xo + 1024],
                                         XB[:, xo + 1024:xo + 2048],
                                         op=ALU.add),
               (xbr,), (xbr,), False, "r3")
            a_("sync",
               lambda e: e.dma_start(out=XB[:, xo + 1024:xo + 2048],
                                     in_=x1tok[tt * 128:(tt + 1) * 128, :]),
               ("x1tok", xbr), (xbr,), True, "f1ld")
            a_("vector",
               lambda e: e.reduce_sum(MVC[:, mb:mb + 1],
                                      XB[:, xo:xo + 1024], axis=AX.X),
               (xbr,), (f"MV0{sfx}",), False, "f3ln")
            a_("scalar",
               lambda e: e.activation(sq[:, 0:1024], XB[:, xo:xo + 1024],
                                      AF.Square),
               (xbr,), sqr, False, "f3sq")
            a_("vector",
               lambda e: e.reduce_sum(MVC[:, mb + 1:mb + 2], sq[:, 0:1024],
                                      axis=AX.X),
               sqr, (f"MV1{sfx}",), False, "f3ln")
            lncols("f3ln")
            a_("vector",
               lambda e: e.tensor_scalar(XB[:, xo:xo + 1024],
                                         XB[:, xo:xo + 1024],
                                         MVC[:, mb + 4:mb + 5],
                                         MVC[:, mb + 5:mb + 6],
                                         op0=ALU.mult, op1=ALU.add),
               (f"MV4{sfx}", f"MV5{sfx}"), (xbr,), False, "f3n1")
            a_("vector",
               lambda e: e.tensor_tensor(XB[:, xo:xo + 1024],
                                         XB[:, xo:xo + 1024],
                                         CMTX[:, 0:1024], op=ALU.mult),
               ("G2B2",), (xbr,), False, "f3n2")
            a_("vector",
               lambda e: e.tensor_tensor(XB[:, xo:xo + 1024],
                                         XB[:, xo:xo + 1024],
                                         CMTX[:, 1024:2048], op=ALU.add),
               ("G2B2",), (xbr,), False, "f3n3")
            a_("vector",
               lambda e: e.tensor_tensor(XB[:, xo:xo + 1024],
                                         XB[:, xo:xo + 1024],
                                         XB[:, xo + 1024:xo + 2048],
                                         op=ALU.add),
               (xbr,), (xbr,), False, "r4")
            a_("vector",
               lambda e: e.reduce_sum(MVC[:, mb:mb + 1],
                                      XB[:, xo:xo + 1024], axis=AX.X),
               (xbr,), (f"MV0{sfx}",), False, "f4ln")
            a_("scalar",
               lambda e: e.activation(sq[:, 0:1024], XB[:, xo:xo + 1024],
                                      AF.Square),
               (xbr,), sqr, False, "f4sq")
            a_("vector",
               lambda e: e.reduce_sum(MVC[:, mb + 1:mb + 2], sq[:, 0:1024],
                                      axis=AX.X),
               sqr, (f"MV1{sfx}",), False, "f4ln")
            lncols("f4ln")
            a_("vector",
               lambda e: e.tensor_scalar(OUTS[:, :], XB[:, xo:xo + 1024],
                                         MVC[:, mb + 4:mb + 5],
                                         MVC[:, mb + 5:mb + 6],
                                         op0=ALU.mult, op1=ALU.add),
               (f"MV4{sfx}", f"MV5{sfx}", xbr), ("OUTS",), False, "fnorm")
            a_("sync",
               lambda e: e.dma_start(out=out[tt * 128:(tt + 1) * 128, :],
                                     in_=OUTS[:, :]),
               ("OUTS",), ("out",), True, "ostore")
            return ops

        for t0 in range(0, 32, 4):
            tls = [tile_ops(t0 + i) for i in range(4)]
            emit_rr(*[tl[:-2] for tl in tls])
            for tl in tls:
                emit_seq(tl[-2:])

    if upto != "a0":
        attention(0, x0sp, "x0sp", x1sp, "x1_out", (qw, 0), (kw, 0), (vw, 0),
                  (ow, 0), alibi=True, causal=True, bias_qkv=False,
                  bias_o=False, gb=None, routing=False,
                  stop_after=upto if upto in ("kv", "qloop") else None,
                  tokspill=(x1tok, "x1tok", False))
    if upto in ("x2", "all"):
        o_("tensor", lambda e: e.matmul(P[5][:, 0:12], ones128, brt_row,
                                        start=True, stop=True),
           reads=("ROWC",), writes=("P5",), tag="brt")
        o_("scalar", lambda e: e.copy(BRB[:, :], P[5][:, 0:12]),
           reads=("P5",), writes=("BRB",), tag="brt")
        x2o = x2sp if upto == "x2" else None
        attention(1, x1sp, "x1_out", x2o, "x2_out", (wiT, 0), (wiT, 1024),
                  (wiT, 2048), (woT, 0), alibi=False, causal=False,
                  bias_qkv=True, bias_o=True, gb=(0, 8), routing=True,
                  tokspill=(x2tok, "x2tok", True))
    if upto == "all":
        emit_moe()
        emit_final()
        sink_reads = ("out", "x1_out")
    elif upto == "x2":
        sink_reads = ("x1_out", "x2_out", "x1tok", "x2tok")
    elif upto == "a0":
        sink_reads = ("x0sp",)
    elif upto == "kv":
        sink_reads = ("qsp", "x0sp", "K_AR")
    elif upto == "qloop":
        sink_reads = ("osp", "qsp", "x0sp")
    else:
        sink_reads = ("x1_out",)
    o_("sync", lambda e: e.nop(), reads=sink_reads, tag="sink")

    bld.emit(nc)
    es.close()
    return nc, bld


def host_inputs(inputs, core):
    f32 = lambda a: np.ascontiguousarray(np.asarray(a, dtype=np.float32))
    bf16 = lambda a: np.ascontiguousarray(np.asarray(a).astype(ml_dtypes.bfloat16))
    x = f32(inputs["L0_attn_x"])[core]
    sl = _slopes()
    s8 = np.array([np.float32(ml_dtypes.bfloat16(-8.0 * s)) for s in sl],
                  dtype=np.float32)
    slope_eff = (-s8 / 8.0).astype(np.float64)

    d = {"x0": x}
    d["qw"] = f32(inputs["L0_attn_q_w"])
    d["kw"] = f32(inputs["L0_attn_k_w"])
    d["vw"] = f32(inputs["L0_attn_v_w"])
    d["ow"] = f32(inputs["L0_attn_o_w"])
    d["wiT"] = f32(np.asarray(inputs["L0_ffn_in_proj_w"], np.float32).T)
    d["woT"] = f32(np.asarray(inputs["L0_ffn_out_proj_w"], np.float32).T)
    bi = f32(inputs["L0_ffn_in_proj_b"])
    d["bi_cols"] = f32(bi.reshape(24, 128).T)
    bo = f32(inputs["L0_ffn_out_proj_b"])
    d["bo_cols"] = f32(bo.reshape(8, 128).T)
    rows32 = np.zeros((1, 3688), np.float32)
    rows32[0, 0:512] = 1.0
    bg = f32(inputs["L0_ffn_b_group"])
    be = f32(inputs["L0_ffn_b_expert"]).reshape(-1)
    rows32[0, 2048:2052] = bg
    rows32[0, 2052:2060] = be
    rows32[0, 3600:3616] = np.arange(16, dtype=np.float32)
    d["rows32_c"] = rows32
    gbc = np.zeros((128, 32), np.float32)
    gbc[:, 0:8] = f32(inputs["L0_ffn_ln1_g"]).reshape(8, 128).T
    gbc[:, 8:16] = f32(inputs["L0_ffn_ln1_b"]).reshape(8, 128).T
    gbc[:, 16:24] = f32(inputs["L0_ffn_ln2_g"]).reshape(8, 128).T
    gbc[:, 24:32] = f32(inputs["L0_ffn_ln2_b"]).reshape(8, 128).T
    d["gbc_c"] = gbc
    d["bvh_c"] = np.ascontiguousarray(bi[2048:3072].reshape(16, 64).T)
    wg = f32(inputs["L0_ffn_w_group"])
    we = f32(inputs["L0_ffn_w_expert"])
    d["wroute"] = np.ascontiguousarray(
        np.concatenate([wg, we.transpose(1, 0, 2).reshape(D, 8)], axis=1))
    w1a = f32(inputs["L0_ffn_w1"])
    d["w1"] = np.ascontiguousarray(
        w1a.reshape(NE, 8, 128, 2, 1024).transpose(0, 2, 3, 1, 4)
        .reshape(NE * 128, 16384))
    w2a = f32(inputs["L0_ffn_w2"])
    d["w2"] = np.ascontiguousarray(
        w2a.reshape(NE, 16, 128, 1024).transpose(0, 2, 1, 3)
        .reshape(NE * 128, 16384))
    b1 = f32(inputs["L0_ffn_b1"])
    b2a = f32(inputs["L0_ffn_b2"])
    b12 = np.zeros((NE, 128, 24), np.float32)
    b12[:, :, 0:16] = b1.reshape(NE, 16, 128).transpose(0, 2, 1)
    b12[:, :, 16:24] = b2a.reshape(NE, 8, 128).transpose(0, 2, 1)
    d["b12r"] = np.ascontiguousarray(b12.reshape(NE * 128, 24))
    gb2 = np.zeros((128, 2048), np.float32)
    gb2[:, 0:1024] = f32(inputs["L0_ffn_ln2_g"]).reshape(-1)[None, :]
    gb2[:, 1024:2048] = f32(inputs["L0_ffn_ln2_b"]).reshape(-1)[None, :]
    d["g2b2b"] = gb2
    d["iotac"] = np.arange(128, dtype=np.float32).reshape(128, 1)
    d["identf_c"] = np.eye(128, dtype=np.float32)
    d["ident_c"] = np.eye(128, dtype=ml_dtypes.bfloat16)
    o16 = np.zeros((128, 128), np.float32)
    for jc in range(8):
        for r in range(128):
            o16[r, jc * 16 + 2 * jc + (r // 64)] = 1.0
    d["ones16_c"] = bf16(o16)
    d["one16c"] = bf16(np.ones((16, 128), np.float32))
    ex16 = np.zeros((16, 576), np.float32)
    for h in range(16):
        ex16[h, h * 36:(h + 1) * 36] = 1.0
    d["exp16_c"] = bf16(ex16)
    hcc = np.zeros((2, 128, 576), np.float32)
    ii = np.arange(128, dtype=np.float64)
    for h in range(16):
        for di in range(36):
            delta = di - 32
            hcc[0, :, h * 36 + di] = (slope_eff[h] * (ii + 128.0 * delta)
                                      ).astype(np.float32)
    d["hcc"] = hcc
    sl2 = np.zeros((NH, 2, S), np.float32)
    for h in range(16):
        sl2[h, :, :] = s8[h]
    d["sl2"] = bf16(sl2)
    j = np.arange(512, dtype=np.float32)
    qj = np.stack([16.0 * np.floor(j / 16.0), j - 16.0 * np.floor(j / 16.0)])
    d["qj_c"] = bf16(qj)
    mk = np.zeros((128, 896), np.float32)
    im = np.arange(128)[:, None]
    mm = np.arange(896)[None, :]
    mk[mm < im + 384] = NEG
    d["maskneg_c"] = mk
    return d


_NC_CACHE = {}


def kernel(**inputs):
    if "nc" not in _NC_CACHE:
        _NC_CACHE["nc"] = build_program("all")
    nc, _ = _NC_CACHE["nc"]
    in_maps = [host_inputs(inputs, c) for c in range(8)]
    res = run_bass_kernel_spmd(nc, in_maps, core_ids=list(range(8)))
    outs = [np.asarray(r["out"], dtype=np.float32) for r in res.results]
    return np.stack(outs, axis=0)


if __name__ == "__main__":
    nc, bld = build_program("all")
    from collections import Counter
    cnt = Counter(o.eng for o in bld.ops)
    print("ops per engine:", dict(cnt), "total", len(bld.ops))
    print("sem domains:", bld.domains)
    print("sem finals:", bld.counters)


# version 15 (final)
# speedup vs baseline: 1.0546x; 1.0023x over previous
import numpy as np
from contextlib import ExitStack

import concourse.bass as bass
import concourse.mybir as mybir
from concourse.bass_utils import run_bass_kernel_spmd
from concourse.bass_core_rust import IndirectOffsetOnAxis
import ml_dtypes

F32 = mybir.dt.float32
BF16 = mybir.dt.bfloat16
AF = mybir.ActivationFunctionType
ALU = mybir.AluOpType
AX = mybir.AxisListType

S = 4096
D = 1024
NB = 8
TB = 512
NCH = 8
NH = 16
FF = 2048
NE = 8
NEG = -1.0e30
LN_EPS = 1e-5

ENGINES = ("tensor", "vector", "scalar", "gpsimd", "sync")


class Op:
    __slots__ = ("eng", "fn", "reads", "writes", "dma", "tag", "deps",
                 "inc_idx", "waits", "sub")

    def __init__(self, eng, fn, reads, writes, dma, tag):
        self.eng = eng
        self.fn = fn
        self.reads = tuple(reads)
        self.writes = tuple(writes)
        self.dma = dma
        self.tag = tag


class Builder:
    def __init__(self):
        self.ops = []

    def op(self, eng, fn, reads=(), writes=(), dma=False, tag=""):
        self.ops.append(Op(eng, fn, reads, writes, dma, tag))

    KSUB = 6

    def resolve(self):
        dma_rank = {}
        for i, o in enumerate(self.ops):
            if o.dma:
                r = dma_rank.get(o.eng, 0)
                dma_rank[o.eng] = r + 1
                o.sub = r % self.KSUB
                o.inc_idx = r // self.KSUB + 1
            else:
                o.sub = 0

        def dom_of(o):
            if o.dma:
                return (o.eng, "d", o.sub)
            return (o.eng, "c", 0)

        last_w = {}
        readers = {}
        for i, o in enumerate(self.ops):
            deps = set()
            for r in o.reads:
                if r in last_w:
                    deps.add(last_w[r])
            for w in o.writes:
                if w in last_w:
                    deps.add(last_w[w])
                deps.update(readers.get(w, ()))
            for w in o.writes:
                last_w[w] = i
                readers[w] = []
            for r in o.reads:
                readers.setdefault(r, []).append(i)
            byd = {}
            for j in deps:
                p = self.ops[j]
                if (p.eng == o.eng and not p.dma and not o.dma
                        and o.eng == "tensor"):
                    continue
                dom = dom_of(p)
                if j > byd.get(dom, -1):
                    byd[dom] = j
            o.deps = set(byd.values())
        need_inc = set()
        for i, o in enumerate(self.ops):
            need_inc |= o.deps
        counters = {}
        for i, o in enumerate(self.ops):
            dom = dom_of(o)
            if o.dma:
                counters[dom] = o.inc_idx
            elif i in need_inc:
                counters[dom] = counters.get(dom, 0) + 1
                o.inc_idx = counters[dom]
            else:
                o.inc_idx = None
        emitted = {}
        for o in self.ops:
            waits = {}
            if o.dma and o.inc_idx > 1:
                waits[dom_of(o)] = (o.inc_idx - 1) * 16
            for j in o.deps:
                p = self.ops[j]
                dom = dom_of(p)
                v = p.inc_idx * (16 if p.dma else 1)
                if v > waits.get(dom, 0):
                    waits[dom] = v
            em = emitted.setdefault(o.eng, {})
            fin = []
            for dom, v in sorted(waits.items(), key=lambda kv: str(kv[0])):
                if v > em.get(dom, 0):
                    em[dom] = v
                    fin.append((dom, v))
            o.waits = fin
        self.domains = sorted(counters, key=str)
        self.counters = counters
        self._dom_of = dom_of

    def emit(self, nc):
        self.resolve()
        es = ExitStack()
        sems = {dom: es.enter_context(
                    nc.semaphore(f"sm_{dom[0][:2]}{dom[1]}{dom[2]}"))
                for dom in self.domains}
        by_eng = {e: [] for e in ENGINES}
        for o in self.ops:
            by_eng[o.eng].append(o)
        dom_of = self._dom_of

        def runner(lst):
            def body(eng):
                for o in lst:
                    for dom, v in o.waits:
                        eng.wait_ge(sems[dom], v)
                    ins = o.fn(eng)
                    if o.inc_idx is not None:
                        ins.then_inc(sems[dom_of(o)], 16 if o.dma else 1)
            return body

        block = es.enter_context(nc.Block())
        with es:
            block.tensor(runner(by_eng["tensor"]))
            block.vector(runner(by_eng["vector"]))
            block.scalar(runner(by_eng["scalar"]))
            block.gpsimd(runner(by_eng["gpsimd"]))
            block.sync(runner(by_eng["sync"]))


def _slopes():
    ratio = 2.0 ** (-8.0 / NH)
    return np.array([ratio ** (h + 1) for h in range(NH)], dtype=np.float64)


def build_program(upto="all"):
    nc = bass.Bass("TRN2", target_bir_lowering=False, debug=False)

    def inp(name, shape, dt=F32):
        return nc.dram_tensor(name, list(shape), dt, kind="ExternalInput").ap()

    x0 = inp("x0", (S, D))
    qw = inp("qw", (D, D))
    kw = inp("kw", (D, D))
    vw = inp("vw", (D, D))
    ow = inp("ow", (D, D))
    wiT = inp("wiT", (D, 3 * D))
    woT = inp("woT", (D, D))
    bi_cols = inp("bi_cols", (128, 24))
    bo_cols = inp("bo_cols", (128, 8))
    rows32_c = inp("rows32_c", (1, 3688))
    gbc_c = inp("gbc_c", (128, 32))
    bvh_c = inp("bvh_c", (64, 16))
    wroute = inp("wroute", (D, 12))
    w1 = inp("w1", (NE * 128, 16384))
    w2 = inp("w2", (NE * 128, 16384))
    b12r = inp("b12r", (NE * 128, 24))
    g2b2b = inp("g2b2b", (128, 2 * D))
    iotac = inp("iotac", (128, 1))
    ident_c = inp("ident_c", (128, 128), BF16)
    identf_c = inp("identf_c", (128, 128))
    ones16_c = inp("ones16_c", (128, 128), BF16)
    one16c = inp("one16c", (16, 128), BF16)
    exp16_c = inp("exp16_c", (16, 576), BF16)
    hcc = inp("hcc", (2, 128, 576))
    sl2 = inp("sl2", (NH, 2, S), BF16)
    qj_c = inp("qj_c", (2, 512), BF16)
    maskneg_c = inp("maskneg_c", (128, 896))

    out = nc.dram_tensor("out", [S, D], F32, kind="ExternalOutput").ap()
    x1sp = nc.dram_tensor("x1sp", [D, S], BF16, kind="ExternalOutput").ap()
    x2sp = nc.dram_tensor("x2sp", [D, S], BF16, kind="ExternalOutput").ap()
    w1b0 = nc.dram_tensor("w1b0", [NE * 128, 8192], BF16, kind="Internal").ap()
    w1b1 = nc.dram_tensor("w1b1", [NE * 128, 8192], BF16, kind="Internal").ap()
    w2b = nc.dram_tensor("w2b", [NE * 128, 16384], BF16, kind="Internal").ap()
    x1tok = nc.dram_tensor("x1tok", [S, D], BF16, kind="Internal").ap()
    x2tok = nc.dram_tensor("x2tok", [S, 1040], BF16, kind="Internal").ap()
    xg = nc.dram_tensor("xg", [15 * 512, 1040], BF16, kind="Internal").ap()
    yg = nc.dram_tensor("yg", [15 * 512, D], BF16, kind="Internal").ap()
    ib1 = nc.dram_tensor("ib1", [1, S], mybir.dt.int32, kind="Internal").ap()
    dbg_kind = "ExternalOutput" if upto in ("a0", "kv", "qloop") else "Internal"
    x0sp = nc.dram_tensor("x0sp", [D, S], BF16, kind=dbg_kind).ap()
    qsp = nc.dram_tensor("qsp", [NH, 64, S], BF16, kind=dbg_kind).ap()
    osp = nc.dram_tensor("osp", [D, S], BF16, kind=dbg_kind).ap()

    es = ExitStack()

    def sbuf(name, shape, dt=BF16):
        return es.enter_context(nc.sbuf_tensor(name, list(shape), dt))

    K_AR = sbuf("K_AR", (128, 8 * S))
    V_AR = sbuf("V_AR", (128, 32 * 520))
    XB = sbuf("XB", (128, 2 * 4096))
    W_AR = sbuf("W_AR", (128, 4 * 4096))
    QS = sbuf("QS", (128, 2 * 512))
    PTB = sbuf("PTB", (128, 4 * 512))
    STG = sbuf("STG", (128, 1040))
    SQ = sbuf("SQ", (128, 2 * 512))
    RB = sbuf("RB", (128, 512))
    OTS = sbuf("OTS", (128, 2 * 512))
    RLN = sbuf("RLN", (128, 8 * 512))
    CMB = sbuf("CMB", (128, 32 * 8))
    CMTX = sbuf("CMTX", (128, 2048))
    AGB = sbuf("AGB", (128, 1024))
    BRB = sbuf("BRB", (128, 12), F32)
    IDENT = sbuf("IDENT", (128, 128))
    ONES16 = sbuf("ONES16", (128, 128))
    ONE16 = sbuf("ONE16", (16, 128))
    EXP16 = sbuf("EXP16", (16, 576))
    ONESB = sbuf("ONESB", (128, 2))
    ALIBB = sbuf("ALIBB", (128, 576), F32)
    MASKN = sbuf("MASKN", (128, 896), F32)
    ROWS32 = sbuf("ROWS32", (128, 3688), F32)
    GBC = sbuf("GBC", (128, 32), F32)
    BVH = sbuf("BVH", (64, 16), F32)
    BCT = sbuf("BCT", (16, 128))
    BIC = sbuf("BIC", (128, 24), F32)
    BOC = sbuf("BOC", (128, 8), F32)
    WRT = sbuf("WRT", (128, 8 * 12))
    RTS = sbuf("RTS", (128, 128), F32)
    SC16 = sbuf("SC16", (16, 8), F32)
    OUTS = sbuf("OUTS", (128, 1024), F32)
    IDENT32 = sbuf("IDENT32", (128, 128), F32)
    DESTC = sbuf("DESTC", (128, 32), mybir.dt.int32)
    EIDF = sbuf("EIDF", (128, 16), F32)
    EIDCI = sbuf("EIDCI", (128, 16), mybir.dt.int32)
    IOCF = sbuf("IOCF", (128, 1), F32)
    B1T = sbuf("B1T", (128, 24), F32)
    BASEC = sbuf("BASEC", (128, 1), F32)
    ONES8F = sbuf("ONES8F", (8, 4), F32)
    XGL = sbuf("XGL", (128, 1040))
    SQT = sbuf("SQT", (128, 1040))
    MVC = sbuf("MVC", (128, 32), F32)
    WCTS = sbuf("WCTS", (128, 32), F32)
    PCS = sbuf("PCS", (128, 1024))
    V32 = V_AR[:, 0:16640].bitcast(F32)
    ONE8 = V32[0:8, 0:S]
    CUM8 = V32[0:8, S:2 * S]

    P = [es.enter_context(nc.psum_tensor(f"P{i}", [128, 512], F32))
         for i in range(6)]
    PT7 = es.enter_context(nc.psum_tensor("PT7", [128, 1024], BF16))
    PT8 = es.enter_context(nc.psum_tensor("PT8", [128, 1024], BF16))

    ones512_row = ROWS32[32:33, 0:512]
    ones128 = ROWS32[32:33, 0:128]
    ones64_row = ROWS32[32:33, 0:64]
    arow_r = ROWS32[32:33, 512:1024]
    brow_r = ROWS32[32:33, 1024:1536]
    rrow_r = ROWS32[32:33, 1536:2048]
    brt_row = ROWS32[32:33, 2048:2060]
    mean_r = ROWS32[32:33, 2064:2576]
    m2_r = ROWS32[32:33, 2576:3088]
    var_r = ROWS32[32:33, 3088:3600]
    iot16_r = ROWS32[32:33, 3600:3616]
    one1_r = ROWS32[32:33, 0:1]
    CMTR = K_AR[0:8, 8192:12288]

    def mr(a, b):
        return ROWS32[32:33, 3616 + a:3616 + b]
    IRV = OUTS[:, 512:1024].bitcast(mybir.dt.int32)

    bld = Builder()
    _sink = [bld.op]

    def o_(eng, fn, reads=(), writes=(), dma=False, tag=""):
        _sink[0](eng, fn, reads=reads, writes=writes, dma=dma, tag=tag)

    def collect_ops(body):
        lst = []

        def cap(eng, fn, reads=(), writes=(), dma=False, tag=""):
            lst.append((eng, fn, tuple(reads), tuple(writes), dma, tag))
        _sink[0] = cap
        try:
            body()
        finally:
            _sink[0] = bld.op
        return lst

    def emit_rr(*lists):
        for i in range(max(len(l) for l in lists)):
            for lst in lists:
                if i < len(lst):
                    eng, fn, rds, wrs, dma, tag = lst[i]
                    bld.op(eng, fn, reads=rds, writes=wrs, dma=dma, tag=tag)

    def emit_seq(lst):
        for eng, fn, rds, wrs, dma, tag in lst:
            bld.op(eng, fn, reads=rds, writes=wrs, dma=dma, tag=tag)
    state = {"pk": 0, "w": 0}
    SRING = [0, 1, 4]
    _sl = _slopes()
    _s8 = np.array([np.float32(ml_dtypes.bfloat16(-8.0 * x)) for x in _sl])
    slope_eff_py = (-_s8 / 8.0).astype(np.float64)

    def wslot():
        s_ = state["w"]
        state["w"] = (s_ + 1) % 4
        return s_

    def cload(dst, src, res, eng="sync"):
        o_(eng, lambda e, a=dst, b_=src: e.dma_start(out=a, in_=b_),
           writes=(res,), dma=True, tag="cload")

    cload(IDENT[:, :], ident_c, "IDENT")
    cload(IDENT32[:, :], identf_c, "IDENT32")
    cload(ONES16[:, :], ones16_c, "ONES16")
    cload(ONE16[:, :], one16c, "ONE16")
    cload(EXP16[:, :], exp16_c, "EXP16")
    cload(MASKN[:, :], maskneg_c, "MASKN")
    cload(ROWS32[32:33, :], rows32_c, "ROWC")
    cload(GBC[:, :], gbc_c, "GBC")
    cload(BVH[:, :], bvh_c, "BVH")
    cload(BIC[:, :], bi_cols, "BIC")
    cload(BOC[:, :], bo_cols, "BOC")
    cload(IOCF[:, :], iotac, "IOCF")
    cload(CMTX[:, 0:2048], g2b2b, "G2B2", eng="gpsimd")
    cload(WRT[:, :].rearrange("p (c j) -> p c j", j=12),
          wroute.rearrange("(c p) j -> p c j", p=128), "WRT", eng="gpsimd")
    cload(QS[64:66, 0:512], qj_c, "QSj0")
    cload(QS[64:66, 512:1024], qj_c, "QSj1")
    o_("vector", lambda e: e.memset(ONESB[:, :], 1.0), writes=("ONESB",))
    o_("vector", lambda e: e.memset(OUTS[64:65, 512:576], 1.0),
       writes=("RRBonesF",))
    o_("vector", lambda e: e.memset(OTS[64:65, 512:640], 1.0),
       writes=("RRBones",))
    va_ones = V_AR[:, :].rearrange("p (t h x) -> p t h x", h=8, x=65)[:, :, :, 64:65]
    o_("vector", lambda e: e.memset(va_ones, 1.0), writes=("VAones",))

    x0r = x0.rearrange("(n p) d -> n p d", p=128)
    for tt in range(32):
        sl = tt % 2
        tok = XB[:, sl * 4096:sl * 4096 + 1024]
        o_("gpsimd", lambda e, a=tok, b_=x0r[tt, :, :]: e.dma_start(out=a, in_=b_),
           writes=(f"XB{sl}",), dma=True, tag="a0ld")
        for dc in range(NCH):
            bank = PT7 if dc % 2 == 0 else PT8
            bres = "PT7" if dc % 2 == 0 else "PT8"
            w0 = (dc // 2) * 128
            o_("tensor",
               lambda e, o2=bank[:, w0:w0 + 128],
                      i2=tok[:, dc * 128:(dc + 1) * 128]:
                   e.transpose(o2, i2, IDENT[:, :]),
               reads=(f"XB{sl}", "IDENT"), writes=(bres,), tag="a0tp")
            o_("scalar",
               lambda e, o2=K_AR[:, dc * S + tt * 128:dc * S + tt * 128 + 128],
                      i2=bank[:, w0:w0 + 128]:
                   e.copy(o2, i2),
               reads=(bres,), writes=("K_AR",), tag="a0ev")
    for dc in range(NCH):
        o_("sync",
           lambda e, a=x0sp[dc * 128:(dc + 1) * 128, :],
                  b_=K_AR[:, dc * S:(dc + 1) * S]:
               e.dma_start(out=a, in_=b_),
           reads=("K_AR",), writes=("x0sp",), dma=True, tag="a0sp")

    def load_weight(dram_ap, c0, c1, slot, tag="wld"):
        n = c1 - c0
        src = dram_ap.rearrange("(c p) j -> p c j", p=128)[:, :, c0:c1]
        dst = W_AR[:, slot * 4096:slot * 4096 + 8 * n].rearrange(
            "p (c j) -> p c j", j=n)
        o_("gpsimd", lambda e, a=dst, b_=src: e.dma_start(out=a, in_=b_),
           writes=(f"W{slot}",), dma=True, tag=tag)

    def load_xblock(xsp, xres, b_, slot, tag="xld"):
        src = xsp.rearrange("(c p) t -> p c t", p=128)[:, :, b_ * TB:(b_ + 1) * TB]
        dst = XB[:, slot * 4096:(slot + 1) * 4096].rearrange(
            "p (c t) -> p c t", t=TB)
        o_("sync", lambda e, a=dst, b2=src: e.dma_start(out=a, in_=b2),
           reads=(xres,), writes=(f"XB{slot}",), dma=True, tag=tag)

    PRE = []
    PRE.append(("vector", lambda e: e.memset(XGL[:, :], 0.0),
                (), ("XGL",), False, "xgz"))
    _zq = [("sync",
            lambda e, z0=zt: e.dma_start(out=xg[z0 * 128:(z0 + 1) * 128, :],
                                         in_=XGL[:, :]),
            ("XGL",), ("xg",), True, "xgz") for zt in range(60)]
    _cq = []
    for _srcw, _w, _tag in ((w1, None, "w1"), (w2, w2b, "w2b")):
        for _ei in range(NE):
            for _q in range(16):
                if _tag == "w1":
                    _dst, _dres = (w1b0, "w1b0") if _q < 8 else (w1b1, "w1b1")
                    _qq = _q % 8
                else:
                    _dst, _dres, _qq = _w, _tag, _q
                _cq.append(("gpsimd",
                            lambda e, s0=_srcw, e0=_ei, q0=_q: e.dma_start(
                                out=PCS[:, :],
                                in_=s0[e0 * 128:(e0 + 1) * 128,
                                       q0 * 1024:(q0 + 1) * 1024]),
                            (), ("PCS",), True, "pcin"))
                _cq.append(("sync",
                            lambda e, d2=_dst, e0=_ei, q0=_qq: e.dma_start(
                                out=d2[e0 * 128:(e0 + 1) * 128,
                                       q0 * 1024:(q0 + 1) * 1024],
                                in_=PCS[:, :]),
                            ("PCS",), (_dres,), True, "pcout"))
    while _zq or _cq:
        if _zq:
            PRE.append(_zq.pop(0))
        for _ in range(9):
            if _cq:
                PRE.append(_cq.pop(0))

    def drain_pre(n):
        for _ in range(n):
            if not PRE:
                return
            eng, fn, reads, writes, dma, tag = PRE.pop(0)
            o_(eng, fn, reads=reads, writes=writes, dma=dma, tag=tag)

    def ln_rows(supb, supr, sqpb, sqpr, tag="ln"):
        o_("vector", lambda e: e.tensor_scalar_mul(mean_r, supb[0:1, :], 1.0 / D),
           reads=(supr,), writes=("mean",), tag=tag)
        o_("vector", lambda e: e.tensor_scalar_mul(m2_r, sqpb[64:65, :], 1.0 / D),
           reads=(sqpr,), writes=("m2",), tag=tag)
        o_("vector", lambda e: e.tensor_tensor(var_r, mean_r, mean_r, op=ALU.mult),
           reads=("mean",), writes=("var",), tag=tag)
        o_("vector", lambda e: e.tensor_tensor(var_r, m2_r, var_r, op=ALU.subtract),
           reads=("m2", "var"), writes=("var",), tag=tag)
        o_("vector", lambda e: e.tensor_scalar_add(var_r, var_r, LN_EPS),
           reads=("var",), writes=("var",), tag=tag)
        o_("scalar", lambda e: e.activation(var_r, var_r, AF.Ln),
           reads=("var",), writes=("var",), tag=tag)
        o_("scalar", lambda e: e.activation(arow_r, var_r, AF.Exp, scale=-0.5),
           reads=("var",), writes=("arow",), tag=tag)
        o_("vector", lambda e: e.tensor_tensor(brow_r, mean_r, arow_r, op=ALU.mult),
           reads=("mean", "arow"), writes=("brow",), tag=tag)
        o_("vector", lambda e: e.tensor_scalar_mul(brow_r, brow_r, -1.0),
           reads=("brow",), writes=("brow",), tag=tag)

    def emit_ln_norm(b_, RSRC, rfmt, gb, xout, xoutres,
                     agb, agrA, agrB, supb, supr, sqpb, sqpr):
        o_("tensor",
           lambda e: e.matmul(supb[:, 0:512], ones128, arow_r,
                              start=True, stop=True),
           reads=("arow", "ROWC"), writes=(supr,), tag="lnag")
        o_("scalar", lambda e: e.copy(agb[:, 0:512], supb[:, 0:512]),
           reads=(supr,), writes=(agrA,), tag="lnev")
        o_("tensor",
           lambda e: e.matmul(sqpb[:, 0:512], ones128, brow_r,
                              start=True, stop=True),
           reads=("brow", "ROWC"), writes=(sqpr,), tag="lnbg")
        o_("scalar", lambda e: e.copy(agb[:, 512:1024], sqpb[:, 0:512]),
           reads=(sqpr,), writes=(agrB,), tag="lnev")
        for dc in range(NCH):
            o_("vector",
               lambda e, d0=dc:
                   e.tensor_tensor(RSRC[:, d0 * 512:(d0 + 1) * 512],
                                   RSRC[:, d0 * 512:(d0 + 1) * 512],
                                   agb[:, 0:512], op=ALU.mult),
               reads=(agrA,), writes=(rfmt.format(dc),), tag="lnmul")
            o_("vector",
               lambda e, d0=dc:
                   e.tensor_tensor(RSRC[:, d0 * 512:(d0 + 1) * 512],
                                   RSRC[:, d0 * 512:(d0 + 1) * 512],
                                   agb[:, 512:1024], op=ALU.add),
               reads=(agrB,), writes=(rfmt.format(dc),), tag="lnadd")
            if gb is not None:
                go, bo_ = gb
                o_("vector",
                   lambda e, d0=dc, g0=go, b0=bo_:
                       e.tensor_scalar(RSRC[:, d0 * 512:(d0 + 1) * 512],
                                       RSRC[:, d0 * 512:(d0 + 1) * 512],
                                       GBC[:, g0 + d0:g0 + d0 + 1],
                                       GBC[:, b0 + d0:b0 + d0 + 1],
                                       op0=ALU.mult, op1=ALU.add),
                   reads=("GBC",), writes=(rfmt.format(dc),), tag="lngb")
        if xout is not None:
            src = RSRC[:, 0:4096].rearrange("p (c t) -> p c t", t=TB)
            dst = xout.rearrange("(c p) t -> p c t", p=128)[:, :, b_ * TB:(b_ + 1) * TB]
            o_("sync", lambda e, a=dst, b2=src: e.dma_start(out=a, in_=b2),
               reads=tuple(rfmt.format(dc) for dc in range(NCH)),
               writes=(xoutres,), dma=True, tag="xspill")

    def emit_routing(b_, tt4, rbk, rbr, rb0):
        tile = 4 * b_ + tt4
        r_ = lambda a, b2: RTS[:, rb0 + a:rb0 + b2]
        G12 = r_(0, 12); E12 = r_(12, 24)
        S4 = r_(24, 25); MX4 = r_(25, 26); R4 = r_(26, 27)
        GP = r_(27, 28); OH4 = r_(28, 32); CUM = r_(32, 33)
        TMP = r_(33, 34); PS4 = r_(36, 40); PM4 = r_(40, 44)
        IE4 = r_(44, 48); PM1 = r_(48, 49); PS1 = r_(49, 50)
        RPS = r_(50, 51); EP = r_(51, 52); WC = r_(52, 53)
        T4A = r_(56, 60); T4B = r_(60, 64)
        RT = f"RTS{rb0 // 64}"

        def v(fn, reads=(RT,), writes=(RT,)):
            o_("vector", fn, reads=reads, writes=writes, tag="rt")

        v(lambda e, t0=tt4: e.tensor_tensor(
            G12, rbk[:, t0 * 12:(t0 + 1) * 12], BRB[:, :], op=ALU.add),
          reads=(rbr, "BRB"))
        o_("scalar", lambda e: e.activation(E12, G12, AF.Exp),
           reads=(RT,), writes=(RT,), tag="rt")
        v(lambda e: e.reduce_sum(S4, E12[:, 0:4], axis=AX.X))
        v(lambda e: e.reduce_max(MX4, E12[:, 0:4], axis=AX.X))
        o_("scalar", lambda e: e.activation(R4, S4, AF.Ln),
           reads=(RT,), writes=(RT,), tag="rt")
        o_("scalar", lambda e: e.activation(R4, R4, AF.Exp, scale=-1.0),
           reads=(RT,), writes=(RT,), tag="rt")
        v(lambda e: e.tensor_tensor(GP, MX4, R4, op=ALU.mult))
        v(lambda e: e.tensor_scalar(OH4, E12[:, 0:4], MX4, None, op0=ALU.is_ge))
        v(lambda e: e.tensor_copy(CUM, OH4[:, 0:1]))
        for g in range(1, 4):
            v(lambda e: e.tensor_scalar(TMP, CUM, -1.0, 1.0,
                                        op0=ALU.mult, op1=ALU.add))
            v(lambda e, g_=g: e.tensor_tensor(OH4[:, g_:g_ + 1],
                                              OH4[:, g_:g_ + 1], TMP, op=ALU.mult))
            if g < 3:
                v(lambda e, g_=g: e.tensor_tensor(CUM, CUM, OH4[:, g_:g_ + 1],
                                                  op=ALU.add))
        Ev = E12[:, 4:12].rearrange("p (g two) -> p g two", two=2)
        v(lambda e: e.tensor_tensor(PS4, Ev[:, :, 0], Ev[:, :, 1], op=ALU.add))
        v(lambda e: e.tensor_tensor(PM4, Ev[:, :, 0], Ev[:, :, 1], op=ALU.max))
        v(lambda e: e.tensor_tensor(IE4, Ev[:, :, 0], Ev[:, :, 1], op=ALU.is_ge))
        v(lambda e: e.tensor_tensor(T4A, OH4, PM4, op=ALU.mult))
        v(lambda e: e.reduce_sum(PM1, T4A, axis=AX.X))
        v(lambda e: e.tensor_tensor(T4A, OH4, PS4, op=ALU.mult))
        v(lambda e: e.reduce_sum(PS1, T4A, axis=AX.X))
        o_("scalar", lambda e: e.activation(RPS, PS1, AF.Ln),
           reads=(RT,), writes=(RT,), tag="rt")
        o_("scalar", lambda e: e.activation(RPS, RPS, AF.Exp, scale=-1.0),
           reads=(RT,), writes=(RT,), tag="rt")
        v(lambda e: e.tensor_tensor(EP, PM1, RPS, op=ALU.mult))
        v(lambda e: e.tensor_tensor(WC, GP, EP, op=ALU.mult))
        cview = CMB[:, tile * 8:(tile + 1) * 8].rearrange("p (g two) -> p g two",
                                                          two=2)
        v(lambda e: e.tensor_tensor(T4A, OH4, IE4, op=ALU.mult))
        v(lambda e: e.tensor_scalar(T4A, T4A, WC, None, op0=ALU.mult))
        v(lambda e, cv=cview: e.tensor_copy(cv[:, :, 0], T4A),
          reads=(RT,), writes=("CMB",))
        v(lambda e: e.tensor_scalar(T4B, IE4, -1.0, 1.0, op0=ALU.mult, op1=ALU.add))
        v(lambda e: e.tensor_tensor(T4B, OH4, T4B, op=ALU.mult))
        v(lambda e: e.tensor_scalar(T4B, T4B, WC, None, op0=ALU.mult))
        v(lambda e, cv=cview: e.tensor_copy(cv[:, :, 1], T4B),
          reads=(RT,), writes=("CMB",))
        v(lambda e, ti=tile: e.tensor_copy(WCTS[:, ti:ti + 1], WC),
          reads=(RT,), writes=("WCTS",))

    def attention(ai, xsp, xres_in, xout, xoutres, WQd, WKd, WVd, WOd,
                  alibi, causal, bias_qkv, bias_o, gb, routing,
                  stop_after=None, tokspill=None):
        kext = 66 if alibi else 64
        pend_tail = [None]
        o_("sync", lambda e, a0=ai: e.dma_start(out=ALIBB[:, :], in_=hcc[a0]),
           writes=("ALIBB",), dma=True, tag="hcld")
        for half in (0, 1):
            wsl_k = wslot()
            load_weight(WKd[0], WKd[1] + half * 512, WKd[1] + half * 512 + 512,
                        wsl_k, "wk")
            wsl_v = wslot()
            load_weight(WVd[0], WVd[1] + half * 512, WVd[1] + half * 512 + 512,
                        wsl_v, "wv")
            if alibi:
                for hh in range(8):
                    h = 8 * half + hh
                    o_("sync",
                       lambda e, h0=h, t0=hh:
                           e.dma_start(out=K_AR[64:66, t0 * S:(t0 + 1) * S],
                                       in_=sl2[h0]),
                       writes=("K_AR",), dma=True, tag="sl2")
            for b_ in range(NB):
                xsl = b_ % 2
                load_xblock(xsp, xres_in, b_, xsl)
                for jjl in range(4):
                    jc = 4 * half + jjl
                    pk = state["pk"] % 2
                    state["pk"] += 1
                    for dc in range(NCH):
                        o_("tensor",
                           lambda e, p0=pk, w0=wsl_k, j0=jjl, d0=dc, x0_=xsl,
                                  st=(dc == 0), sp=(dc == NCH - 1):
                               e.matmul(P[p0][:, :],
                                        W_AR[:, w0 * 4096 + d0 * 512 + j0 * 128:
                                             w0 * 4096 + d0 * 512 + j0 * 128 + 128],
                                        XB[:, x0_ * 4096 + d0 * 512:
                                           x0_ * 4096 + (d0 + 1) * 512],
                                        start=st, stop=sp),
                           reads=(f"W{wsl_k}", f"XB{xsl}"), writes=(f"P{pk}",),
                           tag="kproj")
                    ssl = jjl % 2
                    if bias_qkv:
                        o_("scalar",
                           lambda e, p0=pk, s0=ssl, j0=jc:
                               e.activation(STG[:, s0 * 512:(s0 + 1) * 512],
                                            P[p0][:, :], AF.Identity,
                                            bias=BIC[:, 8 + j0:9 + j0]),
                           reads=(f"P{pk}", "BIC"), writes=(f"STG{ssl}",), tag="kev")
                    else:
                        o_("scalar",
                           lambda e, p0=pk, s0=ssl:
                               e.copy(STG[:, s0 * 512:(s0 + 1) * 512], P[p0][:, :]),
                           reads=(f"P{pk}",), writes=(f"STG{ssl}",), tag="kev")
                    o_("scalar",
                       lambda e, s0=ssl: e.activation(
                           SQ[:, s0 * 512:(s0 + 1) * 512],
                           STG[:, s0 * 512:(s0 + 1) * 512], AF.Square),
                       reads=(f"STG{ssl}",), writes=(f"SQ{ssl}",), tag="ksq")
                    o_("tensor",
                       lambda e, j0=jc, s0=ssl, st=(jjl == 0), sp=(jjl == 3):
                           e.matmul(P[4][0:16, :], ONES16[:, j0 * 16:(j0 + 1) * 16],
                                    SQ[:, s0 * 512:(s0 + 1) * 512],
                                    start=st, stop=sp),
                       reads=(f"SQ{ssl}", "ONES16"), writes=("P4",), tag="kstat")
                    he, ho = 2 * jjl, 2 * jjl + 1
                    if alibi:
                        o_("vector",
                           lambda e, h0=he, s0=ssl, b0=b_:
                               e.tensor_copy(K_AR[0:64, h0 * S + b0 * TB:
                                                  h0 * S + (b0 + 1) * TB],
                                             STG[0:64, s0 * 512:(s0 + 1) * 512]),
                           reads=(f"STG{ssl}",), writes=("K_AR",), tag="kmove")
                        o_("gpsimd",
                           lambda e, h0=ho, s0=ssl, b0=b_:
                               e.dma_start(out=K_AR[0:64, h0 * S + b0 * TB:
                                                   h0 * S + (b0 + 1) * TB],
                                           in_=STG[64:128, s0 * 512:(s0 + 1) * 512]),
                           reads=(f"STG{ssl}",), writes=("K_AR",), dma=True,
                           tag="kmove2")
                    else:
                        o_("vector",
                           lambda e, j0=jjl, s0=ssl, b0=b_:
                               e.tensor_copy(K_AR[:, j0 * S + b0 * TB:
                                                  j0 * S + (b0 + 1) * TB],
                                             STG[:, s0 * 512:(s0 + 1) * 512]),
                           reads=(f"STG{ssl}",), writes=("K_AR",), tag="kmove")
                o_("vector", lambda e: e.reduce_max(SC16[:, 2:3], P[4][0:16, :],
                                                    axis=AX.X),
                   reads=("P4",), writes=("SC16t",), tag="kmax")
                if b_ == 0:
                    o_("vector", lambda e: e.tensor_copy(SC16[:, 1:2], SC16[:, 2:3]),
                       reads=("SC16t",), writes=("SC16k",), tag="kmax2")
                else:
                    o_("vector", lambda e: e.tensor_tensor(
                        SC16[:, 1:2], SC16[:, 1:2], SC16[:, 2:3], op=ALU.max),
                       reads=("SC16t", "SC16k"), writes=("SC16k",), tag="kmax2")
            for b_ in range(NB):
                xsl = b_ % 2
                load_xblock(xsp, xres_in, b_, xsl, tag="xldv")
                for tt4 in range(4):
                    t_abs = 4 * b_ + tt4
                    pk = state["pk"] % 2
                    state["pk"] += 1
                    for dc in range(NCH):
                        o_("tensor",
                           lambda e, p0=pk, w0=wsl_v, t0=tt4, d0=dc, x0_=xsl,
                                  st=(dc == 0), sp=(dc == NCH - 1):
                               e.matmul(P[p0][:, :],
                                        XB[:, x0_ * 4096 + d0 * 512 + t0 * 128:
                                           x0_ * 4096 + d0 * 512 + t0 * 128 + 128],
                                        W_AR[:, w0 * 4096 + d0 * 512:
                                             w0 * 4096 + (d0 + 1) * 512],
                                        start=st, stop=sp),
                           reads=(f"W{wsl_v}", f"XB{xsl}"), writes=(f"P{pk}",),
                           tag="vproj")
                    vview = V_AR[:, t_abs * 520:t_abs * 520 + 8 * 65] \
                        .rearrange("p (h x) -> p h x", x=65)[:, :, 0:64]
                    o_("scalar",
                       lambda e, p0=pk, vv=vview:
                           e.copy(vv, P[p0][:, :].rearrange(
                               "p (h x) -> p h x", x=64)),
                       reads=(f"P{pk}",), writes=("V_AR",), tag="vev")
            wsl_q = wslot()
            load_weight(WQd[0], WQd[1] + half * 512, WQd[1] + half * 512 + 512,
                        wsl_q, "wq")
            for b_ in range(NB):
                xsl = b_ % 2
                load_xblock(xsp, xres_in, b_, xsl, tag="xldq")
                for jjl in range(4):
                    jc = 4 * half + jjl
                    pk = state["pk"] % 2
                    state["pk"] += 1
                    for dc in range(NCH):
                        o_("tensor",
                           lambda e, p0=pk, w0=wsl_q, j0=jjl, d0=dc, x0_=xsl,
                                  st=(dc == 0), sp=(dc == NCH - 1):
                               e.matmul(P[p0][:, :],
                                        W_AR[:, w0 * 4096 + d0 * 512 + j0 * 128:
                                             w0 * 4096 + d0 * 512 + j0 * 128 + 128],
                                        XB[:, x0_ * 4096 + d0 * 512:
                                           x0_ * 4096 + (d0 + 1) * 512],
                                        start=st, stop=sp),
                           reads=(f"W{wsl_q}", f"XB{xsl}"), writes=(f"P{pk}",),
                           tag="qproj")
                    ssl = jjl % 2
                    if bias_qkv:
                        o_("scalar",
                           lambda e, p0=pk, s0=ssl, j0=jc:
                               e.activation(STG[:, s0 * 512:(s0 + 1) * 512],
                                            P[p0][:, :], AF.Identity,
                                            bias=BIC[:, j0:j0 + 1]),
                           reads=(f"P{pk}", "BIC"), writes=(f"STG{ssl}",), tag="qev")
                    else:
                        o_("scalar",
                           lambda e, p0=pk, s0=ssl:
                               e.copy(STG[:, s0 * 512:(s0 + 1) * 512], P[p0][:, :]),
                           reads=(f"P{pk}",), writes=(f"STG{ssl}",), tag="qev")
                    o_("scalar",
                       lambda e, s0=ssl: e.activation(
                           SQ[:, s0 * 512:(s0 + 1) * 512],
                           STG[:, s0 * 512:(s0 + 1) * 512], AF.Square),
                       reads=(f"STG{ssl}",), writes=(f"SQ{ssl}",), tag="qsq")
                    o_("tensor",
                       lambda e, j0=jc, s0=ssl, st=(jjl == 0), sp=(jjl == 3):
                           e.matmul(P[5][0:16, :], ONES16[:, j0 * 16:(j0 + 1) * 16],
                                    SQ[:, s0 * 512:(s0 + 1) * 512],
                                    start=st, stop=sp),
                       reads=(f"SQ{ssl}", "ONES16"), writes=("P5",), tag="qstat")
                    he, ho = 2 * jc, 2 * jc + 1
                    o_("sync",
                       lambda e, h0=he, s0=ssl, b0=b_:
                           e.dma_start(out=qsp[h0, :, b0 * TB:(b0 + 1) * TB],
                                       in_=STG[0:64, s0 * 512:(s0 + 1) * 512]),
                       reads=(f"STG{ssl}",), writes=("qsp",), dma=True, tag="qspill")
                    o_("sync",
                       lambda e, h0=ho, s0=ssl, b0=b_:
                           e.dma_start(out=qsp[h0, :, b0 * TB:(b0 + 1) * TB],
                                       in_=STG[64:128, s0 * 512:(s0 + 1) * 512]),
                       reads=(f"STG{ssl}",), writes=("qsp",), dma=True, tag="qspill")
                o_("vector", lambda e: e.reduce_max(SC16[:, 3:4], P[5][0:16, :],
                                                    axis=AX.X),
                   reads=("P5",), writes=("SC16s",), tag="qmax")
                if b_ == 0:
                    o_("vector", lambda e: e.tensor_copy(SC16[:, 0:1], SC16[:, 3:4]),
                       reads=("SC16s",), writes=("SC16q",), tag="qmax2")
                else:
                    o_("vector", lambda e: e.tensor_tensor(
                        SC16[:, 0:1], SC16[:, 0:1], SC16[:, 3:4], op=ALU.max),
                       reads=("SC16s", "SC16q"), writes=("SC16q",), tag="qmax2")
            o_("vector", lambda e: e.tensor_tensor(SC16[:, 4:5], SC16[:, 0:1],
                                                   SC16[:, 1:2], op=ALU.mult),
               reads=("SC16q", "SC16k"), writes=("SC16p",), tag="bfin")
            o_("scalar", lambda e: e.activation(SC16[:, 5:6], SC16[:, 4:5],
                                                AF.Ln),
               reads=("SC16p",), writes=("SC16l",), tag="bfin")
            o_("scalar", lambda e: e.activation(SC16[:, 6:7], SC16[:, 5:6],
                                                AF.Exp, scale=0.5),
               reads=("SC16l",), writes=("SC16e",), tag="bfin")
            o_("vector", lambda e: e.tensor_scalar(SC16[:, 7:8], SC16[:, 6:7],
                                                   -0.1251250, -0.5,
                                                   op0=ALU.mult, op1=ALU.add),
               reads=("SC16e",), writes=("SC16b",), tag="bfin")
            o_("vector", lambda e: e.tensor_scalar(BCT[:, :], ONE16[:, :],
                                                   SC16[0:16, 7:8], None,
                                                   op0=ALU.mult),
               reads=("SC16b", "ONE16"), writes=("BCT",), tag="bfin")
            c0b = half * 8 * 36
            c1b = (half + 1) * 8 * 36
            o_("tensor",
               lambda e, a0=c0b, a1=c1b: e.matmul(P[5][:, 0:a1 - a0],
                                                  BCT[:, :], EXP16[:, a0:a1],
                                                  start=True, stop=True),
               reads=("BCT", "EXP16"), writes=("P5",), tag="bbc")
            o_("vector",
               lambda e, a0=c0b, a1=c1b: e.tensor_tensor(
                   ALIBB[:, a0:a1], ALIBB[:, a0:a1], P[5][:, 0:a1 - a0],
                   op=ALU.add),
               reads=("P5", "ALIBB"), writes=("ALIBB",), tag="bbc")
            if stop_after == "kv":
                continue
            if not alibi:
                for b_ in range(NB):
                    for pj in range(4):
                        h_e = 8 * half + 2 * pj
                        h_o = h_e + 1
                        pcnt = half * NB * 4 + b_ * 4 + pj
                        qsl = pcnt % 2
                        drain_pre(1)
                        o_("sync",
                           lambda e, h0=h_e, b0=b_, q0=qsl:
                               e.dma_start(out=QS[0:64, q0 * 512:(q0 + 1) * 512],
                                           in_=qsp[h0, :, b0 * TB:(b0 + 1) * TB]),
                           reads=("qsp",), writes=(f"QS{qsl}",), dma=True,
                           tag="qld")
                        o_("sync",
                           lambda e, h0=h_o, b0=b_, q0=qsl:
                               e.dma_start(out=QS[64:128, q0 * 512:(q0 + 1) * 512],
                                           in_=qsp[h0, :, b0 * TB:(b0 + 1) * TB]),
                           reads=("qsp",), writes=(f"QS{qsl}",), dma=True,
                           tag="qld")
                        LAGP = 2

                        def emit_smm(i, pj=pj, qsl=qsl):
                            for (sb_, base, tp) in (
                                    ([0, 1][i % 2], 0, (0, 0)),
                                    ([4, 5][i % 2], 64, (64, 0))):
                                o_("tensor",
                                   lambda e, p0=sb_, j0=pj, c0=i, q0=qsl,
                                          b0=base, t0=tp:
                                       e.matmul(P[p0][:, :],
                                                K_AR[b0:b0 + 64,
                                                     j0 * S + c0 * 128:
                                                     j0 * S + c0 * 128 + 128],
                                                QS[b0:b0 + 64,
                                                   q0 * 512:(q0 + 1) * 512],
                                                start=True, stop=True,
                                                tile_position=t0),
                                   reads=("K_AR", f"QS{qsl}"),
                                   writes=(f"P{sb_}",), tag="smm")

                        def emit_exp(i, h_e=h_e, h_o=h_o):
                            for (sb_, base, hq) in (
                                    ([0, 1][i % 2], 0, h_e),
                                    ([4, 5][i % 2], 64, h_o)):
                                dcol = hq * 36 + 32
                                psl = (2 * i + (0 if base == 0 else 1)) % 4
                                o_("scalar",
                                   lambda e, p0=sb_, s0=psl, cc=dcol:
                                       e.activation(
                                           PTB[:, s0 * 512:(s0 + 1) * 512],
                                           P[p0][:, :], AF.Exp,
                                           bias=ALIBB[:, cc:cc + 1],
                                           scale=0.125),
                                   reads=(f"P{sb_}", "ALIBB"),
                                   writes=(f"PTB{psl}",), tag="exp")

                        def emit_avp(i, pj=pj):
                            for (av, hh2, par) in ((2, 2 * pj, 0),
                                                   (3, 2 * pj + 1, 1)):
                                psl = (2 * i + par) % 4
                                o_("tensor",
                                   lambda e, a0=av, c0=i, hx=hh2, s0=psl,
                                          st=(i == 0), sp=(i == 31):
                                       e.matmul(P[a0][0:65, :],
                                                V_AR[:, c0 * 520 + hx * 65:
                                                     c0 * 520 + hx * 65 + 65],
                                                PTB[:, s0 * 512:(s0 + 1) * 512],
                                                start=st, stop=sp),
                                   reads=("V_AR", f"PTB{psl}"),
                                   writes=(f"P{av}",), tag="avmm")

                        for i in range(32):
                            emit_smm(i)
                            if i == 2 and pend_tail[0] is not None:
                                pend_tail[0]()
                                pend_tail[0] = None
                            if i >= LAGP:
                                emit_avp(i - LAGP)
                            emit_exp(i)
                        for i in range(32 - LAGP, 32):
                            emit_avp(i)
                        if pend_tail[0] is not None:
                            pend_tail[0]()
                            pend_tail[0] = None

                        def mk_tailp(h_e=h_e, h_o=h_o, b_=b_):
                            def tail():
                                for (av, hq, osl) in ((2, h_e, 0), (3, h_o, 1)):
                                    o_("vector",
                                       lambda e, a0=av: e.reciprocal(
                                           OUTS[64:65, 0:512], P[a0][64:65, :]),
                                       reads=(f"P{av}",), writes=("rrow",),
                                       tag="recip")
                                    o_("tensor",
                                       lambda e: e.matmul(P[5][0:64, :],
                                                          OUTS[64:65, 512:576],
                                                          OUTS[64:65, 0:512],
                                                          start=True, stop=True),
                                       reads=("rrow", "RRBonesF"), writes=("P5",),
                                       tag="rbmm")
                                    o_("scalar",
                                       lambda e: e.copy(RB[0:64, :],
                                                        P[5][0:64, :]),
                                       reads=("P5",), writes=("RB",), tag="rbev")
                                    o_("vector",
                                       lambda e, a0=av, s0=osl:
                                           e.tensor_tensor(
                                               OTS[0:64,
                                                   s0 * 512:(s0 + 1) * 512],
                                               P[a0][0:64, :], RB[0:64, :],
                                               op=ALU.mult),
                                       reads=(f"P{av}", "RB"),
                                       writes=(f"OTS{osl}",), tag="onorm")
                                    if bias_qkv:
                                        o_("vector",
                                           lambda e, s0=osl, h0=hq:
                                               e.tensor_scalar(
                                                   OTS[0:64,
                                                       s0 * 512:(s0 + 1) * 512],
                                                   OTS[0:64,
                                                       s0 * 512:(s0 + 1) * 512],
                                                   BVH[:, h0:h0 + 1], None,
                                                   op0=ALU.add),
                                           reads=("BVH",),
                                           writes=(f"OTS{osl}",), tag="obias")
                                    o_("gpsimd",
                                       lambda e, h0=hq, b0=b_, s0=osl:
                                           e.dma_start(
                                               out=osp[64 * h0:64 * h0 + 64,
                                                       b0 * TB:(b0 + 1) * TB],
                                               in_=OTS[0:64,
                                                       s0 * 512:(s0 + 1) * 512]),
                                       reads=(f"OTS{osl}",), writes=("osp",),
                                       dma=True, tag="ospill")
                            return tail
                        pend_tail[0] = mk_tailp()
            else:
              for b_ in range(NB):
                nch = 4 * b_ + 4 if causal else 32
                for hp in range(4):
                    drain_pre(8)
                    hlists = []
                    pair_tails = []
                    for hh in (2 * hp, 2 * hp + 1):
                        h = 8 * half + hh
                        hcnt = half * NB * 8 + b_ * 8 + hh
                        qsl = hcnt % 2
                        avb = 2 + (hcnt % 2)
                        sbks = (0, 1) if hh % 2 == 0 else (4, 5)
                        rbk = 1 if hh % 2 == 0 else 5
                        pt0 = 0 if hh % 2 == 0 else 2
                        if causal:
                            dwin = int(110.0 / max(slope_eff_py[h], 1e-9))
                            cmin = max(0, (512 * b_ - 127 - dwin + 127) // 128)
                        else:
                            cmin = 0
                        chunks = list(range(cmin, nch))
                        LAG = 2

                        def av_op(i, chunks=chunks, avb=avb, hh=hh, pt0=pt0):
                            c = chunks[i]
                            psl = pt0 + i % 2
                            o_("tensor",
                               lambda e, a0=avb, c0=c, hx=hh, s0=psl,
                                      st=(i == 0), sp=(i == len(chunks) - 1):
                                   e.matmul(P[a0][0:65, :],
                                            V_AR[:, c0 * 520 + hx * 65:
                                                 c0 * 520 + hx * 65 + 65],
                                            PTB[:, s0 * 512:(s0 + 1) * 512],
                                            start=st, stop=sp),
                               reads=("V_AR", f"PTB{psl}"),
                               writes=(f"P{avb}",), tag="avmm")

                        def body(hh=hh, h=h, qsl=qsl, sbks=sbks, pt0=pt0,
                                 b2=b_, chunks=chunks, av_op=av_op):
                            o_("sync",
                               lambda e, h0=h, b0=b2, q0=qsl:
                                   e.dma_start(
                                       out=QS[0:64, q0 * 512:(q0 + 1) * 512],
                                       in_=qsp[h0, :, b0 * TB:(b0 + 1) * TB]),
                               reads=("qsp",), writes=(f"QS{qsl}",), dma=True,
                               tag="qld")
                            for i in range(len(chunks)):
                                c = chunks[i]
                                psl = pt0 + i % 2
                                sbk = sbks[i % 2]
                                o_("tensor",
                                   lambda e, p0=sbk, h0=hh, c0=c, q0=qsl:
                                       e.matmul(P[p0][:, :],
                                                K_AR[0:kext,
                                                     h0 * S + c0 * 128:
                                                     h0 * S + c0 * 128 + 128],
                                                QS[0:kext,
                                                   q0 * 512:(q0 + 1) * 512],
                                                start=True, stop=True),
                                   reads=("K_AR", f"QS{qsl}", "QSj0", "QSj1"),
                                   writes=(f"P{sbk}",), tag="smm")
                                if causal and c >= 4 * b2:
                                    off = 384 - 128 * (c - 4 * b2)
                                    o_("vector",
                                       lambda e, p0=sbk, f0=off:
                                           e.tensor_tensor(
                                               P[p0][:, :], P[p0][:, :],
                                               MASKN[:, f0:f0 + 512],
                                               op=ALU.add),
                                       reads=(f"P{sbk}", "MASKN"),
                                       writes=(f"P{sbk}",), tag="mask")
                                if i >= LAG:
                                    av_op(i - LAG)
                                dcol = h * 36 + ((c - 4 * b2 + 32)
                                                 if causal else 32)
                                o_("scalar",
                                   lambda e, p0=sbk, s0=psl, cc=dcol:
                                       e.activation(
                                           PTB[:, s0 * 512:(s0 + 1) * 512],
                                           P[p0][:, :], AF.Exp,
                                           bias=ALIBB[:, cc:cc + 1],
                                           scale=0.125),
                                   reads=(f"P{sbk}", "ALIBB"),
                                   writes=(f"PTB{psl}",), tag="exp")
                            for i in range(max(0, len(chunks) - LAG),
                                           len(chunks)):
                                av_op(i)
                        hlists.append(collect_ops(body))

                        def mk_tail(avb=avb, rbk=rbk, h=h, b2=b_,
                                    osl=hcnt % 2):
                            def tail():
                                o_("scalar",
                                   lambda e, a0=avb: e.activation(
                                       OUTS[64:65, 0:512], P[a0][64:65, :],
                                       AF.Ln),
                                   reads=(f"P{avb}",), writes=("rrow",),
                                   tag="recip")
                                o_("scalar",
                                   lambda e: e.activation(
                                       OUTS[64:65, 0:512],
                                       OUTS[64:65, 0:512],
                                       AF.Exp, scale=-1.0),
                                   reads=("rrow",), writes=("rrow",),
                                   tag="recip")
                                o_("tensor",
                                   lambda e, r0=rbk: e.matmul(
                                       P[r0][0:64, :],
                                       OUTS[64:65, 512:576],
                                       OUTS[64:65, 0:512],
                                       start=True, stop=True),
                                   reads=("rrow", "RRBonesF"),
                                   writes=(f"P{rbk}",), tag="rbmm")
                                o_("scalar",
                                   lambda e, r0=rbk: e.copy(RB[0:64, :],
                                                            P[r0][0:64, :]),
                                   reads=(f"P{rbk}",), writes=("RB",),
                                   tag="rbev")
                                o_("vector",
                                   lambda e, a0=avb, s0=osl:
                                       e.tensor_tensor(
                                           OTS[0:64,
                                               s0 * 512:(s0 + 1) * 512],
                                           P[a0][0:64, :], RB[0:64, :],
                                           op=ALU.mult),
                                   reads=(f"P{avb}", "RB"),
                                   writes=(f"OTS{osl}",), tag="onorm")
                                o_("gpsimd",
                                   lambda e, h0=h, b0=b2, s0=osl:
                                       e.dma_start(
                                           out=osp[64 * h0:64 * h0 + 64,
                                                   b0 * TB:(b0 + 1) * TB],
                                           in_=OTS[0:64,
                                                   s0 * 512:(s0 + 1) * 512]),
                                   reads=(f"OTS{osl}",), writes=("osp",),
                                   dma=True, tag="ospill")
                            return tail
                        pair_tails.append(mk_tail())
                    la, lb = hlists
                    merged = []
                    for i in range(max(len(la), len(lb))):
                        if i < len(la):
                            merged.append(la[i])
                        if i < len(lb):
                            merged.append(lb[i])
                    emit_seq(merged[:6])
                    if pend_tail[0] is not None:
                        pend_tail[0]()
                        pend_tail[0] = None
                    emit_seq(merged[6:])

                    def mk_pair(tails=tuple(pair_tails)):
                        def both():
                            for t2 in tails:
                                t2()
                        return both
                    pend_tail[0] = mk_pair()
        if pend_tail[0] is not None:
            pend_tail[0]()
            pend_tail[0] = None
        if stop_after in ("kv", "qloop"):
            return
        o_("vector", lambda e: e.memset(SC16[:, 5:6], 0.0),
           writes=("K_AR",) + tuple(f"RLB{dc}" for dc in range(NCH)) +
                  ("AGBC", "AGBD", "SQB0", "SQB1",
                   "XBB0", "XBB1", "TKB0", "TKB1", "XBB0", "XBB1",
                   "TKB0", "TKB1"),
           tag="opfence")
        wsl_o1 = wslot()
        load_weight(WOd[0], WOd[1], WOd[1] + 512, wsl_o1, "wo1")
        wsl_o2 = wslot()
        load_weight(WOd[0], WOd[1] + 512, WOd[1] + 1024, wsl_o2, "wo2")
        ARNB = K_AR[:, 0:4096]
        AGBB_ = K_AR[:, 4096:5120]
        SQB = K_AR[:, 5120:6144]
        XBB = K_AR[:, 6144:14336]
        TKB = (K_AR[:, 14336:15376], K_AR[:, 15376:16416])

        def load_oblock(xsp2, xres2, b2, dst, dres, tag):
            src2 = xsp2.rearrange("(c p) t -> p c t",
                                  p=128)[:, :, b2 * TB:(b2 + 1) * TB]
            d2 = dst.rearrange("p (c t) -> p c t", t=TB)
            o_("sync", lambda e, a=d2, s3=src2: e.dma_start(out=a, in_=s3),
               reads=(xres2,), writes=(dres,), dma=True, tag=tag)

        def o_block(b_):
            par = b_ % 2
            if par == 0:
                ARN, afmt = RLN, "RLN{}"
                agb, agrA, agrB = AGB, "AGBA", "AGBB"
                sqb, sqf = SQ, "SQ{}"
                supb, supr = P[4], "P4"
                sqpb, sqpr = P[5], "P5"
                xbs, x0r, x1r = XB, "XB0", "XB1"
                pko = 0
                tstg = ((STG, ("STG0", "STG1")), (SQT, ("SQT",)))
                rb0 = 0
            else:
                ARN, afmt = ARNB, "RLB{}"
                agb, agrA, agrB = AGBB_, "AGBC", "AGBD"
                sqb, sqf = SQB, "SQB{}"
                supb, supr = P[2], "P2"
                sqpb, sqpr = P[3], "P3"
                xbs, x0r, x1r = XBB, "XBB0", "XBB1"
                pko = 1
                tstg = ((TKB[0], ("TKB0",)), (TKB[1], ("TKB1",)))
                rb0 = 64
            load_oblock(osp, "osp", b_, xbs[:, 0:4096], x0r, "old")
            load_oblock(xsp, xres_in, b_, xbs[:, 4096:8192], x1r, "xres")
            for dc in range(NCH):
                wsl_o = wsl_o1 if dc < 4 else wsl_o2
                dcl = dc % 4
                for kc in range(NCH):
                    o_("tensor",
                       lambda e, p0=pko, w0=wsl_o, d0=dcl, k0=kc, x2=xbs,
                              st=(kc == 0), sp=(kc == NCH - 1):
                           e.matmul(P[p0][:, :],
                                    W_AR[:, w0 * 4096 + k0 * 512 + d0 * 128:
                                         w0 * 4096 + k0 * 512 + d0 * 128 + 128],
                                    x2[:, k0 * 512:(k0 + 1) * 512],
                                    start=st, stop=sp),
                       reads=(f"W{wsl_o}", x0r), writes=(f"P{pko}",),
                       tag="oproj")
                if bias_o:
                    o_("vector",
                       lambda e, p0=pko, d0=dc, a2=ARN, x2=xbs:
                           e.scalar_tensor_tensor(
                               a2[:, d0 * 512:(d0 + 1) * 512], P[p0][:, :],
                               BOC[:, d0:d0 + 1],
                               x2[:, 4096 + d0 * 512:4096 + (d0 + 1) * 512],
                               op0=ALU.add, op1=ALU.add),
                       reads=(f"P{pko}", x1r, "BOC"),
                       writes=(afmt.format(dc),), tag="resid")
                else:
                    o_("vector",
                       lambda e, p0=pko, d0=dc, a2=ARN, x2=xbs:
                           e.tensor_tensor(a2[:, d0 * 512:(d0 + 1) * 512],
                                           P[p0][:, :],
                                           x2[:, 4096 + d0 * 512:
                                              4096 + (d0 + 1) * 512],
                                           op=ALU.add),
                       reads=(f"P{pko}", x1r), writes=(afmt.format(dc),),
                       tag="resid")
                o_("scalar",
                   lambda e, d0=dc, s0=dc % 2, a2=ARN, s2=sqb: e.activation(
                       s2[:, s0 * 512:(s0 + 1) * 512],
                       a2[:, d0 * 512:(d0 + 1) * 512], AF.Square),
                   reads=(afmt.format(dc),), writes=(sqf.format(dc % 2),),
                   tag="osq")
                o_("tensor",
                   lambda e, d0=dc, a2=ARN, pb=supb,
                          st=(dc == 0), sp=(dc == NCH - 1):
                       e.matmul(pb[0:1, :], ONESB[:, 0:1],
                                a2[:, d0 * 512:(d0 + 1) * 512],
                                start=st, stop=sp),
                   reads=(afmt.format(dc), "ONESB"), writes=(supr,),
                   tag="osum")
                o_("tensor",
                   lambda e, s0=dc % 2, s2=sqb, pb=sqpb,
                          st=(dc == 0), sp=(dc == NCH - 1):
                       e.matmul(pb[64:65, :], ONESB[:, 0:1],
                                s2[:, s0 * 512:(s0 + 1) * 512],
                                start=st, stop=sp),
                   reads=(sqf.format(dc % 2), "ONESB"),
                   writes=(sqpr,), tag="osumsq")
            ln_rows(supb, supr, sqpb, sqpr)
            emit_ln_norm(b_, ARN, afmt, gb, xout, xoutres,
                         agb, agrA, agrB, supb, supr, sqpb, sqpr)
            if routing:
                for tt4 in range(4):
                    for dc in range(NCH):
                        o_("tensor",
                           lambda e, t0=tt4, d0=dc, a2=ARN, pb=sqpb,
                                  st=(dc == 0), sp=(dc == NCH - 1):
                               e.matmul(pb[:, t0 * 12:(t0 + 1) * 12],
                                        a2[:, d0 * 512 + t0 * 128:
                                            d0 * 512 + t0 * 128 + 128],
                                        WRT[:, d0 * 12:(d0 + 1) * 12],
                                        start=st, stop=sp),
                           reads=(afmt.format(dc), "WRT"),
                           writes=(sqpr,), tag="rtmm")
                for tt4 in range(4):
                    emit_routing(b_, tt4, sqpb, sqpr, rb0)
            if tokspill is not None:
                tokd, tokres, with_wc = tokspill
                twidth = 1040 if with_wc else 1024
                for tt4 in range(4):
                    stg, swr = tstg[tt4 % 2]
                    for w in range(8):
                        bank = PT7 if w % 2 == 0 else PT8
                        bres = "PT7" if w % 2 == 0 else "PT8"
                        wcol = (w // 2) * 128
                        o_("tensor",
                           lambda e, w0=w, t0=tt4, k2=bank, c2=wcol, a2=ARN:
                               e.transpose(k2[:, c2:c2 + 128],
                                           a2[:, w0 * 512 + t0 * 128:
                                               w0 * 512 + t0 * 128 + 128],
                                           IDENT[:, :]),
                           reads=(afmt.format(w), "IDENT"), writes=(bres,),
                           tag="ttp")
                        o_("scalar",
                           lambda e, w0=w, k2=bank, c2=wcol, s2=stg:
                               e.copy(s2[:, w0 * 128:(w0 + 1) * 128],
                                      k2[:, c2:c2 + 128]),
                           reads=(bres,), writes=swr, tag="tte")
                    if with_wc:
                        o_("vector",
                           lambda e, s2=stg, ti=4 * b_ + tt4:
                               e.tensor_copy(s2[:, 1024:1025],
                                             WCTS[:, ti:ti + 1]),
                           reads=("WCTS",), writes=swr, tag="ttwc")
                        o_("vector",
                           lambda e, s2=stg: e.memset(s2[:, 1025:1040], 0.0),
                           writes=swr, tag="ttwc")
                    o_("sync",
                       lambda e, s2=stg, t0=tt4, b0=b_, tw=twidth:
                           e.dma_start(out=tokd[(4 * b0 + t0) * 128:
                                                (4 * b0 + t0) * 128 + 128,
                                                0:tw],
                                       in_=s2[:, 0:tw]),
                       reads=swr, writes=(tokres,), dma=True, tag="tts")

        SEG0 = {"old", "xres", "oproj", "resid", "osq", "osum", "osumsq"}
        SEG1 = {"ln", "lnag", "lnev", "lnbg"}
        SEG3 = {"ttp", "tte", "ttwc", "tts"}

        def segs(lst):
            s0, s1, s2, s3 = [], [], [], []
            for op in lst:
                tg = op[5]
                if tg in SEG0:
                    s0.append(op)
                elif tg in SEG1:
                    s1.append(op)
                elif tg in SEG3:
                    s3.append(op)
                else:
                    s2.append(op)
            return s0, s1, s2, s3

        def rr2(la2, lb2):
            def chunks(lst):
                out_, cur = [], []
                for op in lst:
                    cur.append(op)
                    if op[5] == "tts":
                        out_.append(cur)
                        cur = []
                if cur:
                    out_.append(cur)
                return out_
            ca, cb = chunks(la2), chunks(lb2)
            for i in range(max(len(ca), len(cb))):
                if i < len(ca):
                    emit_seq(ca[i])
                if i < len(cb):
                    emit_seq(cb[i])

        for bp in range(0, NB, 2):
            s0a, s1a, s2a, s3a = segs(collect_ops(lambda: o_block(bp)))
            s0b, s1b, s2b, s3b = segs(collect_ops(lambda: o_block(bp + 1)))
            emit_rr(s0a, s0b)
            emit_seq(s1a)
            emit_rr(s2a, s1b + s2b)
            rr2(s3a, s3b)

    NT = 15

    def Hap(fc):
        if fc < 8:
            return RLN[:, fc * 512:(fc + 1) * 512]
        return K_AR[:, 12288 + (fc - 8) * 512:12288 + (fc - 7) * 512]

    def emit_moe():
        drain_pre(10 ** 9)
        yres = tuple(f"Y{t}" for t in range(NT))
        o_("vector", lambda e: e.memset(SC16[:, 2:3], 0.0),
           reads=("K_AR", "V_AR", "RB"),
           writes=yres + tuple(f"RLN{dc}" for dc in range(NCH)) +
                  tuple(f"RLB{dc}" for dc in range(NCH)) +
                  tuple(f"H{fc}" for fc in range(16)) +
                  ("AGBC", "AGBD", "SQB0", "SQB1",
                   "XBB0", "XBB1", "TKB0", "TKB1",
                   "XB0", "XB1", "SCT0", "SCT1", "SCT2", "SCT3",
                   "K_AR", "V_AR", "RB", "W1F0", "W1F1", "W2F",
                   "W0", "W1", "W2", "W3", "XGT0", "XGT1", "WCROW",
                   "ONE8", "CUM8", "YGT"),
           tag="fence")
        for b_ in range(NB):
            cb = PT7 if b_ % 2 == 0 else PT8
            cbres = "PT7" if b_ % 2 == 0 else "PT8"
            for tt4 in range(4):
                tile = 4 * b_ + tt4
                o_("tensor",
                   lambda e, t0=tt4, ti=tile, k2=cb:
                       e.transpose(k2[0:8, t0 * 128:(t0 + 1) * 128],
                                   CMB[:, ti * 8:(ti + 1) * 8], IDENT[:, :]),
                   reads=("CMB", "IDENT"), writes=(cbres,), tag="ctp")
            o_("scalar",
               lambda e, b0=b_, k2=cb:
                   e.copy(CMTR[:, b0 * 512:(b0 + 1) * 512], k2[0:8, 0:512]),
               reads=(cbres,), writes=(f"CMT{b_}",), tag="ctev")
        rfr = tuple(f"CMT{b2}" for b2 in range(NB))
        o_("vector", lambda e: e.tensor_scalar(ONE8, CMTR[:, 0:S], 0.0, None,
                                               op0=ALU.is_gt),
           reads=rfr, writes=("ONE8",), tag="rf")
        o_("vector", lambda e: e.tensor_tensor_scan(CUM8, ONE8, ONE8, 0.0,
                                                    op0=ALU.add,
                                                    op1=ALU.bypass),
           reads=("ONE8",), writes=("CUM8",), tag="rf")
        o_("tensor", lambda e: e.matmul(P[4][0:1, 0:128], CUM8[:, S - 1:S],
                                        IDENT32[0:8, 0:128],
                                        start=True, stop=True),
           reads=("CUM8", "IDENT32"), writes=("P4",), tag="rfmm")
        o_("scalar", lambda e: e.copy(mr(0, 8), P[4][0:1, 0:8]),
           reads=("P4",), writes=("MROWc",), tag="rf")
        o_("vector", lambda e: e.memset(mr(8, 16), 0.0),
           writes=("MROWt",), tag="rf")
        for k_ in range(NT):
            o_("vector",
               lambda e, k0=k_: e.tensor_scalar(mr(56, 64),
                                                mr(0, 8),
                                                512.0 * k0, None,
                                                op0=ALU.is_gt),
               reads=("MROWc",), writes=("MROWx",), tag="rf")
            o_("vector",
               lambda e: e.tensor_tensor(mr(8, 16),
                                         mr(8, 16),
                                         mr(56, 64), op=ALU.add),
               reads=("MROWx",), writes=("MROWt",), tag="rf")
        o_("vector", lambda e: e.tensor_tensor_scan(mr(16, 24),
                                                    mr(8, 16),
                                                    mr(8, 16), 0.0,
                                                    op0=ALU.add,
                                                    op1=ALU.bypass),
           reads=("MROWt",), writes=("MROWi",), tag="rf")
        o_("vector", lambda e: e.tensor_tensor(mr(24, 32),
                                               mr(16, 24),
                                               mr(8, 16),
                                               op=ALU.subtract),
           reads=("MROWi", "MROWt"), writes=("MROWs",), tag="rf")
        o_("vector", lambda e: e.tensor_scalar(mr(0, 8),
                                               mr(24, 32), 512.0,
                                               None, op0=ALU.mult),
           reads=("MROWs",), writes=("MROWb",), tag="rf")
        o_("vector", lambda e: e.memset(mr(40, 56), -1.0),
           writes=("MROWe",), tag="rf")
        for e_i in range(NE):
            o_("vector",
               lambda e, e0=e_i: e.tensor_scalar(
                   mr(56, 72), iot16_r,
                   mr(24 + e0, 25 + e0), None, op0=ALU.is_ge),
               reads=("MROWs", "ROWC"), writes=("MROWx",), tag="rf")
            o_("vector",
               lambda e: e.tensor_tensor(mr(40, 56),
                                         mr(40, 56),
                                         mr(56, 72), op=ALU.add),
               reads=("MROWx",), writes=("MROWe",), tag="rf")
        o_("tensor", lambda e: e.matmul(P[4][0:8, 128:129], mr(0, 8),
                                        one1_r, start=True, stop=True),
           reads=("MROWb", "ROWC"), writes=("P4",), tag="rfmm")
        o_("scalar", lambda e: e.copy(BASEC[0:8, 0:1], P[4][0:8, 128:129]),
           reads=("P4",), writes=("BASECr",), tag="rf")
        o_("vector", lambda e: e.tensor_scalar(CUM8, CUM8, BASEC[0:8, 0:1],
                                               -1.0, op0=ALU.add,
                                               op1=ALU.add),
           reads=("BASECr", "CUM8"), writes=("CUM8",), tag="rf")
        o_("vector", lambda e: e.tensor_tensor(CUM8, CUM8, ONE8, op=ALU.mult),
           reads=("ONE8", "CUM8"), writes=("CUM8",), tag="rf")
        o_("vector", lambda e: e.memset(ONES8F[:, 0:1], 1.0),
           writes=("O8F",), tag="rf")
        for ch in range(8):
            o_("tensor",
               lambda e, c0=ch: e.matmul(P[5][0:1, 0:512], ONES8F[0:8, 0:1],
                                         CUM8[:, c0 * 512:(c0 + 1) * 512],
                                         start=True, stop=True),
               reads=("CUM8", "O8F"), writes=("P5",), tag="rfmm")
            o_("vector",
               lambda e: e.tensor_copy(IRV[0:1, :], P[5][0:1, 0:512]),
               reads=("P5",), writes=("IROW",), tag="rf")
            o_("sync",
               lambda e, c0=ch: e.dma_start(
                   out=ib1[0:1, c0 * 512:(c0 + 1) * 512],
                   in_=IRV[0:1, :]),
               reads=("IROW",), writes=("ib1",), dma=True, tag="rf")

        def _reload(e):
            with nc.allow_non_contiguous_dma(reason="16KB index repartition"):
                return e.dma_start(out=DESTC[:, 0:32],
                                   in_=ib1.rearrange("o (t p) -> p t o",
                                                     p=128))
        o_("gpsimd", _reload, reads=("ib1",), writes=("DESTC",), dma=True,
           tag="rfr")
        o_("tensor", lambda e: e.matmul(P[4][:, 256:272], ones128,
                                        mr(40, 56),
                                        start=True, stop=True),
           reads=("MROWe", "ROWC"), writes=("P4",), tag="rfmm")
        o_("vector", lambda e: e.tensor_scalar(EIDF[:, 0:16], P[4][:, 256:272],
                                               128.0, None, op0=ALU.mult),
           reads=("P4",), writes=("EIDF",), tag="rf")
        o_("vector", lambda e: e.tensor_scalar(EIDF[:, 0:16], EIDF[:, 0:16],
                                               IOCF[:, 0:1], None,
                                               op0=ALU.add),
           reads=("IOCF", "EIDF"), writes=("EIDF",), tag="rf")
        o_("vector", lambda e: e.tensor_copy(EIDCI[:, 0:16], EIDF[:, 0:16]),
           reads=("EIDF",), writes=("EIDCI",), tag="rf")
        for tt in range(32):
            xsl = tt % 4
            o_("sync",
               lambda e, t0=tt, x0_=xsl * 2048:
                   e.dma_start(out=XB[:, x0_:x0_ + 1040],
                               in_=x2tok[t0 * 128:(t0 + 1) * 128, :]),
               reads=("x2tok",), writes=(f"SCT{xsl}",), dma=True, tag="scl")
            o_("gpsimd",
               lambda e, t0=tt, x0_=xsl * 2048:
                   e.indirect_dma_start(
                       out=xg,
                       out_offset=IndirectOffsetOnAxis(DESTC[:, t0:t0 + 1], 0),
                       in_=XB[:, x0_:x0_ + 1040], in_offset=None),
               reads=("DESTC", f"SCT{xsl}"), writes=("xg",), dma=True,
               tag="sct")
        o_("vector", lambda e: e.memset(SC16[:, 3:4], 0.0),
           writes=("ONE8", "CUM8", "W1F0", "W1F1", "XB0", "XB1",
                   "SCT0", "SCT1", "SCT2", "SCT3", "XGT0", "XGT1") +
                  tuple(f"H{fc}" for fc in range(16)),
           tag="fence2")
        STAGE = ((XGL, ("XGL",)), (STG, ("STG0", "STG1")),
                 (SQT, ("SQT",)), (PTB, ("PTB0", "PTB1", "PTB2")))

        def emit_xgl(j):
            for s in range(4):
                stg, swr = STAGE[s]
                o_("sync",
                   lambda e, j0=j, s0=s, s2=stg:
                       e.dma_start(out=s2[:, 0:1040],
                                   in_=xg[j0 * 512 + s0 * 128:
                                          j0 * 512 + s0 * 128 + 128, :]),
                   reads=("xg",), writes=swr, dma=True, tag="xgl")

        emit_xgl(0)
        for j in range(NT):
            xpar = j % 2
            xga = XB[:, xpar * 4096:(xpar + 1) * 4096]
            xgr = f"XGT{xpar}"
            o_("gpsimd",
               lambda e, j0=j: e.indirect_dma_start(
                   out=V_AR[:, 0:8192], out_offset=None, in_=w1b0,
                   in_offset=IndirectOffsetOnAxis(EIDCI[:, j0:j0 + 1], 0)),
               reads=("EIDCI", "w1b0"), writes=("W1F0",), dma=True, tag="w1g")
            o_("gpsimd",
               lambda e, j0=j: e.indirect_dma_start(
                   out=V_AR[:, 8192:16384], out_offset=None, in_=w1b1,
                   in_offset=IndirectOffsetOnAxis(EIDCI[:, j0:j0 + 1], 0)),
               reads=("EIDCI", "w1b1"), writes=("W1F1",), dma=True, tag="w1g")
            o_("gpsimd",
               lambda e, j0=j: e.indirect_dma_start(
                   out=W_AR[:, 0:16384], out_offset=None, in_=w2b,
                   in_offset=IndirectOffsetOnAxis(EIDCI[:, j0:j0 + 1], 0)),
               reads=("EIDCI", "w2b"), writes=("W2F",), dma=True, tag="w2g")
            o_("gpsimd",
               lambda e, j0=j: e.indirect_dma_start(
                   out=B1T[:, 0:24], out_offset=None, in_=b12r,
                   in_offset=IndirectOffsetOnAxis(EIDCI[:, j0:j0 + 1], 0)),
               reads=("EIDCI",), writes=("B1T",), dma=True, tag="bg")
            for sct in range(4):
                stg, swr = STAGE[sct]
                for w in range(9):
                    bank = PT7 if w % 2 == 0 else PT8
                    bres = "PT7" if w % 2 == 0 else "PT8"
                    wcol = (w // 2) * 128
                    if w < 8:
                        o_("tensor",
                           lambda e, w0=w, k2=bank, c2=wcol, s2=stg:
                               e.transpose(k2[:, c2:c2 + 128],
                                           s2[:, w0 * 128:(w0 + 1) * 128],
                                           IDENT[:, :]),
                           reads=swr + ("IDENT",), writes=(bres,), tag="xgt")
                        o_("scalar",
                           lambda e, w0=w, s0=sct, k2=bank, c2=wcol, x2=xga:
                               e.copy(x2[:, w0 * 512 + s0 * 128:
                                         w0 * 512 + s0 * 128 + 128],
                                      k2[:, c2:c2 + 128]),
                           reads=(bres,), writes=(xgr,), tag="xge")
                    else:
                        o_("tensor",
                           lambda e, s2=stg: e.matmul(P[4][0:1, 384:512],
                                                      s2[:, 1024:1025],
                                                      IDENT[:, :],
                                                      start=True, stop=True),
                           reads=swr + ("IDENT",), writes=("P4",),
                           tag="xgt")
                        o_("scalar",
                           lambda e, s0=sct:
                               e.copy(OTS[64:65, s0 * 128:(s0 + 1) * 128],
                                      P[4][0:1, 384:512]),
                           reads=("P4",), writes=("WCROW",), tag="wce")
            if j + 1 < NT:
                emit_xgl(j + 1)
            o_("tensor", lambda e: e.matmul(P[5][:, :], OTS[64:65, 512:640],
                                            OTS[64:65, 0:512],
                                            start=True, stop=True),
               reads=("WCROW", "RRBones"), writes=("P5",), tag="cbmm")
            o_("scalar", lambda e: e.copy(RB[:, :], P[5][:, :]),
               reads=("P5",), writes=("RB",), tag="cbev")
            for fc in range(16):
                pk = state["pk"] % 2
                state["pk"] += 1
                fh, fcl = fc // 8, fc % 8
                for dc in range(NCH):
                    o_("tensor",
                       lambda e, p0=pk, h0=fh, f0=fcl, d0=dc, x2=xga,
                              st=(dc == 0), sp=(dc == NCH - 1):
                           e.matmul(P[p0][:, :],
                                    V_AR[:, h0 * 8192 + d0 * 1024 + f0 * 128:
                                         h0 * 8192 + d0 * 1024 +
                                         f0 * 128 + 128],
                                    x2[:, d0 * 512:(d0 + 1) * 512],
                                    start=st, stop=sp),
                       reads=(f"W1F{fh}", xgr), writes=(f"P{pk}",),
                       tag="hmm")
                o_("scalar",
                   lambda e, p0=pk, f0=fc:
                       e.activation(Hap(f0), P[p0][:, :], AF.Relu,
                                    bias=B1T[:, f0:f0 + 1]),
                   reads=(f"P{pk}", "B1T"), writes=(f"H{fc}",), tag="relu")
            for dc2 in range(NCH):
                pk = state["pk"] % 2
                state["pk"] += 1
                for fc in range(16):
                    o_("tensor",
                       lambda e, p0=pk, f0=fc, d0=dc2,
                              st=(fc == 0), sp=(fc == 15):
                           e.matmul(P[p0][:, :],
                                    W_AR[:, f0 * 1024 + d0 * 128:
                                         f0 * 1024 + d0 * 128 + 128],
                                    Hap(f0), start=st, stop=sp),
                       reads=("W2F", f"H{fc}"), writes=(f"P{pk}",), tag="ymm")
                o_("scalar",
                   lambda e, p0=pk, d0=dc2:
                       e.activation(K_AR[:, d0 * 512:(d0 + 1) * 512],
                                    P[p0][:, :], AF.Identity,
                                    bias=B1T[:, 16 + d0:17 + d0]),
                   reads=(f"P{pk}", "B1T"), writes=("YGT",), tag="yev")
                o_("vector",
                   lambda e, d0=dc2:
                       e.tensor_tensor(K_AR[:, d0 * 512:(d0 + 1) * 512],
                                       K_AR[:, d0 * 512:(d0 + 1) * 512],
                                       RB[:, :], op=ALU.mult),
                   reads=("RB", "YGT"), writes=("YGT",), tag="yscale")
            for sct in range(4):
                for w in range(8):
                    bank = PT7 if w % 2 == 0 else PT8
                    bres = "PT7" if w % 2 == 0 else "PT8"
                    wcol = (w // 2) * 128
                    o_("tensor",
                       lambda e, w0=w, s0=sct, k2=bank, c2=wcol:
                           e.transpose(k2[:, c2:c2 + 128],
                                       K_AR[:, w0 * 512 + s0 * 128:
                                            w0 * 512 + s0 * 128 + 128],
                                       IDENT[:, :]),
                       reads=("YGT", "IDENT"), writes=(bres,), tag="ygt")
                    o_("scalar",
                       lambda e, w0=w, s0=sct, k2=bank, c2=wcol:
                           e.copy(K_AR[:, 4096 + s0 * 1024 + w0 * 128:
                                       4096 + s0 * 1024 + w0 * 128 + 128],
                                  k2[:, c2:c2 + 128]),
                       reads=(bres,), writes=(f"YTK{sct}",), tag="yge")
                o_("sync",
                   lambda e, j0=j, s0=sct:
                       e.dma_start(out=yg[j0 * 512 + s0 * 128:
                                          j0 * 512 + s0 * 128 + 128, :],
                                   in_=K_AR[:, 4096 + s0 * 1024:
                                            4096 + (s0 + 1) * 1024]),
                   reads=(f"YTK{sct}",), writes=(f"Y{j}",), dma=True,
                   tag="ygs")

    def emit_final():
        o_("vector", lambda e: e.memset(SC16[:, 4:5], 0.0),
           writes=("XB0", "XB1", "XGT0", "XGT1", "rrow", "OUTS",
                   "RRBonesF", "IROW", "XF0", "XF1", "XF2", "XF3"),
           tag="fence3")
        FSQ = ((SQT, ("SQT",)), (STG, ("STG0", "STG1")),
               (XGL, ("XGL",)), (PTB, ("PTB0", "PTB1")))

        def tile_ops(tt):
            ops = []

            def a_(eng, fn, reads=(), writes=(), dma=False, tag="f"):
                ops.append((eng, fn, tuple(reads), tuple(writes), dma, tag))

            q = tt % 4
            xo = q * 2048
            mb = q * 8
            sfx = str(q)
            sq, sqr = FSQ[q]
            xbr = f"XF{q}"

            def lncols(tag):
                a_("vector",
                   lambda e: e.tensor_scalar(MVC[:, mb + 2:mb + 3],
                                             MVC[:, mb:mb + 1],
                                             1.0 / D, None, op0=ALU.mult),
                   (f"MV0{sfx}",), (f"MV2{sfx}",), False, tag)
                a_("vector",
                   lambda e: e.tensor_scalar(MVC[:, mb + 3:mb + 4],
                                             MVC[:, mb + 1:mb + 2],
                                             1.0 / D, None, op0=ALU.mult),
                   (f"MV1{sfx}",), (f"MV3{sfx}",), False, tag)
                a_("vector",
                   lambda e: e.tensor_tensor(MVC[:, mb + 6:mb + 7],
                                             MVC[:, mb + 2:mb + 3],
                                             MVC[:, mb + 2:mb + 3],
                                             op=ALU.mult),
                   (f"MV2{sfx}",), (f"MV6{sfx}",), False, tag)
                a_("vector",
                   lambda e: e.tensor_tensor(MVC[:, mb + 3:mb + 4],
                                             MVC[:, mb + 3:mb + 4],
                                             MVC[:, mb + 6:mb + 7],
                                             op=ALU.subtract),
                   (f"MV6{sfx}", f"MV3{sfx}"), (f"MV3{sfx}",), False, tag)
                a_("vector",
                   lambda e: e.tensor_scalar_add(MVC[:, mb + 3:mb + 4],
                                                 MVC[:, mb + 3:mb + 4],
                                                 LN_EPS),
                   (f"MV3{sfx}",), (f"MV3{sfx}",), False, tag)
                a_("scalar",
                   lambda e: e.activation(MVC[:, mb + 3:mb + 4],
                                          MVC[:, mb + 3:mb + 4], AF.Ln),
                   (f"MV3{sfx}",), (f"MV3{sfx}",), False, tag)
                a_("scalar",
                   lambda e: e.activation(MVC[:, mb + 4:mb + 5],
                                          MVC[:, mb + 3:mb + 4], AF.Exp,
                                          scale=-0.5),
                   (f"MV3{sfx}",), (f"MV4{sfx}",), False, tag)
                a_("vector",
                   lambda e: e.tensor_tensor(MVC[:, mb + 5:mb + 6],
                                             MVC[:, mb + 2:mb + 3],
                                             MVC[:, mb + 4:mb + 5],
                                             op=ALU.mult),
                   (f"MV2{sfx}", f"MV4{sfx}"), (f"MV5{sfx}",), False, tag)
                a_("vector",
                   lambda e: e.tensor_scalar(MVC[:, mb + 5:mb + 6],
                                             MVC[:, mb + 5:mb + 6],
                                             -1.0, None, op0=ALU.mult),
                   (f"MV5{sfx}",), (f"MV5{sfx}",), False, tag)

            a_("gpsimd",
               lambda e: e.indirect_dma_start(
                   out=XB[:, xo:xo + 1024], out_offset=None, in_=yg,
                   in_offset=IndirectOffsetOnAxis(DESTC[:, tt:tt + 1], 0)),
               ("DESTC",) + tuple(f"Y{j2}" for j2 in range(NT)),
               (xbr,), True, "ygr")
            a_("sync",
               lambda e: e.dma_start(out=XB[:, xo + 1024:xo + 2048],
                                     in_=x2tok[tt * 128:(tt + 1) * 128,
                                               0:1024]),
               ("x2tok",), (xbr,), True, "f2ld")
            a_("vector",
               lambda e: e.tensor_tensor(XB[:, xo:xo + 1024],
                                         XB[:, xo:xo + 1024],
                                         XB[:, xo + 1024:xo + 2048],
                                         op=ALU.add),
               (xbr,), (xbr,), False, "r3")
            a_("sync",
               lambda e: e.dma_start(out=XB[:, xo + 1024:xo + 2048],
                                     in_=x1tok[tt * 128:(tt + 1) * 128, :]),
               ("x1tok", xbr), (xbr,), True, "f1ld")
            a_("vector",
               lambda e: e.reduce_sum(MVC[:, mb:mb + 1],
                                      XB[:, xo:xo + 1024], axis=AX.X),
               (xbr,), (f"MV0{sfx}",), False, "f3ln")
            a_("scalar",
               lambda e: e.activation(sq[:, 0:1024], XB[:, xo:xo + 1024],
                                      AF.Square),
               (xbr,), sqr, False, "f3sq")
            a_("vector",
               lambda e: e.reduce_sum(MVC[:, mb + 1:mb + 2], sq[:, 0:1024],
                                      axis=AX.X),
               sqr, (f"MV1{sfx}",), False, "f3ln")
            lncols("f3ln")
            a_("vector",
               lambda e: e.tensor_scalar(XB[:, xo:xo + 1024],
                                         XB[:, xo:xo + 1024],
                                         MVC[:, mb + 4:mb + 5],
                                         MVC[:, mb + 5:mb + 6],
                                         op0=ALU.mult, op1=ALU.add),
               (f"MV4{sfx}", f"MV5{sfx}"), (xbr,), False, "f3n1")
            a_("vector",
               lambda e: e.tensor_tensor(XB[:, xo:xo + 1024],
                                         XB[:, xo:xo + 1024],
                                         CMTX[:, 0:1024], op=ALU.mult),
               ("G2B2",), (xbr,), False, "f3n2")
            a_("vector",
               lambda e: e.tensor_tensor(XB[:, xo:xo + 1024],
                                         XB[:, xo:xo + 1024],
                                         CMTX[:, 1024:2048], op=ALU.add),
               ("G2B2",), (xbr,), False, "f3n3")
            a_("vector",
               lambda e: e.tensor_tensor(XB[:, xo:xo + 1024],
                                         XB[:, xo:xo + 1024],
                                         XB[:, xo + 1024:xo + 2048],
                                         op=ALU.add),
               (xbr,), (xbr,), False, "r4")
            a_("vector",
               lambda e: e.reduce_sum(MVC[:, mb:mb + 1],
                                      XB[:, xo:xo + 1024], axis=AX.X),
               (xbr,), (f"MV0{sfx}",), False, "f4ln")
            a_("scalar",
               lambda e: e.activation(sq[:, 0:1024], XB[:, xo:xo + 1024],
                                      AF.Square),
               (xbr,), sqr, False, "f4sq")
            a_("vector",
               lambda e: e.reduce_sum(MVC[:, mb + 1:mb + 2], sq[:, 0:1024],
                                      axis=AX.X),
               sqr, (f"MV1{sfx}",), False, "f4ln")
            lncols("f4ln")
            a_("vector",
               lambda e: e.tensor_scalar(OUTS[:, :], XB[:, xo:xo + 1024],
                                         MVC[:, mb + 4:mb + 5],
                                         MVC[:, mb + 5:mb + 6],
                                         op0=ALU.mult, op1=ALU.add),
               (f"MV4{sfx}", f"MV5{sfx}", xbr), ("OUTS",), False, "fnorm")
            a_("sync",
               lambda e: e.dma_start(out=out[tt * 128:(tt + 1) * 128, :],
                                     in_=OUTS[:, :]),
               ("OUTS",), ("out",), True, "ostore")
            return ops

        for t0 in range(0, 32, 4):
            tls = [tile_ops(t0 + i) for i in range(4)]
            emit_rr(*[tl[:-2] for tl in tls])
            for tl in tls:
                emit_seq(tl[-2:])

    if upto != "a0":
        attention(0, x0sp, "x0sp", x1sp, "x1_out", (qw, 0), (kw, 0), (vw, 0),
                  (ow, 0), alibi=True, causal=True, bias_qkv=False,
                  bias_o=False, gb=None, routing=False,
                  stop_after=upto if upto in ("kv", "qloop") else None,
                  tokspill=(x1tok, "x1tok", False))
    if upto in ("x2", "all"):
        o_("tensor", lambda e: e.matmul(P[5][:, 0:12], ones128, brt_row,
                                        start=True, stop=True),
           reads=("ROWC",), writes=("P5",), tag="brt")
        o_("scalar", lambda e: e.copy(BRB[:, :], P[5][:, 0:12]),
           reads=("P5",), writes=("BRB",), tag="brt")
        x2o = x2sp if upto == "x2" else None
        attention(1, x1sp, "x1_out", x2o, "x2_out", (wiT, 0), (wiT, 1024),
                  (wiT, 2048), (woT, 0), alibi=False, causal=False,
                  bias_qkv=True, bias_o=True, gb=(0, 8), routing=True,
                  tokspill=(x2tok, "x2tok", True))
    if upto == "all":
        emit_moe()
        emit_final()
        sink_reads = ("out", "x1_out")
    elif upto == "x2":
        sink_reads = ("x1_out", "x2_out", "x1tok", "x2tok")
    elif upto == "a0":
        sink_reads = ("x0sp",)
    elif upto == "kv":
        sink_reads = ("qsp", "x0sp", "K_AR")
    elif upto == "qloop":
        sink_reads = ("osp", "qsp", "x0sp")
    else:
        sink_reads = ("x1_out",)
    o_("sync", lambda e: e.nop(), reads=sink_reads, tag="sink")

    bld.emit(nc)
    es.close()
    return nc, bld


def host_inputs(inputs, core):
    f32 = lambda a: np.ascontiguousarray(np.asarray(a, dtype=np.float32))
    bf16 = lambda a: np.ascontiguousarray(np.asarray(a).astype(ml_dtypes.bfloat16))
    x = f32(inputs["L0_attn_x"])[core]
    sl = _slopes()
    s8 = np.array([np.float32(ml_dtypes.bfloat16(-8.0 * s)) for s in sl],
                  dtype=np.float32)
    slope_eff = (-s8 / 8.0).astype(np.float64)

    d = {"x0": x}
    d["qw"] = f32(inputs["L0_attn_q_w"])
    d["kw"] = f32(inputs["L0_attn_k_w"])
    d["vw"] = f32(inputs["L0_attn_v_w"])
    d["ow"] = f32(inputs["L0_attn_o_w"])
    d["wiT"] = f32(np.asarray(inputs["L0_ffn_in_proj_w"], np.float32).T)
    d["woT"] = f32(np.asarray(inputs["L0_ffn_out_proj_w"], np.float32).T)
    bi = f32(inputs["L0_ffn_in_proj_b"])
    d["bi_cols"] = f32(bi.reshape(24, 128).T)
    bo = f32(inputs["L0_ffn_out_proj_b"])
    d["bo_cols"] = f32(bo.reshape(8, 128).T)
    rows32 = np.zeros((1, 3688), np.float32)
    rows32[0, 0:512] = 1.0
    bg = f32(inputs["L0_ffn_b_group"])
    be = f32(inputs["L0_ffn_b_expert"]).reshape(-1)
    rows32[0, 2048:2052] = bg
    rows32[0, 2052:2060] = be
    rows32[0, 3600:3616] = np.arange(16, dtype=np.float32)
    d["rows32_c"] = rows32
    gbc = np.zeros((128, 32), np.float32)
    gbc[:, 0:8] = f32(inputs["L0_ffn_ln1_g"]).reshape(8, 128).T
    gbc[:, 8:16] = f32(inputs["L0_ffn_ln1_b"]).reshape(8, 128).T
    gbc[:, 16:24] = f32(inputs["L0_ffn_ln2_g"]).reshape(8, 128).T
    gbc[:, 24:32] = f32(inputs["L0_ffn_ln2_b"]).reshape(8, 128).T
    d["gbc_c"] = gbc
    d["bvh_c"] = np.ascontiguousarray(bi[2048:3072].reshape(16, 64).T)
    wg = f32(inputs["L0_ffn_w_group"])
    we = f32(inputs["L0_ffn_w_expert"])
    d["wroute"] = np.ascontiguousarray(
        np.concatenate([wg, we.transpose(1, 0, 2).reshape(D, 8)], axis=1))
    w1a = f32(inputs["L0_ffn_w1"])
    d["w1"] = np.ascontiguousarray(
        w1a.reshape(NE, 8, 128, 2, 1024).transpose(0, 2, 3, 1, 4)
        .reshape(NE * 128, 16384))
    w2a = f32(inputs["L0_ffn_w2"])
    d["w2"] = np.ascontiguousarray(
        w2a.reshape(NE, 16, 128, 1024).transpose(0, 2, 1, 3)
        .reshape(NE * 128, 16384))
    b1 = f32(inputs["L0_ffn_b1"])
    b2a = f32(inputs["L0_ffn_b2"])
    b12 = np.zeros((NE, 128, 24), np.float32)
    b12[:, :, 0:16] = b1.reshape(NE, 16, 128).transpose(0, 2, 1)
    b12[:, :, 16:24] = b2a.reshape(NE, 8, 128).transpose(0, 2, 1)
    d["b12r"] = np.ascontiguousarray(b12.reshape(NE * 128, 24))
    gb2 = np.zeros((128, 2048), np.float32)
    gb2[:, 0:1024] = f32(inputs["L0_ffn_ln2_g"]).reshape(-1)[None, :]
    gb2[:, 1024:2048] = f32(inputs["L0_ffn_ln2_b"]).reshape(-1)[None, :]
    d["g2b2b"] = gb2
    d["iotac"] = np.arange(128, dtype=np.float32).reshape(128, 1)
    d["identf_c"] = np.eye(128, dtype=np.float32)
    d["ident_c"] = np.eye(128, dtype=ml_dtypes.bfloat16)
    o16 = np.zeros((128, 128), np.float32)
    for jc in range(8):
        for r in range(128):
            o16[r, jc * 16 + 2 * jc + (r // 64)] = 1.0
    d["ones16_c"] = bf16(o16)
    d["one16c"] = bf16(np.ones((16, 128), np.float32))
    ex16 = np.zeros((16, 576), np.float32)
    for h in range(16):
        ex16[h, h * 36:(h + 1) * 36] = 1.0
    d["exp16_c"] = bf16(ex16)
    hcc = np.zeros((2, 128, 576), np.float32)
    ii = np.arange(128, dtype=np.float64)
    for h in range(16):
        for di in range(36):
            delta = di - 32
            hcc[0, :, h * 36 + di] = (slope_eff[h] * (ii + 128.0 * delta)
                                      ).astype(np.float32)
    d["hcc"] = hcc
    sl2 = np.zeros((NH, 2, S), np.float32)
    for h in range(16):
        sl2[h, :, :] = s8[h]
    d["sl2"] = bf16(sl2)
    j = np.arange(512, dtype=np.float32)
    qj = np.stack([16.0 * np.floor(j / 16.0), j - 16.0 * np.floor(j / 16.0)])
    d["qj_c"] = bf16(qj)
    mk = np.zeros((128, 896), np.float32)
    im = np.arange(128)[:, None]
    mm = np.arange(896)[None, :]
    mk[mm < im + 384] = NEG
    d["maskneg_c"] = mk
    return d


_NC_CACHE = {}


def kernel(**inputs):
    if "nc" not in _NC_CACHE:
        _NC_CACHE["nc"] = build_program("all")
    nc, _ = _NC_CACHE["nc"]
    in_maps = [host_inputs(inputs, c) for c in range(8)]
    res = run_bass_kernel_spmd(nc, in_maps, core_ids=list(range(8)))
    outs = [np.asarray(r["out"], dtype=np.float32) for r in res.results]
    return np.stack(outs, axis=0)


if __name__ == "__main__":
    nc, bld = build_program("all")
    from collections import Counter
    cnt = Counter(o.eng for o in bld.ops)
    print("ops per engine:", dict(cnt), "total", len(bld.ops))
    print("sem domains:", bld.domains)
    print("sem finals:", bld.counters)


# version 16
# speedup vs baseline: 1.0583x; 1.0035x over previous
import numpy as np
from contextlib import ExitStack

import concourse.bass as bass
import concourse.mybir as mybir
from concourse.bass_utils import run_bass_kernel_spmd
from concourse.bass_core_rust import IndirectOffsetOnAxis
import ml_dtypes

F32 = mybir.dt.float32
BF16 = mybir.dt.bfloat16
AF = mybir.ActivationFunctionType
ALU = mybir.AluOpType
AX = mybir.AxisListType

S = 4096
D = 1024
NB = 8
TB = 512
NCH = 8
NH = 16
FF = 2048
NE = 8
NEG = -1.0e30
LN_EPS = 1e-5

ENGINES = ("tensor", "vector", "scalar", "gpsimd", "sync")


class Op:
    __slots__ = ("eng", "fn", "reads", "writes", "dma", "tag", "deps",
                 "inc_idx", "waits", "sub")

    def __init__(self, eng, fn, reads, writes, dma, tag):
        self.eng = eng
        self.fn = fn
        self.reads = tuple(reads)
        self.writes = tuple(writes)
        self.dma = dma
        self.tag = tag


class Builder:
    def __init__(self):
        self.ops = []

    def op(self, eng, fn, reads=(), writes=(), dma=False, tag=""):
        self.ops.append(Op(eng, fn, reads, writes, dma, tag))

    KSUB = 6

    def resolve(self):
        dma_rank = {}
        for i, o in enumerate(self.ops):
            if o.dma:
                r = dma_rank.get(o.eng, 0)
                dma_rank[o.eng] = r + 1
                o.sub = r % self.KSUB
                o.inc_idx = r // self.KSUB + 1
            else:
                o.sub = 0

        def dom_of(o):
            if o.dma:
                return (o.eng, "d", o.sub)
            return (o.eng, "c", 0)

        last_w = {}
        readers = {}
        for i, o in enumerate(self.ops):
            deps = set()
            for r in o.reads:
                if r in last_w:
                    deps.add(last_w[r])
            for w in o.writes:
                if w in last_w:
                    deps.add(last_w[w])
                deps.update(readers.get(w, ()))
            for w in o.writes:
                last_w[w] = i
                readers[w] = []
            for r in o.reads:
                readers.setdefault(r, []).append(i)
            byd = {}
            for j in deps:
                p = self.ops[j]
                if (p.eng == o.eng and not p.dma and not o.dma
                        and o.eng == "tensor"):
                    continue
                dom = dom_of(p)
                if j > byd.get(dom, -1):
                    byd[dom] = j
            o.deps = set(byd.values())
        need_inc = set()
        for i, o in enumerate(self.ops):
            need_inc |= o.deps
        counters = {}
        for i, o in enumerate(self.ops):
            dom = dom_of(o)
            if o.dma:
                counters[dom] = o.inc_idx
            elif i in need_inc:
                counters[dom] = counters.get(dom, 0) + 1
                o.inc_idx = counters[dom]
            else:
                o.inc_idx = None
        emitted = {}
        for o in self.ops:
            waits = {}
            if o.dma and o.inc_idx > 1:
                waits[dom_of(o)] = (o.inc_idx - 1) * 16
            for j in o.deps:
                p = self.ops[j]
                dom = dom_of(p)
                v = p.inc_idx * (16 if p.dma else 1)
                if v > waits.get(dom, 0):
                    waits[dom] = v
            em = emitted.setdefault(o.eng, {})
            fin = []
            for dom, v in sorted(waits.items(), key=lambda kv: str(kv[0])):
                if v > em.get(dom, 0):
                    em[dom] = v
                    fin.append((dom, v))
            o.waits = fin
        self.domains = sorted(counters, key=str)
        self.counters = counters
        self._dom_of = dom_of

    def emit(self, nc):
        self.resolve()
        es = ExitStack()
        sems = {dom: es.enter_context(
                    nc.semaphore(f"sm_{dom[0][:2]}{dom[1]}{dom[2]}"))
                for dom in self.domains}
        by_eng = {e: [] for e in ENGINES}
        for o in self.ops:
            by_eng[o.eng].append(o)
        dom_of = self._dom_of

        def runner(lst):
            def body(eng):
                for o in lst:
                    for dom, v in o.waits:
                        eng.wait_ge(sems[dom], v)
                    ins = o.fn(eng)
                    if o.inc_idx is not None:
                        ins.then_inc(sems[dom_of(o)], 16 if o.dma else 1)
            return body

        block = es.enter_context(nc.Block())
        with es:
            block.tensor(runner(by_eng["tensor"]))
            block.vector(runner(by_eng["vector"]))
            block.scalar(runner(by_eng["scalar"]))
            block.gpsimd(runner(by_eng["gpsimd"]))
            block.sync(runner(by_eng["sync"]))


def _slopes():
    ratio = 2.0 ** (-8.0 / NH)
    return np.array([ratio ** (h + 1) for h in range(NH)], dtype=np.float64)


def build_program(upto="all"):
    nc = bass.Bass("TRN2", target_bir_lowering=False, debug=False)

    def inp(name, shape, dt=F32):
        return nc.dram_tensor(name, list(shape), dt, kind="ExternalInput").ap()

    x0 = inp("x0", (S, D))
    qw = inp("qw", (D, D))
    kw = inp("kw", (D, D))
    vw = inp("vw", (D, D))
    ow = inp("ow", (D, D))
    wiT = inp("wiT", (D, 3 * D))
    woT = inp("woT", (D, D))
    bi_cols = inp("bi_cols", (128, 24))
    bo_cols = inp("bo_cols", (128, 8))
    rows32_c = inp("rows32_c", (1, 3688))
    gbc_c = inp("gbc_c", (128, 32))
    bvh_c = inp("bvh_c", (64, 16))
    wroute = inp("wroute", (D, 12))
    w1 = inp("w1", (NE * 128, 16384))
    w2 = inp("w2", (NE * 128, 16384))
    b12r = inp("b12r", (NE * 128, 24))
    g2b2b = inp("g2b2b", (128, 2 * D))
    iotac = inp("iotac", (128, 1))
    ident_c = inp("ident_c", (128, 128), BF16)
    identf_c = inp("identf_c", (128, 128))
    ones16_c = inp("ones16_c", (128, 128), BF16)
    one16c = inp("one16c", (16, 128), BF16)
    exp16_c = inp("exp16_c", (16, 576), BF16)
    hcc = inp("hcc", (2, 128, 576))
    sl2 = inp("sl2", (NH, 2, S), BF16)
    qj_c = inp("qj_c", (2, 512), BF16)
    maskneg_c = inp("maskneg_c", (128, 896))

    out = nc.dram_tensor("out", [S, D], F32, kind="ExternalOutput").ap()
    x1sp = nc.dram_tensor("x1sp", [D, S], BF16, kind="ExternalOutput").ap()
    x2sp = nc.dram_tensor("x2sp", [D, S], BF16, kind="ExternalOutput").ap()
    w1b0 = nc.dram_tensor("w1b0", [NE * 128, 8192], BF16, kind="Internal").ap()
    w1b1 = nc.dram_tensor("w1b1", [NE * 128, 8192], BF16, kind="Internal").ap()
    w2b = nc.dram_tensor("w2b", [NE * 128, 16384], BF16, kind="Internal").ap()
    x1tok = nc.dram_tensor("x1tok", [S, D], BF16, kind="Internal").ap()
    x2tok = nc.dram_tensor("x2tok", [S, 1040], BF16, kind="Internal").ap()
    xg = nc.dram_tensor("xg", [15 * 512, 1040], BF16, kind="Internal").ap()
    yg = nc.dram_tensor("yg", [15 * 512, D], BF16, kind="Internal").ap()
    ib1 = nc.dram_tensor("ib1", [1, S], mybir.dt.int32, kind="Internal").ap()
    dbg_kind = "ExternalOutput" if upto in ("a0", "kv", "qloop") else "Internal"
    x0sp = nc.dram_tensor("x0sp", [D, S], BF16, kind=dbg_kind).ap()
    qsp = nc.dram_tensor("qsp", [NH, 64, S], BF16, kind=dbg_kind).ap()
    osp = nc.dram_tensor("osp", [D, S], BF16, kind=dbg_kind).ap()

    es = ExitStack()

    def sbuf(name, shape, dt=BF16):
        return es.enter_context(nc.sbuf_tensor(name, list(shape), dt))

    K_AR = sbuf("K_AR", (128, 8 * S))
    V_AR = sbuf("V_AR", (128, 32 * 520))
    XB = sbuf("XB", (128, 2 * 4096))
    W_AR = sbuf("W_AR", (128, 4 * 4096))
    QS = sbuf("QS", (128, 2 * 512))
    PTB = sbuf("PTB", (128, 4 * 512))
    STG = sbuf("STG", (128, 1040))
    SQ = sbuf("SQ", (128, 2 * 512))
    RB = sbuf("RB", (128, 512))
    OTS = sbuf("OTS", (128, 2 * 512))
    RLN = sbuf("RLN", (128, 8 * 512))
    CMB = sbuf("CMB", (128, 32 * 8))
    CMTX = sbuf("CMTX", (128, 2048))
    AGB = sbuf("AGB", (128, 1024))
    BRB = sbuf("BRB", (128, 12), F32)
    IDENT = sbuf("IDENT", (128, 128))
    ONES16 = sbuf("ONES16", (128, 128))
    ONE16 = sbuf("ONE16", (16, 128))
    EXP16 = sbuf("EXP16", (16, 576))
    ONESB = sbuf("ONESB", (128, 2))
    ALIBB = sbuf("ALIBB", (128, 576), F32)
    MASKN = sbuf("MASKN", (128, 896), F32)
    ROWS32 = sbuf("ROWS32", (128, 3688), F32)
    GBC = sbuf("GBC", (128, 32), F32)
    BVH = sbuf("BVH", (64, 16), F32)
    BCT = sbuf("BCT", (16, 128))
    BIC = sbuf("BIC", (128, 24), F32)
    BOC = sbuf("BOC", (128, 8), F32)
    WRT = sbuf("WRT", (128, 8 * 12))
    RTS = sbuf("RTS", (128, 128), F32)
    SC16 = sbuf("SC16", (16, 8), F32)
    OUTS = sbuf("OUTS", (128, 1024), F32)
    IDENT32 = sbuf("IDENT32", (128, 128), F32)
    DESTC = sbuf("DESTC", (128, 32), mybir.dt.int32)
    EIDF = sbuf("EIDF", (128, 16), F32)
    EIDCI = sbuf("EIDCI", (128, 16), mybir.dt.int32)
    IOCF = sbuf("IOCF", (128, 1), F32)
    B1T = sbuf("B1T", (128, 24), F32)
    BASEC = sbuf("BASEC", (128, 1), F32)
    ONES8F = sbuf("ONES8F", (8, 4), F32)
    XGL = sbuf("XGL", (128, 1040))
    SQT = sbuf("SQT", (128, 1040))
    MVC = sbuf("MVC", (128, 32), F32)
    WCTS = sbuf("WCTS", (128, 32), F32)
    PCS = sbuf("PCS", (128, 1024))
    V32 = V_AR[:, 0:16640].bitcast(F32)
    ONE8 = V32[0:8, 0:S]
    CUM8 = V32[0:8, S:2 * S]

    P = [es.enter_context(nc.psum_tensor(f"P{i}", [128, 512], F32))
         for i in range(6)]
    PT7 = es.enter_context(nc.psum_tensor("PT7", [128, 1024], BF16))
    PT8 = es.enter_context(nc.psum_tensor("PT8", [128, 1024], BF16))

    ones512_row = ROWS32[32:33, 0:512]
    ones128 = ROWS32[32:33, 0:128]
    ones64_row = ROWS32[32:33, 0:64]
    arow_r = ROWS32[32:33, 512:1024]
    brow_r = ROWS32[32:33, 1024:1536]
    rrow_r = ROWS32[32:33, 1536:2048]
    brt_row = ROWS32[32:33, 2048:2060]
    mean_r = ROWS32[32:33, 2064:2576]
    m2_r = ROWS32[32:33, 2576:3088]
    var_r = ROWS32[32:33, 3088:3600]
    iot16_r = ROWS32[32:33, 3600:3616]
    one1_r = ROWS32[32:33, 0:1]
    CMTR = K_AR[0:8, 8192:12288]

    def mr(a, b):
        return ROWS32[32:33, 3616 + a:3616 + b]
    IRV = OUTS[:, 512:1024].bitcast(mybir.dt.int32)

    bld = Builder()
    _sink = [bld.op]

    def o_(eng, fn, reads=(), writes=(), dma=False, tag=""):
        _sink[0](eng, fn, reads=reads, writes=writes, dma=dma, tag=tag)

    def collect_ops(body):
        lst = []

        def cap(eng, fn, reads=(), writes=(), dma=False, tag=""):
            lst.append((eng, fn, tuple(reads), tuple(writes), dma, tag))
        _sink[0] = cap
        try:
            body()
        finally:
            _sink[0] = bld.op
        return lst

    def emit_rr(*lists):
        for i in range(max(len(l) for l in lists)):
            for lst in lists:
                if i < len(lst):
                    eng, fn, rds, wrs, dma, tag = lst[i]
                    bld.op(eng, fn, reads=rds, writes=wrs, dma=dma, tag=tag)

    def emit_seq(lst):
        for eng, fn, rds, wrs, dma, tag in lst:
            bld.op(eng, fn, reads=rds, writes=wrs, dma=dma, tag=tag)
    state = {"pk": 0, "w": 0}
    SRING = [0, 1, 4]
    _sl = _slopes()
    _s8 = np.array([np.float32(ml_dtypes.bfloat16(-8.0 * x)) for x in _sl])
    slope_eff_py = (-_s8 / 8.0).astype(np.float64)

    def wslot():
        s_ = state["w"]
        state["w"] = (s_ + 1) % 4
        return s_

    def cload(dst, src, res, eng="sync"):
        o_(eng, lambda e, a=dst, b_=src: e.dma_start(out=a, in_=b_),
           writes=(res,), dma=True, tag="cload")

    cload(IDENT[:, :], ident_c, "IDENT")
    cload(IDENT32[:, :], identf_c, "IDENT32")
    cload(ONES16[:, :], ones16_c, "ONES16")
    cload(ONE16[:, :], one16c, "ONE16")
    cload(EXP16[:, :], exp16_c, "EXP16")
    cload(MASKN[:, :], maskneg_c, "MASKN")
    cload(ROWS32[32:33, :], rows32_c, "ROWC")
    cload(GBC[:, :], gbc_c, "GBC")
    cload(BVH[:, :], bvh_c, "BVH")
    cload(BIC[:, :], bi_cols, "BIC")
    cload(BOC[:, :], bo_cols, "BOC")
    cload(IOCF[:, :], iotac, "IOCF")
    cload(CMTX[:, 0:2048], g2b2b, "G2B2", eng="gpsimd")
    cload(WRT[:, :].rearrange("p (c j) -> p c j", j=12),
          wroute.rearrange("(c p) j -> p c j", p=128), "WRT", eng="gpsimd")
    cload(QS[64:66, 0:512], qj_c, "QSj0")
    cload(QS[64:66, 512:1024], qj_c, "QSj1")
    o_("vector", lambda e: e.memset(ONESB[:, :], 1.0), writes=("ONESB",))
    o_("vector", lambda e: e.memset(OUTS[64:65, 512:576], 1.0),
       writes=("RRBonesF",))
    o_("vector", lambda e: e.memset(OTS[64:65, 512:640], 1.0),
       writes=("RRBones",))
    va_ones = V_AR[:, :].rearrange("p (t h x) -> p t h x", h=8, x=65)[:, :, :, 64:65]
    o_("vector", lambda e: e.memset(va_ones, 1.0), writes=("VAones",))

    x0r = x0.rearrange("(n p) d -> n p d", p=128)
    for tt in range(32):
        sl = tt % 2
        tok = XB[:, sl * 4096:sl * 4096 + 1024]
        o_("gpsimd", lambda e, a=tok, b_=x0r[tt, :, :]: e.dma_start(out=a, in_=b_),
           writes=(f"XB{sl}",), dma=True, tag="a0ld")
        for dc in range(NCH):
            bank = PT7 if dc % 2 == 0 else PT8
            bres = "PT7" if dc % 2 == 0 else "PT8"
            w0 = (dc // 2) * 128
            o_("tensor",
               lambda e, o2=bank[:, w0:w0 + 128],
                      i2=tok[:, dc * 128:(dc + 1) * 128]:
                   e.transpose(o2, i2, IDENT[:, :]),
               reads=(f"XB{sl}", "IDENT"), writes=(bres,), tag="a0tp")
            o_("scalar",
               lambda e, o2=K_AR[:, dc * S + tt * 128:dc * S + tt * 128 + 128],
                      i2=bank[:, w0:w0 + 128]:
                   e.copy(o2, i2),
               reads=(bres,), writes=("K_AR",), tag="a0ev")
    for dc in range(NCH):
        o_("sync",
           lambda e, a=x0sp[dc * 128:(dc + 1) * 128, :],
                  b_=K_AR[:, dc * S:(dc + 1) * S]:
               e.dma_start(out=a, in_=b_),
           reads=("K_AR",), writes=("x0sp",), dma=True, tag="a0sp")

    def load_weight(dram_ap, c0, c1, slot, tag="wld"):
        n = c1 - c0
        src = dram_ap.rearrange("(c p) j -> p c j", p=128)[:, :, c0:c1]
        dst = W_AR[:, slot * 4096:slot * 4096 + 8 * n].rearrange(
            "p (c j) -> p c j", j=n)
        o_("gpsimd", lambda e, a=dst, b_=src: e.dma_start(out=a, in_=b_),
           writes=(f"W{slot}",), dma=True, tag=tag)

    def load_xblock(xsp, xres, b_, slot, tag="xld"):
        src = xsp.rearrange("(c p) t -> p c t", p=128)[:, :, b_ * TB:(b_ + 1) * TB]
        dst = XB[:, slot * 4096:(slot + 1) * 4096].rearrange(
            "p (c t) -> p c t", t=TB)
        o_("sync", lambda e, a=dst, b2=src: e.dma_start(out=a, in_=b2),
           reads=(xres,), writes=(f"XB{slot}",), dma=True, tag=tag)

    PRE = []
    PRE.append(("vector", lambda e: e.memset(XGL[:, :], 0.0),
                (), ("XGL",), False, "xgz"))
    _zq = [("sync",
            lambda e, z0=zt: e.dma_start(out=xg[z0 * 128:(z0 + 1) * 128, :],
                                         in_=XGL[:, :]),
            ("XGL",), ("xg",), True, "xgz") for zt in range(60)]
    _cq = []
    for _srcw, _w, _tag in ((w1, None, "w1"), (w2, w2b, "w2b")):
        for _ei in range(NE):
            for _q in range(16):
                if _tag == "w1":
                    _dst, _dres = (w1b0, "w1b0") if _q < 8 else (w1b1, "w1b1")
                    _qq = _q % 8
                else:
                    _dst, _dres, _qq = _w, _tag, _q
                _cq.append(("gpsimd",
                            lambda e, s0=_srcw, e0=_ei, q0=_q: e.dma_start(
                                out=PCS[:, :],
                                in_=s0[e0 * 128:(e0 + 1) * 128,
                                       q0 * 1024:(q0 + 1) * 1024]),
                            (), ("PCS",), True, "pcin"))
                _cq.append(("sync",
                            lambda e, d2=_dst, e0=_ei, q0=_qq: e.dma_start(
                                out=d2[e0 * 128:(e0 + 1) * 128,
                                       q0 * 1024:(q0 + 1) * 1024],
                                in_=PCS[:, :]),
                            ("PCS",), (_dres,), True, "pcout"))
    while _zq or _cq:
        if _zq:
            PRE.append(_zq.pop(0))
        for _ in range(9):
            if _cq:
                PRE.append(_cq.pop(0))

    def drain_pre(n):
        for _ in range(n):
            if not PRE:
                return
            eng, fn, reads, writes, dma, tag = PRE.pop(0)
            o_(eng, fn, reads=reads, writes=writes, dma=dma, tag=tag)

    def ln_rows(supb, supr, sqpb, sqpr, tag="ln"):
        o_("vector", lambda e: e.tensor_scalar_mul(mean_r, supb[0:1, :], 1.0 / D),
           reads=(supr,), writes=("mean",), tag=tag)
        o_("vector", lambda e: e.tensor_scalar_mul(m2_r, sqpb[64:65, :], 1.0 / D),
           reads=(sqpr,), writes=("m2",), tag=tag)
        o_("vector", lambda e: e.tensor_tensor(var_r, mean_r, mean_r, op=ALU.mult),
           reads=("mean",), writes=("var",), tag=tag)
        o_("vector", lambda e: e.tensor_tensor(var_r, m2_r, var_r, op=ALU.subtract),
           reads=("m2", "var"), writes=("var",), tag=tag)
        o_("vector", lambda e: e.tensor_scalar_add(var_r, var_r, LN_EPS),
           reads=("var",), writes=("var",), tag=tag)
        o_("scalar", lambda e: e.activation(var_r, var_r, AF.Ln),
           reads=("var",), writes=("var",), tag=tag)
        o_("scalar", lambda e: e.activation(arow_r, var_r, AF.Exp, scale=-0.5),
           reads=("var",), writes=("arow",), tag=tag)
        o_("vector", lambda e: e.tensor_tensor(brow_r, mean_r, arow_r, op=ALU.mult),
           reads=("mean", "arow"), writes=("brow",), tag=tag)
        o_("vector", lambda e: e.tensor_scalar_mul(brow_r, brow_r, -1.0),
           reads=("brow",), writes=("brow",), tag=tag)

    def emit_ln_norm(b_, RSRC, rfmt, gb, xout, xoutres,
                     agb, agrA, agrB, supb, supr, sqpb, sqpr):
        o_("tensor",
           lambda e: e.matmul(supb[:, 0:512], ones128, arow_r,
                              start=True, stop=True),
           reads=("arow", "ROWC"), writes=(supr,), tag="lnag")
        o_("scalar", lambda e: e.copy(agb[:, 0:512], supb[:, 0:512]),
           reads=(supr,), writes=(agrA,), tag="lnev")
        o_("tensor",
           lambda e: e.matmul(sqpb[:, 0:512], ones128, brow_r,
                              start=True, stop=True),
           reads=("brow", "ROWC"), writes=(sqpr,), tag="lnbg")
        o_("scalar", lambda e: e.copy(agb[:, 512:1024], sqpb[:, 0:512]),
           reads=(sqpr,), writes=(agrB,), tag="lnev")
        for dc in range(NCH):
            o_("vector",
               lambda e, d0=dc:
                   e.tensor_tensor(RSRC[:, d0 * 512:(d0 + 1) * 512],
                                   RSRC[:, d0 * 512:(d0 + 1) * 512],
                                   agb[:, 0:512], op=ALU.mult),
               reads=(agrA,), writes=(rfmt.format(dc),), tag="lnmul")
            o_("vector",
               lambda e, d0=dc:
                   e.tensor_tensor(RSRC[:, d0 * 512:(d0 + 1) * 512],
                                   RSRC[:, d0 * 512:(d0 + 1) * 512],
                                   agb[:, 512:1024], op=ALU.add),
               reads=(agrB,), writes=(rfmt.format(dc),), tag="lnadd")
            if gb is not None:
                go, bo_ = gb
                o_("vector",
                   lambda e, d0=dc, g0=go, b0=bo_:
                       e.tensor_scalar(RSRC[:, d0 * 512:(d0 + 1) * 512],
                                       RSRC[:, d0 * 512:(d0 + 1) * 512],
                                       GBC[:, g0 + d0:g0 + d0 + 1],
                                       GBC[:, b0 + d0:b0 + d0 + 1],
                                       op0=ALU.mult, op1=ALU.add),
                   reads=("GBC",), writes=(rfmt.format(dc),), tag="lngb")
        if xout is not None:
            src = RSRC[:, 0:4096].rearrange("p (c t) -> p c t", t=TB)
            dst = xout.rearrange("(c p) t -> p c t", p=128)[:, :, b_ * TB:(b_ + 1) * TB]
            o_("sync", lambda e, a=dst, b2=src: e.dma_start(out=a, in_=b2),
               reads=tuple(rfmt.format(dc) for dc in range(NCH)),
               writes=(xoutres,), dma=True, tag="xspill")

    def emit_routing(b_, tt4, rbk, rbr, rb0):
        tile = 4 * b_ + tt4
        r_ = lambda a, b2: RTS[:, rb0 + a:rb0 + b2]
        G12 = r_(0, 12); E12 = r_(12, 24)
        S4 = r_(24, 25); MX4 = r_(25, 26); R4 = r_(26, 27)
        GP = r_(27, 28); OH4 = r_(28, 32); CUM = r_(32, 33)
        TMP = r_(33, 34); PS4 = r_(36, 40); PM4 = r_(40, 44)
        IE4 = r_(44, 48); PM1 = r_(48, 49); PS1 = r_(49, 50)
        RPS = r_(50, 51); EP = r_(51, 52); WC = r_(52, 53)
        T4A = r_(56, 60); T4B = r_(60, 64)
        RT = f"RTS{rb0 // 64}"

        def v(fn, reads=(RT,), writes=(RT,)):
            o_("vector", fn, reads=reads, writes=writes, tag="rt")

        v(lambda e, t0=tt4: e.tensor_tensor(
            G12, rbk[:, t0 * 12:(t0 + 1) * 12], BRB[:, :], op=ALU.add),
          reads=(rbr, "BRB"))
        o_("scalar", lambda e: e.activation(E12, G12, AF.Exp),
           reads=(RT,), writes=(RT,), tag="rt")
        v(lambda e: e.reduce_sum(S4, E12[:, 0:4], axis=AX.X))
        v(lambda e: e.reduce_max(MX4, E12[:, 0:4], axis=AX.X))
        o_("scalar", lambda e: e.activation(R4, S4, AF.Ln),
           reads=(RT,), writes=(RT,), tag="rt")
        o_("scalar", lambda e: e.activation(R4, R4, AF.Exp, scale=-1.0),
           reads=(RT,), writes=(RT,), tag="rt")
        v(lambda e: e.tensor_tensor(GP, MX4, R4, op=ALU.mult))
        v(lambda e: e.tensor_scalar(OH4, E12[:, 0:4], MX4, None, op0=ALU.is_ge))
        v(lambda e: e.tensor_copy(CUM, OH4[:, 0:1]))
        for g in range(1, 4):
            v(lambda e: e.tensor_scalar(TMP, CUM, -1.0, 1.0,
                                        op0=ALU.mult, op1=ALU.add))
            v(lambda e, g_=g: e.tensor_tensor(OH4[:, g_:g_ + 1],
                                              OH4[:, g_:g_ + 1], TMP, op=ALU.mult))
            if g < 3:
                v(lambda e, g_=g: e.tensor_tensor(CUM, CUM, OH4[:, g_:g_ + 1],
                                                  op=ALU.add))
        Ev = E12[:, 4:12].rearrange("p (g two) -> p g two", two=2)
        v(lambda e: e.tensor_tensor(PS4, Ev[:, :, 0], Ev[:, :, 1], op=ALU.add))
        v(lambda e: e.tensor_tensor(PM4, Ev[:, :, 0], Ev[:, :, 1], op=ALU.max))
        v(lambda e: e.tensor_tensor(IE4, Ev[:, :, 0], Ev[:, :, 1], op=ALU.is_ge))
        v(lambda e: e.tensor_tensor(T4A, OH4, PM4, op=ALU.mult))
        v(lambda e: e.reduce_sum(PM1, T4A, axis=AX.X))
        v(lambda e: e.tensor_tensor(T4A, OH4, PS4, op=ALU.mult))
        v(lambda e: e.reduce_sum(PS1, T4A, axis=AX.X))
        o_("scalar", lambda e: e.activation(RPS, PS1, AF.Ln),
           reads=(RT,), writes=(RT,), tag="rt")
        o_("scalar", lambda e: e.activation(RPS, RPS, AF.Exp, scale=-1.0),
           reads=(RT,), writes=(RT,), tag="rt")
        v(lambda e: e.tensor_tensor(EP, PM1, RPS, op=ALU.mult))
        v(lambda e: e.tensor_tensor(WC, GP, EP, op=ALU.mult))
        cview = CMB[:, tile * 8:(tile + 1) * 8].rearrange("p (g two) -> p g two",
                                                          two=2)
        v(lambda e: e.tensor_tensor(T4A, OH4, IE4, op=ALU.mult))
        v(lambda e: e.tensor_scalar(T4A, T4A, WC, None, op0=ALU.mult))
        v(lambda e, cv=cview: e.tensor_copy(cv[:, :, 0], T4A),
          reads=(RT,), writes=("CMB",))
        v(lambda e: e.tensor_scalar(T4B, IE4, -1.0, 1.0, op0=ALU.mult, op1=ALU.add))
        v(lambda e: e.tensor_tensor(T4B, OH4, T4B, op=ALU.mult))
        v(lambda e: e.tensor_scalar(T4B, T4B, WC, None, op0=ALU.mult))
        v(lambda e, cv=cview: e.tensor_copy(cv[:, :, 1], T4B),
          reads=(RT,), writes=("CMB",))
        v(lambda e, ti=tile: e.tensor_copy(WCTS[:, ti:ti + 1], WC),
          reads=(RT,), writes=("WCTS",))

    def attention(ai, xsp, xres_in, xout, xoutres, WQd, WKd, WVd, WOd,
                  alibi, causal, bias_qkv, bias_o, gb, routing,
                  stop_after=None, tokspill=None):
        kext = 66 if alibi else 64
        pend_tail = [None]
        o_("sync", lambda e, a0=ai: e.dma_start(out=ALIBB[:, :], in_=hcc[a0]),
           writes=("ALIBB",), dma=True, tag="hcld")
        for half in (0, 1):
            wsl_k = wslot()
            load_weight(WKd[0], WKd[1] + half * 512, WKd[1] + half * 512 + 512,
                        wsl_k, "wk")
            wsl_v = wslot()
            load_weight(WVd[0], WVd[1] + half * 512, WVd[1] + half * 512 + 512,
                        wsl_v, "wv")
            if alibi:
                for hh in range(8):
                    h = 8 * half + hh
                    o_("sync",
                       lambda e, h0=h, t0=hh:
                           e.dma_start(out=K_AR[64:66, t0 * S:(t0 + 1) * S],
                                       in_=sl2[h0]),
                       writes=("K_AR",), dma=True, tag="sl2")
            for b_ in range(NB):
                xsl = b_ % 2
                load_xblock(xsp, xres_in, b_, xsl)
                for jjl in range(4):
                    jc = 4 * half + jjl
                    pk = state["pk"] % 2
                    state["pk"] += 1
                    for dc in range(NCH):
                        o_("tensor",
                           lambda e, p0=pk, w0=wsl_k, j0=jjl, d0=dc, x0_=xsl,
                                  st=(dc == 0), sp=(dc == NCH - 1):
                               e.matmul(P[p0][:, :],
                                        W_AR[:, w0 * 4096 + d0 * 512 + j0 * 128:
                                             w0 * 4096 + d0 * 512 + j0 * 128 + 128],
                                        XB[:, x0_ * 4096 + d0 * 512:
                                           x0_ * 4096 + (d0 + 1) * 512],
                                        start=st, stop=sp),
                           reads=(f"W{wsl_k}", f"XB{xsl}"), writes=(f"P{pk}",),
                           tag="kproj")
                    ssl = jjl % 2
                    if bias_qkv:
                        o_("scalar",
                           lambda e, p0=pk, s0=ssl, j0=jc:
                               e.activation(STG[:, s0 * 512:(s0 + 1) * 512],
                                            P[p0][:, :], AF.Identity,
                                            bias=BIC[:, 8 + j0:9 + j0]),
                           reads=(f"P{pk}", "BIC"), writes=(f"STG{ssl}",), tag="kev")
                    else:
                        o_("scalar",
                           lambda e, p0=pk, s0=ssl:
                               e.copy(STG[:, s0 * 512:(s0 + 1) * 512], P[p0][:, :]),
                           reads=(f"P{pk}",), writes=(f"STG{ssl}",), tag="kev")
                    o_("scalar",
                       lambda e, s0=ssl: e.activation(
                           SQ[:, s0 * 512:(s0 + 1) * 512],
                           STG[:, s0 * 512:(s0 + 1) * 512], AF.Square),
                       reads=(f"STG{ssl}",), writes=(f"SQ{ssl}",), tag="ksq")
                    o_("tensor",
                       lambda e, j0=jc, s0=ssl, st=(jjl == 0), sp=(jjl == 3):
                           e.matmul(P[4][0:16, :], ONES16[:, j0 * 16:(j0 + 1) * 16],
                                    SQ[:, s0 * 512:(s0 + 1) * 512],
                                    start=st, stop=sp),
                       reads=(f"SQ{ssl}", "ONES16"), writes=("P4",), tag="kstat")
                    he, ho = 2 * jjl, 2 * jjl + 1
                    if alibi:
                        o_("vector",
                           lambda e, h0=he, s0=ssl, b0=b_:
                               e.tensor_copy(K_AR[0:64, h0 * S + b0 * TB:
                                                  h0 * S + (b0 + 1) * TB],
                                             STG[0:64, s0 * 512:(s0 + 1) * 512]),
                           reads=(f"STG{ssl}",), writes=("K_AR",), tag="kmove")
                        o_("gpsimd",
                           lambda e, h0=ho, s0=ssl, b0=b_:
                               e.dma_start(out=K_AR[0:64, h0 * S + b0 * TB:
                                                   h0 * S + (b0 + 1) * TB],
                                           in_=STG[64:128, s0 * 512:(s0 + 1) * 512]),
                           reads=(f"STG{ssl}",), writes=("K_AR",), dma=True,
                           tag="kmove2")
                    else:
                        o_("vector",
                           lambda e, j0=jjl, s0=ssl, b0=b_:
                               e.tensor_copy(K_AR[:, j0 * S + b0 * TB:
                                                  j0 * S + (b0 + 1) * TB],
                                             STG[:, s0 * 512:(s0 + 1) * 512]),
                           reads=(f"STG{ssl}",), writes=("K_AR",), tag="kmove")
                o_("vector", lambda e: e.reduce_max(SC16[:, 2:3], P[4][0:16, :],
                                                    axis=AX.X),
                   reads=("P4",), writes=("SC16t",), tag="kmax")
                if b_ == 0:
                    o_("vector", lambda e: e.tensor_copy(SC16[:, 1:2], SC16[:, 2:3]),
                       reads=("SC16t",), writes=("SC16k",), tag="kmax2")
                else:
                    o_("vector", lambda e: e.tensor_tensor(
                        SC16[:, 1:2], SC16[:, 1:2], SC16[:, 2:3], op=ALU.max),
                       reads=("SC16t", "SC16k"), writes=("SC16k",), tag="kmax2")
            for b_ in range(NB):
                xsl = b_ % 2
                load_xblock(xsp, xres_in, b_, xsl, tag="xldv")
                for tt4 in range(4):
                    t_abs = 4 * b_ + tt4
                    pk = state["pk"] % 2
                    state["pk"] += 1
                    for dc in range(NCH):
                        o_("tensor",
                           lambda e, p0=pk, w0=wsl_v, t0=tt4, d0=dc, x0_=xsl,
                                  st=(dc == 0), sp=(dc == NCH - 1):
                               e.matmul(P[p0][:, :],
                                        XB[:, x0_ * 4096 + d0 * 512 + t0 * 128:
                                           x0_ * 4096 + d0 * 512 + t0 * 128 + 128],
                                        W_AR[:, w0 * 4096 + d0 * 512:
                                             w0 * 4096 + (d0 + 1) * 512],
                                        start=st, stop=sp),
                           reads=(f"W{wsl_v}", f"XB{xsl}"), writes=(f"P{pk}",),
                           tag="vproj")
                    vview = V_AR[:, t_abs * 520:t_abs * 520 + 8 * 65] \
                        .rearrange("p (h x) -> p h x", x=65)[:, :, 0:64]
                    o_("scalar",
                       lambda e, p0=pk, vv=vview:
                           e.copy(vv, P[p0][:, :].rearrange(
                               "p (h x) -> p h x", x=64)),
                       reads=(f"P{pk}",), writes=("V_AR",), tag="vev")
            wsl_q = wslot()
            load_weight(WQd[0], WQd[1] + half * 512, WQd[1] + half * 512 + 512,
                        wsl_q, "wq")
            for b_ in range(NB):
                xsl = b_ % 2
                load_xblock(xsp, xres_in, b_, xsl, tag="xldq")
                for jjl in range(4):
                    jc = 4 * half + jjl
                    pk = state["pk"] % 2
                    state["pk"] += 1
                    for dc in range(NCH):
                        o_("tensor",
                           lambda e, p0=pk, w0=wsl_q, j0=jjl, d0=dc, x0_=xsl,
                                  st=(dc == 0), sp=(dc == NCH - 1):
                               e.matmul(P[p0][:, :],
                                        W_AR[:, w0 * 4096 + d0 * 512 + j0 * 128:
                                             w0 * 4096 + d0 * 512 + j0 * 128 + 128],
                                        XB[:, x0_ * 4096 + d0 * 512:
                                           x0_ * 4096 + (d0 + 1) * 512],
                                        start=st, stop=sp),
                           reads=(f"W{wsl_q}", f"XB{xsl}"), writes=(f"P{pk}",),
                           tag="qproj")
                    ssl = jjl % 2
                    if bias_qkv:
                        o_("scalar",
                           lambda e, p0=pk, s0=ssl, j0=jc:
                               e.activation(STG[:, s0 * 512:(s0 + 1) * 512],
                                            P[p0][:, :], AF.Identity,
                                            bias=BIC[:, j0:j0 + 1]),
                           reads=(f"P{pk}", "BIC"), writes=(f"STG{ssl}",), tag="qev")
                    else:
                        o_("scalar",
                           lambda e, p0=pk, s0=ssl:
                               e.copy(STG[:, s0 * 512:(s0 + 1) * 512], P[p0][:, :]),
                           reads=(f"P{pk}",), writes=(f"STG{ssl}",), tag="qev")
                    o_("scalar",
                       lambda e, s0=ssl: e.activation(
                           SQ[:, s0 * 512:(s0 + 1) * 512],
                           STG[:, s0 * 512:(s0 + 1) * 512], AF.Square),
                       reads=(f"STG{ssl}",), writes=(f"SQ{ssl}",), tag="qsq")
                    o_("tensor",
                       lambda e, j0=jc, s0=ssl, st=(jjl == 0), sp=(jjl == 3):
                           e.matmul(P[5][0:16, :], ONES16[:, j0 * 16:(j0 + 1) * 16],
                                    SQ[:, s0 * 512:(s0 + 1) * 512],
                                    start=st, stop=sp),
                       reads=(f"SQ{ssl}", "ONES16"), writes=("P5",), tag="qstat")
                    he, ho = 2 * jc, 2 * jc + 1
                    o_("sync",
                       lambda e, h0=he, s0=ssl, b0=b_:
                           e.dma_start(out=qsp[h0, :, b0 * TB:(b0 + 1) * TB],
                                       in_=STG[0:64, s0 * 512:(s0 + 1) * 512]),
                       reads=(f"STG{ssl}",), writes=("qsp",), dma=True, tag="qspill")
                    o_("sync",
                       lambda e, h0=ho, s0=ssl, b0=b_:
                           e.dma_start(out=qsp[h0, :, b0 * TB:(b0 + 1) * TB],
                                       in_=STG[64:128, s0 * 512:(s0 + 1) * 512]),
                       reads=(f"STG{ssl}",), writes=("qsp",), dma=True, tag="qspill")
                o_("vector", lambda e: e.reduce_max(SC16[:, 3:4], P[5][0:16, :],
                                                    axis=AX.X),
                   reads=("P5",), writes=("SC16s",), tag="qmax")
                if b_ == 0:
                    o_("vector", lambda e: e.tensor_copy(SC16[:, 0:1], SC16[:, 3:4]),
                       reads=("SC16s",), writes=("SC16q",), tag="qmax2")
                else:
                    o_("vector", lambda e: e.tensor_tensor(
                        SC16[:, 0:1], SC16[:, 0:1], SC16[:, 3:4], op=ALU.max),
                       reads=("SC16s", "SC16q"), writes=("SC16q",), tag="qmax2")
            o_("vector", lambda e: e.tensor_tensor(SC16[:, 4:5], SC16[:, 0:1],
                                                   SC16[:, 1:2], op=ALU.mult),
               reads=("SC16q", "SC16k"), writes=("SC16p",), tag="bfin")
            o_("scalar", lambda e: e.activation(SC16[:, 5:6], SC16[:, 4:5],
                                                AF.Ln),
               reads=("SC16p",), writes=("SC16l",), tag="bfin")
            o_("scalar", lambda e: e.activation(SC16[:, 6:7], SC16[:, 5:6],
                                                AF.Exp, scale=0.5),
               reads=("SC16l",), writes=("SC16e",), tag="bfin")
            o_("vector", lambda e: e.tensor_scalar(SC16[:, 7:8], SC16[:, 6:7],
                                                   -0.1251250, -0.5,
                                                   op0=ALU.mult, op1=ALU.add),
               reads=("SC16e",), writes=("SC16b",), tag="bfin")
            o_("vector", lambda e: e.tensor_scalar(BCT[:, :], ONE16[:, :],
                                                   SC16[0:16, 7:8], None,
                                                   op0=ALU.mult),
               reads=("SC16b", "ONE16"), writes=("BCT",), tag="bfin")
            c0b = half * 8 * 36
            c1b = (half + 1) * 8 * 36
            o_("tensor",
               lambda e, a0=c0b, a1=c1b: e.matmul(P[5][:, 0:a1 - a0],
                                                  BCT[:, :], EXP16[:, a0:a1],
                                                  start=True, stop=True),
               reads=("BCT", "EXP16"), writes=("P5",), tag="bbc")
            o_("vector",
               lambda e, a0=c0b, a1=c1b: e.tensor_tensor(
                   ALIBB[:, a0:a1], ALIBB[:, a0:a1], P[5][:, 0:a1 - a0],
                   op=ALU.add),
               reads=("P5", "ALIBB"), writes=("ALIBB",), tag="bbc")
            if stop_after == "kv":
                continue
            if not alibi:
                for b_ in range(NB):
                    for pj in range(4):
                        h_e = 8 * half + 2 * pj
                        h_o = h_e + 1
                        pcnt = half * NB * 4 + b_ * 4 + pj
                        qsl = pcnt % 2
                        drain_pre(1)
                        o_("sync",
                           lambda e, h0=h_e, b0=b_, q0=qsl:
                               e.dma_start(out=QS[0:64, q0 * 512:(q0 + 1) * 512],
                                           in_=qsp[h0, :, b0 * TB:(b0 + 1) * TB]),
                           reads=("qsp",), writes=(f"QS{qsl}",), dma=True,
                           tag="qld")
                        o_("sync",
                           lambda e, h0=h_o, b0=b_, q0=qsl:
                               e.dma_start(out=QS[64:128, q0 * 512:(q0 + 1) * 512],
                                           in_=qsp[h0, :, b0 * TB:(b0 + 1) * TB]),
                           reads=("qsp",), writes=(f"QS{qsl}",), dma=True,
                           tag="qld")
                        LAGP = 2

                        def emit_smm(i, pj=pj, qsl=qsl):
                            for (sb_, base, tp) in (
                                    ([0, 1][i % 2], 0, (0, 0)),
                                    ([4, 5][i % 2], 64, (64, 0))):
                                o_("tensor",
                                   lambda e, p0=sb_, j0=pj, c0=i, q0=qsl,
                                          b0=base, t0=tp:
                                       e.matmul(P[p0][:, :],
                                                K_AR[b0:b0 + 64,
                                                     j0 * S + c0 * 128:
                                                     j0 * S + c0 * 128 + 128],
                                                QS[b0:b0 + 64,
                                                   q0 * 512:(q0 + 1) * 512],
                                                start=True, stop=True,
                                                tile_position=t0),
                                   reads=("K_AR", f"QS{qsl}"),
                                   writes=(f"P{sb_}",), tag="smm")

                        def emit_exp(i, h_e=h_e, h_o=h_o):
                            for (sb_, base, hq) in (
                                    ([0, 1][i % 2], 0, h_e),
                                    ([4, 5][i % 2], 64, h_o)):
                                dcol = hq * 36 + 32
                                psl = (2 * i + (0 if base == 0 else 1)) % 4
                                o_("scalar",
                                   lambda e, p0=sb_, s0=psl, cc=dcol:
                                       e.activation(
                                           PTB[:, s0 * 512:(s0 + 1) * 512],
                                           P[p0][:, :], AF.Exp,
                                           bias=ALIBB[:, cc:cc + 1],
                                           scale=0.125),
                                   reads=(f"P{sb_}", "ALIBB"),
                                   writes=(f"PTB{psl}",), tag="exp")

                        def emit_avp(i, pj=pj):
                            for (av, hh2, par) in ((2, 2 * pj, 0),
                                                   (3, 2 * pj + 1, 1)):
                                psl = (2 * i + par) % 4
                                o_("tensor",
                                   lambda e, a0=av, c0=i, hx=hh2, s0=psl,
                                          st=(i == 0), sp=(i == 31):
                                       e.matmul(P[a0][0:65, :],
                                                V_AR[:, c0 * 520 + hx * 65:
                                                     c0 * 520 + hx * 65 + 65],
                                                PTB[:, s0 * 512:(s0 + 1) * 512],
                                                start=st, stop=sp),
                                   reads=("V_AR", f"PTB{psl}"),
                                   writes=(f"P{av}",), tag="avmm")

                        for i in range(32):
                            emit_smm(i)
                            if i == 2 and pend_tail[0] is not None:
                                pend_tail[0]()
                                pend_tail[0] = None
                            if i >= LAGP:
                                emit_avp(i - LAGP)
                            emit_exp(i)
                        for i in range(32 - LAGP, 32):
                            emit_avp(i)
                        if pend_tail[0] is not None:
                            pend_tail[0]()
                            pend_tail[0] = None

                        def mk_tailp(h_e=h_e, h_o=h_o, b_=b_):
                            def tail():
                                for (av, hq, osl) in ((2, h_e, 0), (3, h_o, 1)):
                                    o_("vector",
                                       lambda e, a0=av: e.reciprocal(
                                           OUTS[64:65, 0:512], P[a0][64:65, :]),
                                       reads=(f"P{av}",), writes=("rrow",),
                                       tag="recip")
                                    o_("tensor",
                                       lambda e: e.matmul(P[5][0:64, :],
                                                          OUTS[64:65, 512:576],
                                                          OUTS[64:65, 0:512],
                                                          start=True, stop=True),
                                       reads=("rrow", "RRBonesF"), writes=("P5",),
                                       tag="rbmm")
                                    o_("scalar",
                                       lambda e: e.copy(RB[0:64, :],
                                                        P[5][0:64, :]),
                                       reads=("P5",), writes=("RB",), tag="rbev")
                                    o_("vector",
                                       lambda e, a0=av, s0=osl:
                                           e.tensor_tensor(
                                               OTS[0:64,
                                                   s0 * 512:(s0 + 1) * 512],
                                               P[a0][0:64, :], RB[0:64, :],
                                               op=ALU.mult),
                                       reads=(f"P{av}", "RB"),
                                       writes=(f"OTS{osl}",), tag="onorm")
                                    if bias_qkv:
                                        o_("vector",
                                           lambda e, s0=osl, h0=hq:
                                               e.tensor_scalar(
                                                   OTS[0:64,
                                                       s0 * 512:(s0 + 1) * 512],
                                                   OTS[0:64,
                                                       s0 * 512:(s0 + 1) * 512],
                                                   BVH[:, h0:h0 + 1], None,
                                                   op0=ALU.add),
                                           reads=("BVH",),
                                           writes=(f"OTS{osl}",), tag="obias")
                                    o_("gpsimd",
                                       lambda e, h0=hq, b0=b_, s0=osl:
                                           e.dma_start(
                                               out=osp[64 * h0:64 * h0 + 64,
                                                       b0 * TB:(b0 + 1) * TB],
                                               in_=OTS[0:64,
                                                       s0 * 512:(s0 + 1) * 512]),
                                       reads=(f"OTS{osl}",), writes=("osp",),
                                       dma=True, tag="ospill")
                            return tail
                        pend_tail[0] = mk_tailp()
            else:
              for b_ in range(NB):
                nch = 4 * b_ + 4 if causal else 32
                for hp in range(4):
                    drain_pre(8)
                    hlists = []
                    pair_tails = []
                    for hh in (2 * hp, 2 * hp + 1):
                        h = 8 * half + hh
                        hcnt = half * NB * 8 + b_ * 8 + hh
                        qsl = hcnt % 2
                        avb = 2 + (hcnt % 2)
                        sbks = (0, 1) if hh % 2 == 0 else (4, 5)
                        rbk = 1 if hh % 2 == 0 else 5
                        pt0 = 0 if hh % 2 == 0 else 2
                        if causal:
                            dwin = int(110.0 / max(slope_eff_py[h], 1e-9))
                            cmin = max(0, (512 * b_ - 127 - dwin + 127) // 128)
                        else:
                            cmin = 0
                        chunks = list(range(cmin, nch))
                        LAG = 2

                        def av_op(i, chunks=chunks, avb=avb, hh=hh, pt0=pt0):
                            c = chunks[i]
                            psl = pt0 + i % 2
                            o_("tensor",
                               lambda e, a0=avb, c0=c, hx=hh, s0=psl,
                                      st=(i == 0), sp=(i == len(chunks) - 1):
                                   e.matmul(P[a0][0:65, :],
                                            V_AR[:, c0 * 520 + hx * 65:
                                                 c0 * 520 + hx * 65 + 65],
                                            PTB[:, s0 * 512:(s0 + 1) * 512],
                                            start=st, stop=sp),
                               reads=("V_AR", f"PTB{psl}"),
                               writes=(f"P{avb}",), tag="avmm")

                        def body(hh=hh, h=h, qsl=qsl, sbks=sbks, pt0=pt0,
                                 b2=b_, chunks=chunks, av_op=av_op):
                            o_("sync",
                               lambda e, h0=h, b0=b2, q0=qsl:
                                   e.dma_start(
                                       out=QS[0:64, q0 * 512:(q0 + 1) * 512],
                                       in_=qsp[h0, :, b0 * TB:(b0 + 1) * TB]),
                               reads=("qsp",), writes=(f"QS{qsl}",), dma=True,
                               tag="qld")
                            for i in range(len(chunks)):
                                c = chunks[i]
                                psl = pt0 + i % 2
                                sbk = sbks[i % 2]
                                o_("tensor",
                                   lambda e, p0=sbk, h0=hh, c0=c, q0=qsl:
                                       e.matmul(P[p0][:, :],
                                                K_AR[0:kext,
                                                     h0 * S + c0 * 128:
                                                     h0 * S + c0 * 128 + 128],
                                                QS[0:kext,
                                                   q0 * 512:(q0 + 1) * 512],
                                                start=True, stop=True),
                                   reads=("K_AR", f"QS{qsl}", "QSj0", "QSj1"),
                                   writes=(f"P{sbk}",), tag="smm")
                                if causal and c >= 4 * b2:
                                    off = 384 - 128 * (c - 4 * b2)
                                    o_("vector",
                                       lambda e, p0=sbk, f0=off:
                                           e.tensor_tensor(
                                               P[p0][:, :], P[p0][:, :],
                                               MASKN[:, f0:f0 + 512],
                                               op=ALU.add),
                                       reads=(f"P{sbk}", "MASKN"),
                                       writes=(f"P{sbk}",), tag="mask")
                                if i >= LAG:
                                    av_op(i - LAG)
                                dcol = h * 36 + ((c - 4 * b2 + 32)
                                                 if causal else 32)
                                o_("scalar",
                                   lambda e, p0=sbk, s0=psl, cc=dcol:
                                       e.activation(
                                           PTB[:, s0 * 512:(s0 + 1) * 512],
                                           P[p0][:, :], AF.Exp,
                                           bias=ALIBB[:, cc:cc + 1],
                                           scale=0.125),
                                   reads=(f"P{sbk}", "ALIBB"),
                                   writes=(f"PTB{psl}",), tag="exp")
                            for i in range(max(0, len(chunks) - LAG),
                                           len(chunks)):
                                av_op(i)
                        hlists.append(collect_ops(body))

                        def mk_tail(avb=avb, rbk=rbk, h=h, b2=b_,
                                    osl=hcnt % 2):
                            def tail():
                                o_("scalar",
                                   lambda e, a0=avb: e.activation(
                                       OUTS[64:65, 0:512], P[a0][64:65, :],
                                       AF.Ln),
                                   reads=(f"P{avb}",), writes=("rrow",),
                                   tag="recip")
                                o_("scalar",
                                   lambda e: e.activation(
                                       OUTS[64:65, 0:512],
                                       OUTS[64:65, 0:512],
                                       AF.Exp, scale=-1.0),
                                   reads=("rrow",), writes=("rrow",),
                                   tag="recip")
                                o_("tensor",
                                   lambda e, r0=rbk: e.matmul(
                                       P[r0][0:64, :],
                                       OUTS[64:65, 512:576],
                                       OUTS[64:65, 0:512],
                                       start=True, stop=True),
                                   reads=("rrow", "RRBonesF"),
                                   writes=(f"P{rbk}",), tag="rbmm")
                                o_("scalar",
                                   lambda e, r0=rbk: e.copy(RB[0:64, :],
                                                            P[r0][0:64, :]),
                                   reads=(f"P{rbk}",), writes=("RB",),
                                   tag="rbev")
                                o_("vector",
                                   lambda e, a0=avb, s0=osl:
                                       e.tensor_tensor(
                                           OTS[0:64,
                                               s0 * 512:(s0 + 1) * 512],
                                           P[a0][0:64, :], RB[0:64, :],
                                           op=ALU.mult),
                                   reads=(f"P{avb}", "RB"),
                                   writes=(f"OTS{osl}",), tag="onorm")
                                o_("gpsimd",
                                   lambda e, h0=h, b0=b2, s0=osl:
                                       e.dma_start(
                                           out=osp[64 * h0:64 * h0 + 64,
                                                   b0 * TB:(b0 + 1) * TB],
                                           in_=OTS[0:64,
                                                   s0 * 512:(s0 + 1) * 512]),
                                   reads=(f"OTS{osl}",), writes=("osp",),
                                   dma=True, tag="ospill")
                            return tail
                        pair_tails.append(mk_tail())
                    la, lb = hlists
                    merged = []
                    for i in range(max(len(la), len(lb))):
                        if i < len(la):
                            merged.append(la[i])
                        if i < len(lb):
                            merged.append(lb[i])
                    emit_seq(merged[:6])
                    if pend_tail[0] is not None:
                        pend_tail[0]()
                        pend_tail[0] = None
                    emit_seq(merged[6:])

                    def mk_pair(tails=tuple(pair_tails)):
                        def both():
                            for t2 in tails:
                                t2()
                        return both
                    pend_tail[0] = mk_pair()
        if pend_tail[0] is not None:
            pend_tail[0]()
            pend_tail[0] = None
        if stop_after in ("kv", "qloop"):
            return
        o_("vector", lambda e: e.memset(SC16[:, 5:6], 0.0),
           writes=("K_AR",) + tuple(f"RLB{dc}" for dc in range(NCH)) +
                  ("AGBC", "AGBD", "SQB0", "SQB1",
                   "XBB0", "XBB1", "TKB0", "TKB1", "XBB0", "XBB1",
                   "TKB0", "TKB1"),
           tag="opfence")
        wsl_o1 = wslot()
        load_weight(WOd[0], WOd[1], WOd[1] + 512, wsl_o1, "wo1")
        wsl_o2 = wslot()
        load_weight(WOd[0], WOd[1] + 512, WOd[1] + 1024, wsl_o2, "wo2")
        ARNB = K_AR[:, 0:4096]
        AGBB_ = K_AR[:, 4096:5120]
        SQB = K_AR[:, 5120:6144]
        XBB = K_AR[:, 6144:14336]
        TKB = (K_AR[:, 14336:15376], K_AR[:, 15376:16416])

        def load_oblock(xsp2, xres2, b2, dst, dres, tag):
            src2 = xsp2.rearrange("(c p) t -> p c t",
                                  p=128)[:, :, b2 * TB:(b2 + 1) * TB]
            d2 = dst.rearrange("p (c t) -> p c t", t=TB)
            o_("sync", lambda e, a=d2, s3=src2: e.dma_start(out=a, in_=s3),
               reads=(xres2,), writes=(dres,), dma=True, tag=tag)

        def o_block(b_):
            par = b_ % 2
            if par == 0:
                ARN, afmt = RLN, "RLN{}"
                agb, agrA, agrB = AGB, "AGBA", "AGBB"
                sqb, sqf = SQ, "SQ{}"
                supb, supr = P[4], "P4"
                sqpb, sqpr = P[5], "P5"
                xbs, x0r, x1r = XB, "XB0", "XB1"
                pko = 0
                tstg = ((STG, ("STG0", "STG1")), (SQT, ("SQT",)))
                rb0 = 0
            else:
                ARN, afmt = ARNB, "RLB{}"
                agb, agrA, agrB = AGBB_, "AGBC", "AGBD"
                sqb, sqf = SQB, "SQB{}"
                supb, supr = P[2], "P2"
                sqpb, sqpr = P[3], "P3"
                xbs, x0r, x1r = XBB, "XBB0", "XBB1"
                pko = 1
                tstg = ((TKB[0], ("TKB0",)), (TKB[1], ("TKB1",)))
                rb0 = 64
            load_oblock(osp, "osp", b_, xbs[:, 0:4096], x0r, "old")
            load_oblock(xsp, xres_in, b_, xbs[:, 4096:8192], x1r, "xres")
            for dc in range(NCH):
                wsl_o = wsl_o1 if dc < 4 else wsl_o2
                dcl = dc % 4
                for kc in range(NCH):
                    o_("tensor",
                       lambda e, p0=pko, w0=wsl_o, d0=dcl, k0=kc, x2=xbs,
                              st=(kc == 0), sp=(kc == NCH - 1):
                           e.matmul(P[p0][:, :],
                                    W_AR[:, w0 * 4096 + k0 * 512 + d0 * 128:
                                         w0 * 4096 + k0 * 512 + d0 * 128 + 128],
                                    x2[:, k0 * 512:(k0 + 1) * 512],
                                    start=st, stop=sp),
                       reads=(f"W{wsl_o}", x0r), writes=(f"P{pko}",),
                       tag="oproj")
                if bias_o:
                    o_("vector",
                       lambda e, p0=pko, d0=dc, a2=ARN, x2=xbs:
                           e.scalar_tensor_tensor(
                               a2[:, d0 * 512:(d0 + 1) * 512], P[p0][:, :],
                               BOC[:, d0:d0 + 1],
                               x2[:, 4096 + d0 * 512:4096 + (d0 + 1) * 512],
                               op0=ALU.add, op1=ALU.add),
                       reads=(f"P{pko}", x1r, "BOC"),
                       writes=(afmt.format(dc),), tag="resid")
                else:
                    o_("vector",
                       lambda e, p0=pko, d0=dc, a2=ARN, x2=xbs:
                           e.tensor_tensor(a2[:, d0 * 512:(d0 + 1) * 512],
                                           P[p0][:, :],
                                           x2[:, 4096 + d0 * 512:
                                              4096 + (d0 + 1) * 512],
                                           op=ALU.add),
                       reads=(f"P{pko}", x1r), writes=(afmt.format(dc),),
                       tag="resid")
                o_("scalar",
                   lambda e, d0=dc, s0=dc % 2, a2=ARN, s2=sqb: e.activation(
                       s2[:, s0 * 512:(s0 + 1) * 512],
                       a2[:, d0 * 512:(d0 + 1) * 512], AF.Square),
                   reads=(afmt.format(dc),), writes=(sqf.format(dc % 2),),
                   tag="osq")
                o_("tensor",
                   lambda e, d0=dc, a2=ARN, pb=supb,
                          st=(dc == 0), sp=(dc == NCH - 1):
                       e.matmul(pb[0:1, :], ONESB[:, 0:1],
                                a2[:, d0 * 512:(d0 + 1) * 512],
                                start=st, stop=sp),
                   reads=(afmt.format(dc), "ONESB"), writes=(supr,),
                   tag="osum")
                o_("tensor",
                   lambda e, s0=dc % 2, s2=sqb, pb=sqpb,
                          st=(dc == 0), sp=(dc == NCH - 1):
                       e.matmul(pb[64:65, :], ONESB[:, 0:1],
                                s2[:, s0 * 512:(s0 + 1) * 512],
                                start=st, stop=sp),
                   reads=(sqf.format(dc % 2), "ONESB"),
                   writes=(sqpr,), tag="osumsq")
            ln_rows(supb, supr, sqpb, sqpr)
            emit_ln_norm(b_, ARN, afmt, gb, xout, xoutres,
                         agb, agrA, agrB, supb, supr, sqpb, sqpr)
            if routing:
                for tt4 in range(4):
                    for dc in range(NCH):
                        o_("tensor",
                           lambda e, t0=tt4, d0=dc, a2=ARN, pb=sqpb,
                                  st=(dc == 0), sp=(dc == NCH - 1):
                               e.matmul(pb[:, t0 * 12:(t0 + 1) * 12],
                                        a2[:, d0 * 512 + t0 * 128:
                                            d0 * 512 + t0 * 128 + 128],
                                        WRT[:, d0 * 12:(d0 + 1) * 12],
                                        start=st, stop=sp),
                           reads=(afmt.format(dc), "WRT"),
                           writes=(sqpr,), tag="rtmm")
                for tt4 in range(4):
                    emit_routing(b_, tt4, sqpb, sqpr, rb0)
            if tokspill is not None:
                tokd, tokres, with_wc = tokspill
                twidth = 1040 if with_wc else 1024
                for tt4 in range(4):
                    stg, swr = tstg[tt4 % 2]
                    for w in range(8):
                        bank = PT7 if w % 2 == 0 else PT8
                        bres = "PT7" if w % 2 == 0 else "PT8"
                        wcol = (w // 2) * 128
                        o_("tensor",
                           lambda e, w0=w, t0=tt4, k2=bank, c2=wcol, a2=ARN:
                               e.transpose(k2[:, c2:c2 + 128],
                                           a2[:, w0 * 512 + t0 * 128:
                                               w0 * 512 + t0 * 128 + 128],
                                           IDENT[:, :]),
                           reads=(afmt.format(w), "IDENT"), writes=(bres,),
                           tag="ttp")
                        o_("scalar",
                           lambda e, w0=w, k2=bank, c2=wcol, s2=stg:
                               e.copy(s2[:, w0 * 128:(w0 + 1) * 128],
                                      k2[:, c2:c2 + 128]),
                           reads=(bres,), writes=swr, tag="tte")
                    if with_wc:
                        o_("vector",
                           lambda e, s2=stg, ti=4 * b_ + tt4:
                               e.tensor_copy(s2[:, 1024:1025],
                                             WCTS[:, ti:ti + 1]),
                           reads=("WCTS",), writes=swr, tag="ttwc")
                        o_("vector",
                           lambda e, s2=stg: e.memset(s2[:, 1025:1040], 0.0),
                           writes=swr, tag="ttwc")
                    o_("sync",
                       lambda e, s2=stg, t0=tt4, b0=b_, tw=twidth:
                           e.dma_start(out=tokd[(4 * b0 + t0) * 128:
                                                (4 * b0 + t0) * 128 + 128,
                                                0:tw],
                                       in_=s2[:, 0:tw]),
                       reads=swr, writes=(tokres,), dma=True, tag="tts")

        SEG0 = {"old", "xres", "oproj", "resid", "osq", "osum", "osumsq"}
        SEG1 = {"ln", "lnag", "lnev", "lnbg"}
        SEG3 = {"ttp", "tte", "ttwc", "tts"}

        def segs(lst):
            s0, s1, s2, s3 = [], [], [], []
            for op in lst:
                tg = op[5]
                if tg in SEG0:
                    s0.append(op)
                elif tg in SEG1:
                    s1.append(op)
                elif tg in SEG3:
                    s3.append(op)
                else:
                    s2.append(op)
            return s0, s1, s2, s3

        def rr2(la2, lb2):
            def chunks(lst):
                out_, cur = [], []
                for op in lst:
                    cur.append(op)
                    if op[5] == "tts":
                        out_.append(cur)
                        cur = []
                if cur:
                    out_.append(cur)
                return out_
            ca, cb = chunks(la2), chunks(lb2)
            for i in range(max(len(ca), len(cb))):
                if i < len(ca):
                    emit_seq(ca[i])
                if i < len(cb):
                    emit_seq(cb[i])

        for bp in range(0, NB, 2):
            s0a, s1a, s2a, s3a = segs(collect_ops(lambda: o_block(bp)))
            s0b, s1b, s2b, s3b = segs(collect_ops(lambda: o_block(bp + 1)))
            emit_rr(s0a, s0b)
            emit_seq(s1a)
            emit_rr(s2a, s1b + s2b)
            rr2(s3a, s3b)

    NT = 15

    def Hap(fc):
        if fc < 8:
            return RLN[:, fc * 512:(fc + 1) * 512]
        return K_AR[:, 12288 + (fc - 8) * 512:12288 + (fc - 7) * 512]

    def emit_moe():
        drain_pre(10 ** 9)
        yres = tuple(f"Y{t}" for t in range(NT))
        o_("vector", lambda e: e.memset(SC16[:, 2:3], 0.0),
           reads=("K_AR", "V_AR", "RB"),
           writes=yres + tuple(f"RLN{dc}" for dc in range(NCH)) +
                  tuple(f"RLB{dc}" for dc in range(NCH)) +
                  tuple(f"H{fc}" for fc in range(16)) +
                  ("AGBC", "AGBD", "SQB0", "SQB1",
                   "XBB0", "XBB1", "TKB0", "TKB1",
                   "XB0", "XB1", "SCT0", "SCT1", "SCT2", "SCT3",
                   "K_AR", "V_AR", "RB", "W1F0", "W1F1", "W2F",
                   "W0", "W1", "W2", "W3", "XGT0", "XGT1", "WCROW",
                   "ONE8", "CUM8", "YGT"),
           tag="fence")
        for b_ in range(NB):
            cb = PT7 if b_ % 2 == 0 else PT8
            cbres = "PT7" if b_ % 2 == 0 else "PT8"
            for tt4 in range(4):
                tile = 4 * b_ + tt4
                o_("tensor",
                   lambda e, t0=tt4, ti=tile, k2=cb:
                       e.transpose(k2[0:8, t0 * 128:(t0 + 1) * 128],
                                   CMB[:, ti * 8:(ti + 1) * 8], IDENT[:, :]),
                   reads=("CMB", "IDENT"), writes=(cbres,), tag="ctp")
            o_("scalar",
               lambda e, b0=b_, k2=cb:
                   e.copy(CMTR[:, b0 * 512:(b0 + 1) * 512], k2[0:8, 0:512]),
               reads=(cbres,), writes=(f"CMT{b_}",), tag="ctev")
        rfr = tuple(f"CMT{b2}" for b2 in range(NB))
        o_("vector", lambda e: e.tensor_scalar(ONE8, CMTR[:, 0:S], 0.0, None,
                                               op0=ALU.is_gt),
           reads=rfr, writes=("ONE8",), tag="rf")
        o_("vector", lambda e: e.tensor_tensor_scan(CUM8, ONE8, ONE8, 0.0,
                                                    op0=ALU.add,
                                                    op1=ALU.bypass),
           reads=("ONE8",), writes=("CUM8",), tag="rf")
        o_("tensor", lambda e: e.matmul(P[4][0:1, 0:128], CUM8[:, S - 1:S],
                                        IDENT32[0:8, 0:128],
                                        start=True, stop=True),
           reads=("CUM8", "IDENT32"), writes=("P4",), tag="rfmm")
        o_("scalar", lambda e: e.copy(mr(0, 8), P[4][0:1, 0:8]),
           reads=("P4",), writes=("MROWc",), tag="rf")
        o_("vector", lambda e: e.memset(mr(8, 16), 0.0),
           writes=("MROWt",), tag="rf")
        for k_ in range(NT):
            o_("vector",
               lambda e, k0=k_: e.tensor_scalar(mr(56, 64),
                                                mr(0, 8),
                                                512.0 * k0, None,
                                                op0=ALU.is_gt),
               reads=("MROWc",), writes=("MROWx",), tag="rf")
            o_("vector",
               lambda e: e.tensor_tensor(mr(8, 16),
                                         mr(8, 16),
                                         mr(56, 64), op=ALU.add),
               reads=("MROWx",), writes=("MROWt",), tag="rf")
        o_("vector", lambda e: e.tensor_tensor_scan(mr(16, 24),
                                                    mr(8, 16),
                                                    mr(8, 16), 0.0,
                                                    op0=ALU.add,
                                                    op1=ALU.bypass),
           reads=("MROWt",), writes=("MROWi",), tag="rf")
        o_("vector", lambda e: e.tensor_tensor(mr(24, 32),
                                               mr(16, 24),
                                               mr(8, 16),
                                               op=ALU.subtract),
           reads=("MROWi", "MROWt"), writes=("MROWs",), tag="rf")
        o_("vector", lambda e: e.tensor_scalar(mr(0, 8),
                                               mr(24, 32), 512.0,
                                               None, op0=ALU.mult),
           reads=("MROWs",), writes=("MROWb",), tag="rf")
        o_("vector", lambda e: e.memset(mr(40, 56), -1.0),
           writes=("MROWe",), tag="rf")
        for e_i in range(NE):
            o_("vector",
               lambda e, e0=e_i: e.tensor_scalar(
                   mr(56, 72), iot16_r,
                   mr(24 + e0, 25 + e0), None, op0=ALU.is_ge),
               reads=("MROWs", "ROWC"), writes=("MROWx",), tag="rf")
            o_("vector",
               lambda e: e.tensor_tensor(mr(40, 56),
                                         mr(40, 56),
                                         mr(56, 72), op=ALU.add),
               reads=("MROWx",), writes=("MROWe",), tag="rf")
        o_("tensor", lambda e: e.matmul(P[4][0:8, 128:129], mr(0, 8),
                                        one1_r, start=True, stop=True),
           reads=("MROWb", "ROWC"), writes=("P4",), tag="rfmm")
        o_("scalar", lambda e: e.copy(BASEC[0:8, 0:1], P[4][0:8, 128:129]),
           reads=("P4",), writes=("BASECr",), tag="rf")
        o_("vector", lambda e: e.tensor_scalar(CUM8, CUM8, BASEC[0:8, 0:1],
                                               -1.0, op0=ALU.add,
                                               op1=ALU.add),
           reads=("BASECr", "CUM8"), writes=("CUM8",), tag="rf")
        o_("vector", lambda e: e.tensor_tensor(CUM8, CUM8, ONE8, op=ALU.mult),
           reads=("ONE8", "CUM8"), writes=("CUM8",), tag="rf")
        o_("vector", lambda e: e.memset(ONES8F[:, 0:1], 1.0),
           writes=("O8F",), tag="rf")
        for ch in range(8):
            o_("tensor",
               lambda e, c0=ch: e.matmul(P[5][0:1, 0:512], ONES8F[0:8, 0:1],
                                         CUM8[:, c0 * 512:(c0 + 1) * 512],
                                         start=True, stop=True),
               reads=("CUM8", "O8F"), writes=("P5",), tag="rfmm")
            o_("vector",
               lambda e: e.tensor_copy(IRV[0:1, :], P[5][0:1, 0:512]),
               reads=("P5",), writes=("IROW",), tag="rf")
            o_("sync",
               lambda e, c0=ch: e.dma_start(
                   out=ib1[0:1, c0 * 512:(c0 + 1) * 512],
                   in_=IRV[0:1, :]),
               reads=("IROW",), writes=("ib1",), dma=True, tag="rf")

        def _reload(e):
            with nc.allow_non_contiguous_dma(reason="16KB index repartition"):
                return e.dma_start(out=DESTC[:, 0:32],
                                   in_=ib1.rearrange("o (t p) -> p t o",
                                                     p=128))
        o_("gpsimd", _reload, reads=("ib1",), writes=("DESTC",), dma=True,
           tag="rfr")
        o_("tensor", lambda e: e.matmul(P[4][:, 256:272], ones128,
                                        mr(40, 56),
                                        start=True, stop=True),
           reads=("MROWe", "ROWC"), writes=("P4",), tag="rfmm")
        o_("vector", lambda e: e.tensor_scalar(EIDF[:, 0:16], P[4][:, 256:272],
                                               128.0, None, op0=ALU.mult),
           reads=("P4",), writes=("EIDF",), tag="rf")
        o_("vector", lambda e: e.tensor_scalar(EIDF[:, 0:16], EIDF[:, 0:16],
                                               IOCF[:, 0:1], None,
                                               op0=ALU.add),
           reads=("IOCF", "EIDF"), writes=("EIDF",), tag="rf")
        o_("vector", lambda e: e.tensor_copy(EIDCI[:, 0:16], EIDF[:, 0:16]),
           reads=("EIDF",), writes=("EIDCI",), tag="rf")
        o_("vector", lambda e: e.memset(SC16[:, 3:4], 0.0),
           writes=("ONE8", "CUM8", "W1F0", "W1F1", "XB0", "XB1",
                   "SCT0", "SCT1", "SCT2", "SCT3", "XGT0", "XGT1") +
                  tuple(f"H{fc}" for fc in range(16)),
           tag="fence2")
        for _g in (0,):
            o_("gpsimd",
               lambda e: e.indirect_dma_start(
                   out=V_AR[:, 0:8192], out_offset=None, in_=w1b0,
                   in_offset=IndirectOffsetOnAxis(EIDCI[:, 0:1], 0)),
               reads=("EIDCI", "w1b0"), writes=("W1F0",), dma=True, tag="w1g")
            o_("gpsimd",
               lambda e: e.indirect_dma_start(
                   out=V_AR[:, 8192:16384], out_offset=None, in_=w1b1,
                   in_offset=IndirectOffsetOnAxis(EIDCI[:, 0:1], 0)),
               reads=("EIDCI", "w1b1"), writes=("W1F1",), dma=True, tag="w1g")
            o_("gpsimd",
               lambda e: e.indirect_dma_start(
                   out=W_AR[:, 0:16384], out_offset=None, in_=w2b,
                   in_offset=IndirectOffsetOnAxis(EIDCI[:, 0:1], 0)),
               reads=("EIDCI", "w2b"), writes=("W2F",), dma=True, tag="w2g")
            o_("gpsimd",
               lambda e: e.indirect_dma_start(
                   out=B1T[:, 0:24], out_offset=None, in_=b12r,
                   in_offset=IndirectOffsetOnAxis(EIDCI[:, 0:1], 0)),
               reads=("EIDCI",), writes=("B1T",), dma=True, tag="bg")
        for tt in range(32):
            xsl = tt % 4
            o_("sync",
               lambda e, t0=tt, x0_=xsl * 2048:
                   e.dma_start(out=XB[:, x0_:x0_ + 1040],
                               in_=x2tok[t0 * 128:(t0 + 1) * 128, :]),
               reads=("x2tok",), writes=(f"SCT{xsl}",), dma=True, tag="scl")
            o_("gpsimd",
               lambda e, t0=tt, x0_=xsl * 2048:
                   e.indirect_dma_start(
                       out=xg,
                       out_offset=IndirectOffsetOnAxis(DESTC[:, t0:t0 + 1], 0),
                       in_=XB[:, x0_:x0_ + 1040], in_offset=None),
               reads=("DESTC", f"SCT{xsl}"), writes=("xg",), dma=True,
               tag="sct")
        STAGE = ((XGL, ("XGL",)), (STG, ("STG0", "STG1")),
                 (SQT, ("SQT",)), (PTB, ("PTB0", "PTB1", "PTB2")))

        def emit_wg(j):
            o_("gpsimd",
               lambda e, j0=j: e.indirect_dma_start(
                   out=V_AR[:, 0:8192], out_offset=None, in_=w1b0,
                   in_offset=IndirectOffsetOnAxis(EIDCI[:, j0:j0 + 1], 0)),
               reads=("EIDCI", "w1b0"), writes=("W1F0",), dma=True, tag="w1g")
            o_("gpsimd",
               lambda e, j0=j: e.indirect_dma_start(
                   out=V_AR[:, 8192:16384], out_offset=None, in_=w1b1,
                   in_offset=IndirectOffsetOnAxis(EIDCI[:, j0:j0 + 1], 0)),
               reads=("EIDCI", "w1b1"), writes=("W1F1",), dma=True, tag="w1g")
            o_("gpsimd",
               lambda e, j0=j: e.indirect_dma_start(
                   out=W_AR[:, 0:16384], out_offset=None, in_=w2b,
                   in_offset=IndirectOffsetOnAxis(EIDCI[:, j0:j0 + 1], 0)),
               reads=("EIDCI", "w2b"), writes=("W2F",), dma=True, tag="w2g")
            o_("gpsimd",
               lambda e, j0=j: e.indirect_dma_start(
                   out=B1T[:, 0:24], out_offset=None, in_=b12r,
                   in_offset=IndirectOffsetOnAxis(EIDCI[:, j0:j0 + 1], 0)),
               reads=("EIDCI",), writes=("B1T",), dma=True, tag="bg")

        def emit_xgl(j):
            for s in range(4):
                stg, swr = STAGE[s]
                o_("sync",
                   lambda e, j0=j, s0=s, s2=stg:
                       e.dma_start(out=s2[:, 0:1040],
                                   in_=xg[j0 * 512 + s0 * 128:
                                          j0 * 512 + s0 * 128 + 128, :]),
                   reads=("xg",), writes=swr, dma=True, tag="xgl")

        emit_xgl(0)
        for j in range(NT):
            xpar = j % 2
            xga = XB[:, xpar * 4096:(xpar + 1) * 4096]
            xgr = f"XGT{xpar}"
            if j > 0:
                emit_wg(j)
            for sct in range(4):
                stg, swr = STAGE[sct]
                for w in range(9):
                    bank = PT7 if w % 2 == 0 else PT8
                    bres = "PT7" if w % 2 == 0 else "PT8"
                    wcol = (w // 2) * 128
                    if w < 8:
                        o_("tensor",
                           lambda e, w0=w, k2=bank, c2=wcol, s2=stg:
                               e.transpose(k2[:, c2:c2 + 128],
                                           s2[:, w0 * 128:(w0 + 1) * 128],
                                           IDENT[:, :]),
                           reads=swr + ("IDENT",), writes=(bres,), tag="xgt")
                        o_("scalar",
                           lambda e, w0=w, s0=sct, k2=bank, c2=wcol, x2=xga:
                               e.copy(x2[:, w0 * 512 + s0 * 128:
                                         w0 * 512 + s0 * 128 + 128],
                                      k2[:, c2:c2 + 128]),
                           reads=(bres,), writes=(xgr,), tag="xge")
                    else:
                        o_("tensor",
                           lambda e, s2=stg: e.matmul(P[4][0:1, 384:512],
                                                      s2[:, 1024:1025],
                                                      IDENT[:, :],
                                                      start=True, stop=True),
                           reads=swr + ("IDENT",), writes=("P4",),
                           tag="xgt")
                        o_("scalar",
                           lambda e, s0=sct:
                               e.copy(OTS[64:65, s0 * 128:(s0 + 1) * 128],
                                      P[4][0:1, 384:512]),
                           reads=("P4",), writes=("WCROW",), tag="wce")
            if j + 1 < NT:
                emit_xgl(j + 1)
            o_("tensor", lambda e: e.matmul(P[5][:, :], OTS[64:65, 512:640],
                                            OTS[64:65, 0:512],
                                            start=True, stop=True),
               reads=("WCROW", "RRBones"), writes=("P5",), tag="cbmm")
            o_("scalar", lambda e: e.copy(RB[:, :], P[5][:, :]),
               reads=("P5",), writes=("RB",), tag="cbev")
            for fc in range(16):
                pk = state["pk"] % 2
                state["pk"] += 1
                fh, fcl = fc // 8, fc % 8
                for dc in range(NCH):
                    o_("tensor",
                       lambda e, p0=pk, h0=fh, f0=fcl, d0=dc, x2=xga,
                              st=(dc == 0), sp=(dc == NCH - 1):
                           e.matmul(P[p0][:, :],
                                    V_AR[:, h0 * 8192 + d0 * 1024 + f0 * 128:
                                         h0 * 8192 + d0 * 1024 +
                                         f0 * 128 + 128],
                                    x2[:, d0 * 512:(d0 + 1) * 512],
                                    start=st, stop=sp),
                       reads=(f"W1F{fh}", xgr), writes=(f"P{pk}",),
                       tag="hmm")
                o_("scalar",
                   lambda e, p0=pk, f0=fc:
                       e.activation(Hap(f0), P[p0][:, :], AF.Relu,
                                    bias=B1T[:, f0:f0 + 1]),
                   reads=(f"P{pk}", "B1T"), writes=(f"H{fc}",), tag="relu")
            for dc2 in range(NCH):
                pk = state["pk"] % 2
                state["pk"] += 1
                for fc in range(16):
                    o_("tensor",
                       lambda e, p0=pk, f0=fc, d0=dc2,
                              st=(fc == 0), sp=(fc == 15):
                           e.matmul(P[p0][:, :],
                                    W_AR[:, f0 * 1024 + d0 * 128:
                                         f0 * 1024 + d0 * 128 + 128],
                                    Hap(f0), start=st, stop=sp),
                       reads=("W2F", f"H{fc}"), writes=(f"P{pk}",), tag="ymm")
                o_("scalar",
                   lambda e, p0=pk, d0=dc2:
                       e.activation(K_AR[:, d0 * 512:(d0 + 1) * 512],
                                    P[p0][:, :], AF.Identity,
                                    bias=B1T[:, 16 + d0:17 + d0]),
                   reads=(f"P{pk}", "B1T"), writes=("YGT",), tag="yev")
                o_("vector",
                   lambda e, d0=dc2:
                       e.tensor_tensor(K_AR[:, d0 * 512:(d0 + 1) * 512],
                                       K_AR[:, d0 * 512:(d0 + 1) * 512],
                                       RB[:, :], op=ALU.mult),
                   reads=("RB", "YGT"), writes=("YGT",), tag="yscale")
            for sct in range(4):
                for w in range(8):
                    bank = PT7 if w % 2 == 0 else PT8
                    bres = "PT7" if w % 2 == 0 else "PT8"
                    wcol = (w // 2) * 128
                    o_("tensor",
                       lambda e, w0=w, s0=sct, k2=bank, c2=wcol:
                           e.transpose(k2[:, c2:c2 + 128],
                                       K_AR[:, w0 * 512 + s0 * 128:
                                            w0 * 512 + s0 * 128 + 128],
                                       IDENT[:, :]),
                       reads=("YGT", "IDENT"), writes=(bres,), tag="ygt")
                    o_("scalar",
                       lambda e, w0=w, s0=sct, k2=bank, c2=wcol:
                           e.copy(K_AR[:, 4096 + s0 * 1024 + w0 * 128:
                                       4096 + s0 * 1024 + w0 * 128 + 128],
                                  k2[:, c2:c2 + 128]),
                       reads=(bres,), writes=(f"YTK{sct}",), tag="yge")
                o_("sync",
                   lambda e, j0=j, s0=sct:
                       e.dma_start(out=yg[j0 * 512 + s0 * 128:
                                          j0 * 512 + s0 * 128 + 128, :],
                                   in_=K_AR[:, 4096 + s0 * 1024:
                                            4096 + (s0 + 1) * 1024]),
                   reads=(f"YTK{sct}",), writes=(f"Y{j}",), dma=True,
                   tag="ygs")

    def emit_final():
        o_("vector", lambda e: e.memset(SC16[:, 4:5], 0.0),
           writes=("XB0", "XB1", "XGT0", "XGT1", "rrow", "OUTS",
                   "RRBonesF", "IROW", "XF0", "XF1", "XF2", "XF3"),
           tag="fence3")
        FSQ = ((SQT, ("SQT",)), (STG, ("STG0", "STG1")),
               (XGL, ("XGL",)), (PTB, ("PTB0", "PTB1")))

        def tile_ops(tt):
            ops = []

            def a_(eng, fn, reads=(), writes=(), dma=False, tag="f"):
                ops.append((eng, fn, tuple(reads), tuple(writes), dma, tag))

            q = tt % 4
            xo = q * 2048
            mb = q * 8
            sfx = str(q)
            sq, sqr = FSQ[q]
            xbr = f"XF{q}"

            def lncols(tag):
                a_("vector",
                   lambda e: e.tensor_scalar(MVC[:, mb + 2:mb + 3],
                                             MVC[:, mb:mb + 1],
                                             1.0 / D, None, op0=ALU.mult),
                   (f"MV0{sfx}",), (f"MV2{sfx}",), False, tag)
                a_("vector",
                   lambda e: e.tensor_scalar(MVC[:, mb + 3:mb + 4],
                                             MVC[:, mb + 1:mb + 2],
                                             1.0 / D, None, op0=ALU.mult),
                   (f"MV1{sfx}",), (f"MV3{sfx}",), False, tag)
                a_("vector",
                   lambda e: e.tensor_tensor(MVC[:, mb + 6:mb + 7],
                                             MVC[:, mb + 2:mb + 3],
                                             MVC[:, mb + 2:mb + 3],
                                             op=ALU.mult),
                   (f"MV2{sfx}",), (f"MV6{sfx}",), False, tag)
                a_("vector",
                   lambda e: e.tensor_tensor(MVC[:, mb + 3:mb + 4],
                                             MVC[:, mb + 3:mb + 4],
                                             MVC[:, mb + 6:mb + 7],
                                             op=ALU.subtract),
                   (f"MV6{sfx}", f"MV3{sfx}"), (f"MV3{sfx}",), False, tag)
                a_("vector",
                   lambda e: e.tensor_scalar_add(MVC[:, mb + 3:mb + 4],
                                                 MVC[:, mb + 3:mb + 4],
                                                 LN_EPS),
                   (f"MV3{sfx}",), (f"MV3{sfx}",), False, tag)
                a_("scalar",
                   lambda e: e.activation(MVC[:, mb + 3:mb + 4],
                                          MVC[:, mb + 3:mb + 4], AF.Ln),
                   (f"MV3{sfx}",), (f"MV3{sfx}",), False, tag)
                a_("scalar",
                   lambda e: e.activation(MVC[:, mb + 4:mb + 5],
                                          MVC[:, mb + 3:mb + 4], AF.Exp,
                                          scale=-0.5),
                   (f"MV3{sfx}",), (f"MV4{sfx}",), False, tag)
                a_("vector",
                   lambda e: e.tensor_tensor(MVC[:, mb + 5:mb + 6],
                                             MVC[:, mb + 2:mb + 3],
                                             MVC[:, mb + 4:mb + 5],
                                             op=ALU.mult),
                   (f"MV2{sfx}", f"MV4{sfx}"), (f"MV5{sfx}",), False, tag)
                a_("vector",
                   lambda e: e.tensor_scalar(MVC[:, mb + 5:mb + 6],
                                             MVC[:, mb + 5:mb + 6],
                                             -1.0, None, op0=ALU.mult),
                   (f"MV5{sfx}",), (f"MV5{sfx}",), False, tag)

            a_("gpsimd",
               lambda e: e.indirect_dma_start(
                   out=XB[:, xo:xo + 1024], out_offset=None, in_=yg,
                   in_offset=IndirectOffsetOnAxis(DESTC[:, tt:tt + 1], 0)),
               ("DESTC",) + tuple(f"Y{j2}" for j2 in range(NT)),
               (xbr,), True, "ygr")
            a_("sync",
               lambda e: e.dma_start(out=XB[:, xo + 1024:xo + 2048],
                                     in_=x2tok[tt * 128:(tt + 1) * 128,
                                               0:1024]),
               ("x2tok",), (xbr,), True, "f2ld")
            a_("vector",
               lambda e: e.tensor_tensor(XB[:, xo:xo + 1024],
                                         XB[:, xo:xo + 1024],
                                         XB[:, xo + 1024:xo + 2048],
                                         op=ALU.add),
               (xbr,), (xbr,), False, "r3")
            a_("sync",
               lambda e: e.dma_start(out=XB[:, xo + 1024:xo + 2048],
                                     in_=x1tok[tt * 128:(tt + 1) * 128, :]),
               ("x1tok", xbr), (xbr,), True, "f1ld")
            a_("vector",
               lambda e: e.reduce_sum(MVC[:, mb:mb + 1],
                                      XB[:, xo:xo + 1024], axis=AX.X),
               (xbr,), (f"MV0{sfx}",), False, "f3ln")
            a_("scalar",
               lambda e: e.activation(sq[:, 0:1024], XB[:, xo:xo + 1024],
                                      AF.Square),
               (xbr,), sqr, False, "f3sq")
            a_("vector",
               lambda e: e.reduce_sum(MVC[:, mb + 1:mb + 2], sq[:, 0:1024],
                                      axis=AX.X),
               sqr, (f"MV1{sfx}",), False, "f3ln")
            lncols("f3ln")
            a_("vector",
               lambda e: e.tensor_scalar(XB[:, xo:xo + 1024],
                                         XB[:, xo:xo + 1024],
                                         MVC[:, mb + 4:mb + 5],
                                         MVC[:, mb + 5:mb + 6],
                                         op0=ALU.mult, op1=ALU.add),
               (f"MV4{sfx}", f"MV5{sfx}"), (xbr,), False, "f3n1")
            a_("vector",
               lambda e: e.tensor_tensor(XB[:, xo:xo + 1024],
                                         XB[:, xo:xo + 1024],
                                         CMTX[:, 0:1024], op=ALU.mult),
               ("G2B2",), (xbr,), False, "f3n2")
            a_("vector",
               lambda e: e.tensor_tensor(XB[:, xo:xo + 1024],
                                         XB[:, xo:xo + 1024],
                                         CMTX[:, 1024:2048], op=ALU.add),
               ("G2B2",), (xbr,), False, "f3n3")
            a_("vector",
               lambda e: e.tensor_tensor(XB[:, xo:xo + 1024],
                                         XB[:, xo:xo + 1024],
                                         XB[:, xo + 1024:xo + 2048],
                                         op=ALU.add),
               (xbr,), (xbr,), False, "r4")
            a_("vector",
               lambda e: e.reduce_sum(MVC[:, mb:mb + 1],
                                      XB[:, xo:xo + 1024], axis=AX.X),
               (xbr,), (f"MV0{sfx}",), False, "f4ln")
            a_("scalar",
               lambda e: e.activation(sq[:, 0:1024], XB[:, xo:xo + 1024],
                                      AF.Square),
               (xbr,), sqr, False, "f4sq")
            a_("vector",
               lambda e: e.reduce_sum(MVC[:, mb + 1:mb + 2], sq[:, 0:1024],
                                      axis=AX.X),
               sqr, (f"MV1{sfx}",), False, "f4ln")
            lncols("f4ln")
            a_("vector",
               lambda e: e.tensor_scalar(OUTS[:, :], XB[:, xo:xo + 1024],
                                         MVC[:, mb + 4:mb + 5],
                                         MVC[:, mb + 5:mb + 6],
                                         op0=ALU.mult, op1=ALU.add),
               (f"MV4{sfx}", f"MV5{sfx}", xbr), ("OUTS",), False, "fnorm")
            a_("sync",
               lambda e: e.dma_start(out=out[tt * 128:(tt + 1) * 128, :],
                                     in_=OUTS[:, :]),
               ("OUTS",), ("out",), True, "ostore")
            return ops

        for t0 in range(0, 32, 4):
            tls = [tile_ops(t0 + i) for i in range(4)]
            emit_rr(*[tl[:-2] for tl in tls])
            for tl in tls:
                emit_seq(tl[-2:])

    if upto != "a0":
        attention(0, x0sp, "x0sp", x1sp, "x1_out", (qw, 0), (kw, 0), (vw, 0),
                  (ow, 0), alibi=True, causal=True, bias_qkv=False,
                  bias_o=False, gb=None, routing=False,
                  stop_after=upto if upto in ("kv", "qloop") else None,
                  tokspill=(x1tok, "x1tok", False))
    if upto in ("x2", "all"):
        o_("tensor", lambda e: e.matmul(P[5][:, 0:12], ones128, brt_row,
                                        start=True, stop=True),
           reads=("ROWC",), writes=("P5",), tag="brt")
        o_("scalar", lambda e: e.copy(BRB[:, :], P[5][:, 0:12]),
           reads=("P5",), writes=("BRB",), tag="brt")
        x2o = x2sp if upto == "x2" else None
        attention(1, x1sp, "x1_out", x2o, "x2_out", (wiT, 0), (wiT, 1024),
                  (wiT, 2048), (woT, 0), alibi=False, causal=False,
                  bias_qkv=True, bias_o=True, gb=(0, 8), routing=True,
                  tokspill=(x2tok, "x2tok", True))
    if upto == "all":
        emit_moe()
        emit_final()
        sink_reads = ("out", "x1_out")
    elif upto == "x2":
        sink_reads = ("x1_out", "x2_out", "x1tok", "x2tok")
    elif upto == "a0":
        sink_reads = ("x0sp",)
    elif upto == "kv":
        sink_reads = ("qsp", "x0sp", "K_AR")
    elif upto == "qloop":
        sink_reads = ("osp", "qsp", "x0sp")
    else:
        sink_reads = ("x1_out",)
    o_("sync", lambda e: e.nop(), reads=sink_reads, tag="sink")

    bld.emit(nc)
    es.close()
    return nc, bld


def host_inputs(inputs, core):
    f32 = lambda a: np.ascontiguousarray(np.asarray(a, dtype=np.float32))
    bf16 = lambda a: np.ascontiguousarray(np.asarray(a).astype(ml_dtypes.bfloat16))
    x = f32(inputs["L0_attn_x"])[core]
    sl = _slopes()
    s8 = np.array([np.float32(ml_dtypes.bfloat16(-8.0 * s)) for s in sl],
                  dtype=np.float32)
    slope_eff = (-s8 / 8.0).astype(np.float64)

    d = {"x0": x}
    d["qw"] = f32(inputs["L0_attn_q_w"])
    d["kw"] = f32(inputs["L0_attn_k_w"])
    d["vw"] = f32(inputs["L0_attn_v_w"])
    d["ow"] = f32(inputs["L0_attn_o_w"])
    d["wiT"] = f32(np.asarray(inputs["L0_ffn_in_proj_w"], np.float32).T)
    d["woT"] = f32(np.asarray(inputs["L0_ffn_out_proj_w"], np.float32).T)
    bi = f32(inputs["L0_ffn_in_proj_b"])
    d["bi_cols"] = f32(bi.reshape(24, 128).T)
    bo = f32(inputs["L0_ffn_out_proj_b"])
    d["bo_cols"] = f32(bo.reshape(8, 128).T)
    rows32 = np.zeros((1, 3688), np.float32)
    rows32[0, 0:512] = 1.0
    bg = f32(inputs["L0_ffn_b_group"])
    be = f32(inputs["L0_ffn_b_expert"]).reshape(-1)
    rows32[0, 2048:2052] = bg
    rows32[0, 2052:2060] = be
    rows32[0, 3600:3616] = np.arange(16, dtype=np.float32)
    d["rows32_c"] = rows32
    gbc = np.zeros((128, 32), np.float32)
    gbc[:, 0:8] = f32(inputs["L0_ffn_ln1_g"]).reshape(8, 128).T
    gbc[:, 8:16] = f32(inputs["L0_ffn_ln1_b"]).reshape(8, 128).T
    gbc[:, 16:24] = f32(inputs["L0_ffn_ln2_g"]).reshape(8, 128).T
    gbc[:, 24:32] = f32(inputs["L0_ffn_ln2_b"]).reshape(8, 128).T
    d["gbc_c"] = gbc
    d["bvh_c"] = np.ascontiguousarray(bi[2048:3072].reshape(16, 64).T)
    wg = f32(inputs["L0_ffn_w_group"])
    we = f32(inputs["L0_ffn_w_expert"])
    d["wroute"] = np.ascontiguousarray(
        np.concatenate([wg, we.transpose(1, 0, 2).reshape(D, 8)], axis=1))
    w1a = f32(inputs["L0_ffn_w1"])
    d["w1"] = np.ascontiguousarray(
        w1a.reshape(NE, 8, 128, 2, 1024).transpose(0, 2, 3, 1, 4)
        .reshape(NE * 128, 16384))
    w2a = f32(inputs["L0_ffn_w2"])
    d["w2"] = np.ascontiguousarray(
        w2a.reshape(NE, 16, 128, 1024).transpose(0, 2, 1, 3)
        .reshape(NE * 128, 16384))
    b1 = f32(inputs["L0_ffn_b1"])
    b2a = f32(inputs["L0_ffn_b2"])
    b12 = np.zeros((NE, 128, 24), np.float32)
    b12[:, :, 0:16] = b1.reshape(NE, 16, 128).transpose(0, 2, 1)
    b12[:, :, 16:24] = b2a.reshape(NE, 8, 128).transpose(0, 2, 1)
    d["b12r"] = np.ascontiguousarray(b12.reshape(NE * 128, 24))
    gb2 = np.zeros((128, 2048), np.float32)
    gb2[:, 0:1024] = f32(inputs["L0_ffn_ln2_g"]).reshape(-1)[None, :]
    gb2[:, 1024:2048] = f32(inputs["L0_ffn_ln2_b"]).reshape(-1)[None, :]
    d["g2b2b"] = gb2
    d["iotac"] = np.arange(128, dtype=np.float32).reshape(128, 1)
    d["identf_c"] = np.eye(128, dtype=np.float32)
    d["ident_c"] = np.eye(128, dtype=ml_dtypes.bfloat16)
    o16 = np.zeros((128, 128), np.float32)
    for jc in range(8):
        for r in range(128):
            o16[r, jc * 16 + 2 * jc + (r // 64)] = 1.0
    d["ones16_c"] = bf16(o16)
    d["one16c"] = bf16(np.ones((16, 128), np.float32))
    ex16 = np.zeros((16, 576), np.float32)
    for h in range(16):
        ex16[h, h * 36:(h + 1) * 36] = 1.0
    d["exp16_c"] = bf16(ex16)
    hcc = np.zeros((2, 128, 576), np.float32)
    ii = np.arange(128, dtype=np.float64)
    for h in range(16):
        for di in range(36):
            delta = di - 32
            hcc[0, :, h * 36 + di] = (slope_eff[h] * (ii + 128.0 * delta)
                                      ).astype(np.float32)
    d["hcc"] = hcc
    sl2 = np.zeros((NH, 2, S), np.float32)
    for h in range(16):
        sl2[h, :, :] = s8[h]
    d["sl2"] = bf16(sl2)
    j = np.arange(512, dtype=np.float32)
    qj = np.stack([16.0 * np.floor(j / 16.0), j - 16.0 * np.floor(j / 16.0)])
    d["qj_c"] = bf16(qj)
    mk = np.zeros((128, 896), np.float32)
    im = np.arange(128)[:, None]
    mm = np.arange(896)[None, :]
    mk[mm < im + 384] = NEG
    d["maskneg_c"] = mk
    return d


_NC_CACHE = {}


def kernel(**inputs):
    if "nc" not in _NC_CACHE:
        _NC_CACHE["nc"] = build_program("all")
    nc, _ = _NC_CACHE["nc"]
    in_maps = [host_inputs(inputs, c) for c in range(8)]
    res = run_bass_kernel_spmd(nc, in_maps, core_ids=list(range(8)))
    outs = [np.asarray(r["out"], dtype=np.float32) for r in res.results]
    return np.stack(outs, axis=0)


if __name__ == "__main__":
    nc, bld = build_program("all")
    from collections import Counter
    cnt = Counter(o.eng for o in bld.ops)
    print("ops per engine:", dict(cnt), "total", len(bld.ops))
    print("sem domains:", bld.domains)
    print("sem finals:", bld.counters)
